# version 04
# baseline (speedup 1.0000x reference)
import numpy as np
from contextlib import ExitStack

import concourse.bass as bass
import concourse.mybir as mybir
from concourse.bass_utils import run_bass_kernel_spmd

F32 = mybir.dt.float32
BF16 = mybir.dt.bfloat16
I32 = mybir.dt.int32
AF = mybir.ActivationFunctionType
ALU = mybir.AluOpType
AX = mybir.AxisListType

D = 1024
H = 4096
DC = D // 128
HC = H // 128
TT = 512

N_ENG = 5


def build_nc(S):
    NT = S // TT
    NL = S // 128
    NG = NT
    nc = bass.Bass("TRN2", target_bir_lowering=False, debug=False)

    x_in = nc.dram_tensor("x", [S, D], F32, kind="ExternalInput").ap()
    wbt = nc.dram_tensor("wbt", [D, D], F32, kind="ExternalInput").ap()
    wct = nc.dram_tensor("wct", [D, D], F32, kind="ExternalInput").ap()
    w1n = nc.dram_tensor("w1n", [H, D], F32, kind="ExternalInput").ap()
    w2n = nc.dram_tensor("w2n", [D, H], F32, kind="ExternalInput").ap()
    bB_d = nc.dram_tensor("bB", [128, DC], F32, kind="ExternalInput").ap()
    b1_d = nc.dram_tensor("b1", [128, HC], F32, kind="ExternalInput").ap()
    b2_d = nc.dram_tensor("b2", [128, DC], F32, kind="ExternalInput").ap()
    bC_d = nc.dram_tensor("bC", [128, DC], F32, kind="ExternalInput").ap()
    a_d = nc.dram_tensor("afull", [128, DC], F32, kind="ExternalInput").ap()
    y_out = nc.dram_tensor("y", [S, D], F32, kind="ExternalOutput").ap()

    xnt_d = nc.dram_tensor("xnt_d", [DC, 128, S], BF16, kind="Internal").ap()
    hs_d = nc.dram_tensor("hs_d", [NT, 128, DC * TT], BF16, kind="Internal").ap()
    g_d = nc.dram_tensor("g_d", [NT, 128, DC * TT], F32, kind="Internal").ap()
    s1_d = nc.dram_tensor("s1_d", [16, 128, 2048], BF16, kind="Internal").ap()
    s2_d = nc.dram_tensor("s2_d", [16, 128, 2048], BF16, kind="Internal").ap()

    with ExitStack() as ctx:
        E = ctx.enter_context
        ident_b = E(nc.sbuf_tensor("ident_b", [128, 128], BF16))
        ident_f = E(nc.sbuf_tensor("ident_f", [128, 128], F32))
        io_r = E(nc.sbuf_tensor("io_r", [128, 128], I32))
        io_c = E(nc.sbuf_tensor("io_c", [128, 128], I32))
        wbt_sb = E(nc.sbuf_tensor("wbt_sb", [128, DC * D], BF16))
        bB_sb = E(nc.sbuf_tensor("bB_sb", [128, DC], F32))
        b1_sb = E(nc.sbuf_tensor("b1_sb", [128, HC], F32))
        b2_sb = E(nc.sbuf_tensor("b2_sb", [128, DC], F32))
        bC_sb = E(nc.sbuf_tensor("bC_sb", [128, DC], F32))
        a_sb = E(nc.sbuf_tensor("a_sb", [128, DC], F32))
        al1_sb = E(nc.sbuf_tensor("al1_sb", [128, HC], F32))
        al1s_sb = E(nc.sbuf_tensor("al1s_sb", [128, HC], F32))
        al2_sb = E(nc.sbuf_tensor("al2_sb", [128, DC], F32))
        al2s_sb = E(nc.sbuf_tensor("al2s_sb", [128, DC], F32))
        al2p_sb = E(nc.sbuf_tensor("al2p_sb", [128, 4], F32))
        zs = E(nc.sbuf_tensor("zs", [128, 8], F32))
        m_sc = E(nc.sbuf_tensor("m_sc", [128, 16], F32))
        ln_sc = E(nc.sbuf_tensor("ln_sc", [128, 32], F32))

        AR_W = 188 * 1024
        arena = E(nc.sbuf_tensor("arena", [128, AR_W // 2], BF16))
        ar_f = arena.bitcast(F32)

        def fview(byte_off, n_el):
            assert byte_off % 4 == 0 and byte_off + 4 * n_el <= AR_W, byte_off
            return ar_f.ap()[:, byte_off // 4: byte_off // 4 + n_el]

        def bview(byte_off, n_el):
            assert byte_off % 2 == 0 and byte_off + 2 * n_el <= AR_W, byte_off
            return arena.ap()[:, byte_off // 2: byte_off // 2 + n_el]

        K = 1024
        bxw_v = [fview(0 + 16 * K * p, 8 * TT) for p in range(2)]
        xnw_v = [[bview(32 * K + 8 * K * r + 1024 * dc, TT)
                  for dc in range(DC)] for r in range(3)]
        hswin_v = fview(56 * K, 2 * 8 * TT)
        hsst_v = bview(88 * K, DC * TT)
        xt_v = [fview(96 * K + 4 * K * p, 1024) for p in range(4)]
        xnm_v = [bview(112 * K + 2 * K * p, 1024) for p in range(8)]
        xnst_v = bview(128 * K, DC * TT)
        s1p_v = [bview(136 * K + 4 * K * p, 2048) for p in range(2)]
        s2p_v = [bview(144 * K + 4 * K * p, 2048) for p in range(2)]
        gst_v = [fview(152 * K + 2 * K * p, TT) for p in range(2)]
        hst_v = bview(156 * K, HC * TT)
        wsc = [fview(144 * K + 4 * K * p, 1024) for p in range(2)]
        pstage = bview(152 * K, 2048)
        hsb_v = [bview(0 + 8 * K * p, DC * TT) for p in range(2)]
        gb_v = [fview(16 * K + 16 * K * p, DC * TT) for p in range(2)]
        sst_v = [fview(48 * K + 16 * K * p, DC * TT) for p in range(2)]
        sgst_v = [fview(80 * K + 16 * K * p, DC * TT) for p in range(2)]
        xr_v = [fview(112 * K + 4 * K * s, 1024) for s in range(4)]
        ost_v = [fview(128 * K + 4 * K * s, 1024) for s in range(4)]
        wct_v = bview(144 * K, DC * D)

        pb = [E(nc.psum_tensor(f"pb{i}", [128, 512], F32)) for i in range(6)]
        pb_tr = E(nc.psum_tensor("pb_tr", [128, 1024], BF16))
        pb_u = E(nc.psum_tensor("pb_u", [128, 512], F32))

        sem_names = [
            "s_wld", "s_wst", "s_wst2", "s_xld", "s_xld2", "s_xnw", "s_gld", "s_gw",
            "s_ln0", "s_ln1", "s_ln2", "s_ln3",
            "s_lne", "s_lns", "s_lnq", "s_tr", "s_xsp",
            "s_prep", "s_alv", "s_sgT", "s_sgn", "s_ssp", "s_s1l", "s_s2l",
            "s_f1m", "s_f1e", "s_f2m", "s_f2e", "s_gsp",
            "s_bxm", "s_bxe",
            "ss", "sv", "vm", "s_hc", "s_hss",
            "s_hsl", "s_gl", "s_wcm", "s_wce", "s_sg", "s_tr2", "s_cmb",
            "s_xr", "s_yst", "s_ph",
        ]
        sems = {n: E(nc.semaphore(n)) for n in sem_names}
        SM = sems
        blk = E(nc.Block())

        dma_ctr = {n: 0 for n in sem_names}

        def dma(eng, sem, out, in_, waits=()):
            for s, v in waits:
                if v > 0:
                    eng.wait_ge(SM[s], v)
            pre = dma_ctr[sem]
            d = eng.dma_start(out=out, in_=in_)
            if pre > 0:
                d._wait_ge(SM[sem], pre)
            d.then_inc(SM[sem], 16)
            dma_ctr[sem] = pre + 16
            return d

        def cW(c):
            return ("s_wst" if c % 2 == 0 else "s_wst2", 16 * (c // 2 + 1))

        def c_av2(p):
            return 32 + (p + 1) + (p + 1) // 4

        def c_avr(p):
            return 32 + (p + 1) + p // 4

        def LNS(k):
            return SM[f"s_ln{k % 4}"]

        def LNV(k, off):
            return 7 * (k // 4) + off

        def xld_sem(k):
            if k < 20:
                return "s_xld", 16 * (k + 1)
            return "s_xld2", 16 * (k - 19)

        def x_emit(eng, g):
            for kk in range(4):
                k = 4 * g + kk
                ws = []
                if k >= 4:
                    ws = [("s_lns", (k - 4) + 1), (f"s_ln{k % 4}", 7 * (k // 4))]
                dma(eng, xld_sem(k)[0], xt_v[k % 4],
                    x_in[128 * k:128 * (k + 1), :], waits=ws)

        def xnw_emit(sync, g):
            for dc in range(DC):
                ws = []
                if dc == 0:
                    ws.append(("s_xsp", 16 * DC * (g + 1)))
                    if g >= 3:
                        ws.append(("s_f1m", HC * (g - 2)))
                        ws.append(("s_bxm", DC * (g - 1)))
                dma(sync, "s_xnw", xnw_v[g % 3][dc],
                    xnt_d[dc, :, TT * g:TT * (g + 1)], waits=ws)

        def s1p_emit(sync, t, q):
            P = 16 * t + q
            ws = []
            if P == 0:
                ws.append(("s_ssp", 16 * 16))
            if P >= 2:
                ws.append(("s_f1m", HC * t + 2 * q - 2))
            dma(sync, "s_s1l", s1p_v[P % 2], s1_d[q, :, :], waits=ws)

        def s2p_emit(sync, t, q):
            P = 16 * t + q
            ws = []
            if P == 0:
                ws.append(("s_ssp", 16 * 32))
                ws.append(("s_alv", 72))
                ws.append(("s_sgT", 64))
            if P >= 2:
                ws.append(("s_f2m", (P - 2) // 2 + 1))
            dma(sync, "s_s2l", s2p_v[P % 2], s2_d[q, :, :], waits=ws)

        def xsp_emit(sync, g):
            for dc in range(DC):
                ws = [("s_lne", 8 * (4 * g + 4))] if dc == 0 else []
                dma(sync, "s_xsp", xnt_d[dc, :, TT * g:TT * (g + 1)],
                    xnst_v[:, TT * dc:TT * (dc + 1)], waits=ws)

        def ln_ve_parts(vector, k):
            L = ln_sc.ap()[:, 8 * (k % 4):8 * (k % 4) + 8]
            g7 = 7 * (k // 4)

            def mk_a(k=k, L=L, g7=g7):
                if k >= 4:
                    vector.wait_ge(LNS(k), g7)
                i = vector.tensor_reduce(out=L[:, 0:1], in_=xt_v[k % 4],
                                         axis=AX.X, op=ALU.add)
                xs, xv = xld_sem(k)
                i._wait_ge(SM[xs], xv)
                i.then_inc(LNS(k), 1)
                vector.wait_ge(LNS(k), g7 + 1)
                i = vector.tensor_scalar_mul(out=L[:, 1:2], in0=L[:, 0:1],
                                             scalar1=1.0 / D)
                i.then_inc(LNS(k), 1)
                vector.wait_ge(LNS(k), g7 + 2)
                i = vector.tensor_tensor(out=L[:, 2:3], in0=L[:, 1:2],
                                         in1=L[:, 1:2], op=ALU.mult)
                i.then_inc(LNS(k), 1)

            def mk_b1(k=k, L=L, g7=g7):
                vector.wait_ge(LNS(k), g7 + 3)
                i = vector.tensor_scalar(out=L[:, 3:4], in0=L[:, 4:5],
                                         scalar1=1.0 / D, scalar2=L[:, 2:3],
                                         op0=ALU.mult, op1=ALU.subtract)
                i._wait_ge(SM["s_lns"], k + 1)
                i.then_inc(LNS(k), 1)
                vector.wait_ge(LNS(k), g7 + 4)
                i = vector.tensor_scalar_add(out=L[:, 5:6], in0=L[:, 3:4],
                                             scalar1=1e-5)
                i.then_inc(LNS(k), 1)
                vector.wait_ge(LNS(k), g7 + 5)
                i = vector.reciprocal(out=L[:, 6:7], in_=L[:, 5:6])
                i.then_inc(LNS(k), 1)

            def mk_b2(k=k, L=L):
                if k >= 8:
                    vector.wait_ge(SM["s_tr"], k - 7)
                i = vector.tensor_scalar(out=xnm_v[k % 8], in0=xt_v[k % 4],
                                         scalar1=L[:, 1:2], scalar2=L[:, 7:8],
                                         op0=ALU.subtract, op1=ALU.mult)
                i._wait_ge(SM["s_lnq"], k + 1)
                i.then_inc(LNS(k), 1)

            return mk_a, mk_b1, mk_b2

        def ln_compute_ve(vector, g):
            out = []
            for kk in range(4):
                mk_a, mk_b1, mk_b2 = ln_ve_parts(vector, 4 * g + kk)
                out.append((128 + 24 * kk, mk_a))
                out.append((176 + 24 * kk, mk_b1))
                out.append((272 + 24 * kk, mk_b2))
            return out

        def ln_se_parts(scalar, k):
            L = ln_sc.ap()[:, 8 * (k % 4):8 * (k % 4) + 8]

            def mk_sq(k=k, L=L):
                if k >= 2:
                    scalar.wait_ge(LNS(k - 2), LNV(k - 2, 4))
                if k >= 8:
                    scalar.wait_ge(SM["s_tr"], k - 7)
                i = scalar.activation(out=xnm_v[k % 8], in_=xt_v[k % 4],
                                      func=AF.Square, accum_out=L[:, 4:5])
                xs, xv = xld_sem(k)
                i._wait_ge(SM[xs], xv)
                i.then_inc(SM["s_lns"], 1)

            def mk_sqrt(k=k, L=L):
                i = scalar.activation(out=L[:, 7:8], in_=L[:, 6:7],
                                      func=AF.Sqrt)
                i._wait_ge(LNS(k), LNV(k, 6))
                i.then_inc(SM["s_lnq"], 1)

            return mk_sq, mk_sqrt

        def ln_compute_se(scalar, g):
            out = []
            for kk in range(4):
                mk_sq, mk_sqrt = ln_se_parts(scalar, 4 * g + kk)
                out.append((136 + 24 * kk, mk_sq))
                out.append((260, mk_sqrt))
            return out

        def tr_emit(tensor, g, kk):
            k = 4 * g + kk
            tensor.wait_ge(SM["s_lne"], 8 * k)
            for dcb in range(DC):
                i = tensor.transpose(
                    out=pb_tr.ap()[:, 128 * dcb:128 * (dcb + 1)],
                    in_=xnm_v[k % 8][:, 128 * dcb:128 * (dcb + 1)],
                    identity=ident_b.ap()[:, :])
                if dcb == 0:
                    i._wait_ge(LNS(k), LNV(k, 7))
                if dcb == DC - 1:
                    i.then_inc(SM["s_tr"], 1)

        def ln_evac_tile(vector, k):
            kk = k % 4
            g = k // 4
            fns = []
            for dcb in range(DC):
                def mk(k=k, kk=kk, dcb=dcb, g=g):
                    if dcb == 0 and kk == 0 and g > 0:
                        vector.wait_ge(SM["s_xsp"], 16 * DC * g)
                    i = vector.tensor_copy(
                        out=xnst_v[:, TT * dcb + 128 * kk:
                                   TT * dcb + 128 * (kk + 1)],
                        in_=pb_tr.ap()[:, 128 * dcb:128 * (dcb + 1)])
                    if dcb == 0:
                        i._wait_ge(SM["s_tr"], k + 1)
                    i.then_inc(SM["s_lne"], 1)
                fns.append(mk)
            return fns

        def ln_evac(vector, g):
            out = []
            for kk in range(4):
                for dcb, mk in enumerate(ln_evac_tile(vector, 4 * g + kk)):
                    out.append((100 + 2 * (8 * kk + dcb), mk))
            return out

        def bx_mm(tensor, w):
            for ec in range(DC):
                g = DC * w + ec
                if g >= 2:
                    tensor.wait_ge(SM["s_bxe"], g - 1)
                for dc in range(DC):
                    i = tensor.matmul(
                        out=pb[ec % 2].ap()[:, :],
                        lhsT=wbt_sb.ap()[:, D * dc + 128 * ec:
                                         D * dc + 128 * (ec + 1)],
                        rhs=xnw_v[w % 3][dc],
                        start=(dc == 0), stop=(dc == DC - 1))
                    if dc == 0:
                        i._wait_ge(SM["s_xnw"], 16 * DC * (w + 1))
                    if dc == DC - 1:
                        i.then_inc(SM["s_bxm"], 1)

        def f1_mm(tensor, t):
            for q in range(16):
                tensor.wait_ge(SM["s_s1l"], 16 * (16 * t + q + 1))
                for r in range(2):
                    hc = 2 * q + r
                    g = HC * t + hc
                    if g >= 2:
                        tensor.wait_ge(SM["s_f1e"], g - 1)
                    for dc in range(DC):
                        i = tensor.matmul(
                            out=pb[2 + hc % 2].ap()[:, :],
                            lhsT=s1p_v[(16 * t + q) % 2][:, r * 1024 + dc * 128:
                                                         r * 1024 + dc * 128 + 128],
                            rhs=xnw_v[t % 3][dc],
                            start=(dc == 0), stop=(dc == DC - 1))
                        if dc == 0:
                            i._wait_ge(SM["s_xnw"], 16 * DC * (t + 1))
                        if dc == DC - 1:
                            i.then_inc(SM["s_f1m"], 1)

        def f2_mm(tensor, t, douts):
            for dout in douts:
                g = DC * t + dout
                if t == 0 and dout == 0:
                    tensor.wait_ge(SM["s_sgn"], 128)
                if g >= 2:
                    tensor.wait_ge(SM["s_f2e"], g - 1)
                for hc in range(HC):
                    q2 = 2 * dout + hc // 16
                    if hc % 16 == 0:
                        tensor.wait_ge(SM["s_s2l"], 16 * (16 * t + q2 + 1))
                    i = tensor.matmul(
                        out=pb[4 + dout % 2].ap()[:, :],
                        lhsT=s2p_v[(16 * t + q2) % 2][:, (hc % 16) * 128:
                                                      (hc % 16) * 128 + 128],
                        rhs=hst_v[:, TT * hc:TT * (hc + 1)],
                        start=(hc == 0), stop=(hc == HC - 1))
                    i._wait_ge(SM["s_f1e"], HC * t + hc + 1)
                    if hc == HC - 1:
                        i.then_inc(SM["s_f2m"], 1)

        def sgT_emit(tensor, c):
            if c >= 1:
                tensor.wait_ge(SM["s_sgn"], 2 * c)
            src = wsc[c % 2]
            for b in range(8):
                bank = pb[4 + b // 4]
                i = tensor.transpose(
                    out=bank.ap()[:, 128 * (b % 4):128 * (b % 4 + 1)],
                    in_=src[:, 128 * b:128 * (b + 1)],
                    identity=ident_f.ap()[:, :])
                if b == 0:
                    i._wait_ge(SM[cW(c)[0]], cW(c)[1])
                if b == 7:
                    i.then_inc(SM["s_sgT"], 1)

        def mk_sign(scalar, c, half):
            def fn(c=c, half=half):
                if half == 0 and c >= 2:
                    scalar.wait_ge(SM["s_ssp"], 16 * (c // 2))
                i = scalar.activation(
                    out=pstage[:, (c % 2) * 1024 + 512 * half:
                               (c % 2) * 1024 + 512 * (half + 1)],
                    in_=pb[4 + half].ap()[:, :], func=AF.Sign)
                i._wait_ge(SM["s_sgT"], c + 1)
                i.then_inc(SM["s_sgn"], 1)
            return fn

        @blk.sync
        def _(sync):
            for srcp, dst in ((bB_d, bB_sb), (b1_d, b1_sb), (b2_d, b2_sb),
                              (bC_d, bC_sb), (a_d, a_sb)):
                dma(sync, "s_wld", dst.ap()[:, :], srcp[:, :])
            x_emit(sync, 0)
            if NG > 1:
                x_emit(sync, 1)
            xsp_emit(sync, 0)
            if NG > 2:
                x_emit(sync, 2)
            if NG > 1:
                xsp_emit(sync, 1)
            if NG > 3:
                x_emit(sync, 3)
            if NG > 2:
                xsp_emit(sync, 2)
            if NG > 4:
                x_emit(sync, 4)
            xnw_emit(sync, 0)
            if NT > 1:
                xnw_emit(sync, 1)
            for w in range(NT):
                if w + 3 < NG:
                    xsp_emit(sync, w + 3)
                if w + 2 < NT:
                    xnw_emit(sync, w + 2)
                if w >= 1:
                    for q in range(16):
                        s2p_emit(sync, w - 1, q)
                for q in range(16):
                    s1p_emit(sync, w, q)
            for q in range(16):
                s2p_emit(sync, NT - 1, q)
            sync.wait_ge(SM["s_xld"], 16 * min(NL, 20))
            sync.wait_ge(SM["s_xnw"], 16 * DC * NT)
            sync.wait_ge(SM["s_xsp"], 16 * DC * NG)
            sync.wait_ge(SM["s_s1l"], 16 * 16 * NT)
            sync.wait_ge(SM["s_s2l"], 16 * 16 * NT)
            sync.nop().then_inc(SM["s_ph"], 1)
            sync.wait_ge(SM["s_ph"], N_ENG)
            for t in range(NT):
                dma(sync, "s_hsl", hsb_v[t % 2], hs_d[t, :, :],
                    waits=[("s_wcm", DC * (t - 1) if t >= 2 else 0)])
                for s4 in range(4):
                    dma(sync, "s_xr", xr_v[s4],
                        x_in[TT * t + 128 * s4:TT * t + 128 * (s4 + 1), :],
                        waits=[("s_cmb", 8 * t if (t >= 1 and s4 == 0) else 0)])
            sync.wait_ge(SM["s_yst"], 16 * 4 * NT)

        @blk.gpsimd
        def _(gpsimd):
            gpsimd.iota(io_r.ap()[:, :], pattern=[[1, 128]], base=0,
                        channel_multiplier=0).then_inc(SM["s_gld"], 1)
            gpsimd.iota(io_c.ap()[:, :], pattern=[[0, 128]], base=0,
                        channel_multiplier=1).then_inc(SM["s_gld"], 1)
            for i in range(DC):
                dma(gpsimd, "s_gw", wbt_sb.ap()[:, i * D:(i + 1) * D],
                    wbt[128 * i:128 * (i + 1), :])
            for c in range(HC):
                ws = []
                if c >= 2:
                    ws = [("s_alv", c - 1), ("s_sgT", c - 1)]
                dma(gpsimd, cW(c)[0], wsc[c % 2],
                    w1n[128 * c:128 * (c + 1), :], waits=ws)
                if c % 2 == 1:
                    q = c // 2
                    dma(gpsimd, "s_ssp", s1_d[q, :, :], pstage,
                        waits=[("s_sgn", 4 * q + 4)])
            if 5 < NG:
                x_emit(gpsimd, 5)
            for p in range(32):
                c = 32 + p
                ws = [("s_alv", 31 + p if p < 2 else c_avr(p - 2)),
                      ("s_sgT", c - 1)]
                dc, jp = p // 4, p % 4
                dma(gpsimd, cW(c)[0], wsc[c % 2],
                    w2n[128 * dc:128 * (dc + 1), 1024 * jp:1024 * (jp + 1)],
                    waits=ws)
                if p % 2 == 1:
                    q2 = 2 * (p // 4) + (p % 4) // 2
                    dma(gpsimd, "s_ssp", s2_d[q2, :, :], pstage,
                        waits=[("s_sgn", 64 + 4 * (p // 2) + 4)])
            for w in range(NT):
                if w >= 1:
                    dma(gpsimd, "s_hss", hs_d[w - 1, :, :], hsst_v,
                        waits=[("s_hc", DC * w)])
                    for dout in range(DC):
                        dma(gpsimd, "s_gsp",
                            g_d[w - 1, :, TT * dout:TT * (dout + 1)],
                            gst_v[dout % 2],
                            waits=[("s_f2e", DC * (w - 1) + dout + 1)])
                if w + 6 < NG:
                    x_emit(gpsimd, w + 6)
            dma(gpsimd, "s_hss", hs_d[NT - 1, :, :], hsst_v,
                waits=[("s_hc", DC * NT)])
            for dout in range(DC):
                dma(gpsimd, "s_gsp", g_d[NT - 1, :, TT * dout:TT * (dout + 1)],
                    gst_v[dout % 2],
                    waits=[("s_f2e", DC * (NT - 1) + dout + 1)])
            gpsimd.wait_ge(SM["s_hss"], 16 * NT)
            gpsimd.wait_ge(SM["s_gsp"], 16 * DC * NT)
            gpsimd.wait_ge(SM["s_ssp"], 16 * 32)
            gpsimd.wait_ge(SM["s_wst"], 16 * 32)
            gpsimd.wait_ge(SM["s_wst2"], 16 * 32)
            if NL > 20:
                gpsimd.wait_ge(SM["s_xld2"], 16 * (NL - 20))
            gpsimd.engine_nop().then_inc(SM["s_ph"], 1)
            gpsimd.wait_ge(SM["s_ph"], N_ENG)
            for i in range(DC):
                dma(gpsimd, "s_gw", wct_v[:, i * D:(i + 1) * D],
                    wct[128 * i:128 * (i + 1), :])
            for t in range(NT):
                dma(gpsimd, "s_gl", gb_v[t % 2], g_d[t, :, :],
                    waits=[("s_sg", DC * (t - 1) if t >= 2 else 0)])
                for s4 in range(4):
                    u = 4 * t + s4
                    dma(gpsimd, "s_yst",
                        y_out[TT * t + 128 * s4:TT * t + 128 * (s4 + 1), :],
                        ost_v[s4], waits=[("s_cmb", 2 * (u + 1))])

        @blk.vector
        def _(vector):
            i = vector.tensor_tensor(out=ident_b.ap()[:, :], in0=io_r.ap()[:, :],
                                     in1=io_c.ap()[:, :], op=ALU.is_equal)
            i._wait_ge(SM["s_gld"], 2)
            i.then_inc(SM["s_prep"], 1)
            i = vector.tensor_copy(out=ident_f.ap()[:, :], in_=ident_b.ap()[:, :])
            i._wait_ge(SM["s_prep"], 1)
            i.then_inc(SM["s_prep"], 1)
            vector.memset(zs.ap()[:, :], 0.0).then_inc(SM["s_prep"], 1)
            pre_tiles = 4 * min(4, NG)
            pre_ev = 4 * min(3, NG)
            for k in range(pre_tiles):
                mk_a, mk_b1, mk_b2 = ln_ve_parts(vector, k)
                mk_a()
                mk_b1()
                mk_b2()
                if k >= 1 and (k - 1) < pre_ev:
                    for fn in ln_evac_tile(vector, k - 1):
                        fn()
            if pre_ev >= pre_tiles:
                for fn in ln_evac_tile(vector, pre_tiles - 1):
                    fn()
            vector.wait_ge(SM["s_prep"], 3)
            vector.wait_ge(SM["s_wld"], 16 * 5)
            hswr = hswin_v.rearrange("p (s c) -> p c s", c=8)
            for w in range(NT):
                ins = {}

                def put(j, fn):
                    ins.setdefault(j, []).append(fn)
                if w >= 1:
                    for dc in range(DC):
                        def mkc(dc=dc, w=w):
                            pw = (w - 1) % 2
                            if dc == 0:
                                vector.wait_ge(SM["s_hss"], 16 * (w - 1))
                            i = vector.tensor_copy(
                                out=hsst_v[:, TT * dc:TT * (dc + 1)],
                                in_=hswr[:, dc, pw * TT:pw * TT + TT])
                            i._wait_ge(SM["ss"], w * TT)
                            i.then_inc(SM["s_hc"], 1)
                        put(64 + 56 * dc, mkc)
                if w + 3 < NG:
                    for j, fn in ln_evac(vector, w + 3):
                        put(j, fn)
                if w + 4 < NG:
                    for j, fn in ln_compute_ve(vector, w + 4):
                        put(j, fn)
                if w == 0:
                    for hc in range(HC):
                        def mkr(hc=hc):
                            i = vector.tensor_reduce(
                                out=al1_sb.ap()[:, hc:hc + 1],
                                in_=wsc[hc % 2], axis=AX.X,
                                op=ALU.add, apply_absolute_value=True)
                            i._wait_ge(SM[cW(hc)[0]], cW(hc)[1])
                            i.then_inc(SM["s_alv"], 1)
                        put(250 + 8 * hc, mkr)

                    def mk_al1s():
                        i = vector.tensor_scalar_mul(
                            out=al1s_sb.ap()[:, :], in0=al1_sb.ap()[:, :],
                            scalar1=1.0 / D)
                        i._wait_ge(SM["s_alv"], 32)
                        i.then_inc(SM["s_prep"], 1)
                    put(506, mk_al1s)
                if w == 1:
                    for p in range(32):
                        def mkr2(p=p):
                            if p % 4 == 0 and p >= 4:
                                vector.wait_ge(SM["s_alv"], c_av2(p - 1))
                            i = vector.tensor_reduce(
                                out=al2p_sb.ap()[:, p % 4:p % 4 + 1],
                                in_=wsc[(32 + p) % 2], axis=AX.X,
                                op=ALU.add, apply_absolute_value=True)
                            i._wait_ge(SM[cW(32 + p)[0]], cW(32 + p)[1])
                            i.then_inc(SM["s_alv"], 1)
                        put(310 + 6 * p, mkr2)
                        if p % 4 == 3:
                            def mkcmb(p=p):
                                dc = p // 4
                                i = vector.tensor_reduce(
                                    out=al2_sb.ap()[:, dc:dc + 1],
                                    in_=al2p_sb.ap()[:, :], axis=AX.X,
                                    op=ALU.add)
                                i._wait_ge(SM["s_alv"], c_av2(p) - 1)
                                i.then_inc(SM["s_alv"], 1)
                            put(314 + 6 * p, mkcmb)

                if w == 2:
                    def mk_al2s():
                        i = vector.tensor_scalar_mul(
                            out=al2s_sb.ap()[:, :], in0=al2_sb.ap()[:, :],
                            scalar1=1.0 / H)
                        i._wait_ge(SM["s_alv"], 72)
                        i.then_inc(SM["s_prep"], 1)
                    put(20, mk_al2s)
                vector.wait_ge(SM["s_bxe"], DC * (w + 1))
                for j in range(TT):
                    t = w * TT + j
                    prev = (t - 1) % (2 * TT)
                    hp = zs.ap()[:, 0:8] if t == 0 else \
                        hswin_v[:, 8 * prev:8 * prev + 8]
                    i1 = vector.tensor_tensor(
                        out=m_sc.ap()[:, 8 * (t % 2):8 * (t % 2) + 8],
                        in0=hp, in1=a_sb.ap()[:, 0:8], op=ALU.mult)
                    if t > 0:
                        i1._wait_ge(SM["ss"], t)
                    i1.then_inc(SM["vm"], 1)
                    usl = t % 64
                    i2 = vector.tensor_tensor(
                        out=pb_u.ap()[:, 8 * usl:8 * usl + 8],
                        in0=m_sc.ap()[:, 8 * (t % 2):8 * (t % 2) + 8],
                        in1=bxw_v[w % 2][:, 8 * j:8 * j + 8], op=ALU.add)
                    i2._wait_ge(SM["vm"], t + 1)
                    i2.then_inc(SM["sv"], 1)
                    for fn in ins.get(j, ()):
                        fn()
            vector.wait_ge(SM["ss"], NT * TT)
            vector.wait_ge(SM["s_hss"], 16 * (NT - 1))
            pw = (NT - 1) % 2
            for dc in range(DC):
                i = vector.tensor_copy(
                    out=hsst_v[:, TT * dc:TT * (dc + 1)],
                    in_=hswr[:, dc, pw * TT:pw * TT + TT])
                i.then_inc(SM["s_hc"], 1)
            vector.wait_ge(SM["s_hc"], DC * NT)
            vector.wait_ge(SM["sv"], S)
            vector.wait_ge(SM["vm"], S)
            vector.wait_ge(SM["s_lne"], 8 * NL)
            for l in range(4):
                vector.wait_ge(SM[f"s_ln{l}"], 7 * (NL // 4))
            vector.engine_nop().then_inc(SM["s_ph"], 1)
            vector.wait_ge(SM["s_ph"], N_ENG)
            for t in range(NT):
                p = t % 2
                if t >= 1:
                    vector.wait_ge(SM["s_wce"], DC * (t - 1))
                    vector.wait_ge(SM["s_yst"], 16 * 4 * t)
                for ec in range(DC):
                    i = vector.tensor_scalar_add(
                        out=sst_v[p][:, TT * ec:TT * (ec + 1)],
                        in0=pb[ec % 2].ap()[:, :],
                        scalar1=bC_sb.ap()[:, ec:ec + 1])
                    i._wait_ge(SM["s_wcm"], DC * t + ec + 1)
                    i.then_inc(SM["s_wce"], 1)
                vector.wait_ge(SM["s_wce"], DC * (t + 1))
                for ec in range(DC):
                    i = vector.tensor_tensor(
                        out=sgst_v[p][:, TT * ec:TT * (ec + 1)],
                        in0=sst_v[p][:, TT * ec:TT * (ec + 1)],
                        in1=gb_v[p][:, TT * ec:TT * (ec + 1)], op=ALU.mult)
                    if ec == 0:
                        i._wait_ge(SM["s_gl"], 16 * (t + 1))
                    i.then_inc(SM["s_sg"], 1)
                for s4 in range(4):
                    u = 4 * t + s4
                    vector.wait_ge(SM["s_xr"], 16 * (u + 1))
                    i = vector.tensor_tensor(
                        out=ost_v[s4][:, 0:512], in0=pb[2].ap()[:, :],
                        in1=xr_v[s4][:, 0:512], op=ALU.add)
                    i._wait_ge(SM["s_tr2"], u + 1)
                    i.then_inc(SM["s_cmb"], 1)
                    i = vector.tensor_tensor(
                        out=ost_v[s4][:, 512:1024], in0=pb[3].ap()[:, :],
                        in1=xr_v[s4][:, 512:1024], op=ALU.add)
                    i._wait_ge(SM["s_xr"], 16 * (u + 1))
                    i.then_inc(SM["s_cmb"], 1)

        @blk.scalar
        def _(scalar):
            pre_tiles = 4 * min(4, NG)
            for k in range(pre_tiles):
                mk_sq, mk_sqrt = ln_se_parts(scalar, k)
                mk_sq()
                mk_sqrt()
            scalar.wait_ge(SM["s_wld"], 16 * 5)
            bxwr = [bxw_v[p].rearrange("p (t c) -> p c t", c=8) for p in range(2)]
            for ec in range(DC):
                i = scalar.add(out=bxwr[0][:, ec, :], in_=pb[ec % 2].ap()[:, :],
                               add=bB_sb.ap()[:, ec:ec + 1])
                i._wait_ge(SM["s_bxm"], ec + 1)
                i.then_inc(SM["s_bxe"], 1)
            for w in range(NT):
                ins = {}

                def put(j, fn):
                    ins.setdefault(j, []).append(fn)
                if w + 1 < NT:
                    for ec in range(DC):
                        def mke(ec=ec, w=w):
                            if ec == 0:
                                scalar.wait_ge(SM["sv"], w * TT)
                            i = scalar.add(out=bxwr[(w + 1) % 2][:, ec, :],
                                           in_=pb[ec % 2].ap()[:, :],
                                           add=bB_sb.ap()[:, ec:ec + 1])
                            i._wait_ge(SM["s_bxm"], DC * (w + 1) + ec + 1)
                            i.then_inc(SM["s_bxe"], 1)
                        jb = {0: 170}.get(w, 150)
                        put(jb + 8 * ec, mke)
                silu_sched = []
                if w >= 2 and w - 2 <= NT - 3:
                    silu_sched = [(w - 2, 245)]
                for t_s, jb in silu_sched:
                    for dout in range(DC):
                        def mks(dout=dout, t=t_s):
                            if t == 0 and dout == 0:
                                scalar.wait_ge(SM["s_ssp"], 16 * 32)
                            if DC * t + dout >= 2:
                                scalar.wait_ge(SM["s_gsp"],
                                               16 * (DC * t + dout - 1))
                            i = scalar.activation(
                                out=gst_v[dout % 2],
                                in_=pb[4 + dout % 2].ap()[:, :], func=AF.Silu,
                                bias=b2_sb.ap()[:, dout:dout + 1],
                                scale=al2s_sb.ap()[:, dout:dout + 1])
                            if t == 0 and dout == 0:
                                scalar.wait_ge(SM["s_prep"], 5)
                            i._wait_ge(SM["s_f2m"], DC * t + dout + 1)
                            i.then_inc(SM["s_f2e"], 1)
                        put(jb + 30 * dout, mks)
                gelu_specs = []
                if w == 1:
                    gelu_specs.append((0, 200))
                elif w >= 3 and w - 2 <= NT - 3:
                    gelu_specs.append((w - 2, 4))
                if w == NT - 1 and NT >= 4:
                    gelu_specs.append((NT - 2, 460))
                for t_g, jb in gelu_specs:
                    for hc in range(HC):
                        def mkg(hc=hc, t=t_g):
                            if hc == 0:
                                scalar.wait_ge(SM["s_f2m"], DC * t)
                                if t == 0:
                                    scalar.wait_ge(SM["s_prep"], 4)
                            i = scalar.activation(
                                out=hst_v[:, TT * hc:TT * (hc + 1)],
                                in_=pb[2 + hc % 2].ap()[:, :], func=AF.Gelu,
                                bias=b1_sb.ap()[:, hc:hc + 1],
                                scale=al1s_sb.ap()[:, hc:hc + 1])
                            i._wait_ge(SM["s_f1m"], HC * t + hc + 1)
                            i.then_inc(SM["s_f1e"], 1)
                        put(min(jb + 3 * hc, 508), mkg)
                if w + 4 < NG:
                    for j, fn in ln_compute_se(scalar, w + 4):
                        put(j, fn)
                if w == 0:
                    for hc in range(HC):
                        put(240 + 8 * hc, mk_sign(scalar, hc, 0))
                        put(242 + 8 * hc, mk_sign(scalar, hc, 1))
                if w == 1:
                    for p in range(32):
                        put(300 + 6 * p, mk_sign(scalar, 32 + p, 0))
                        put(302 + 6 * p, mk_sign(scalar, 32 + p, 1))
                if w >= 2:
                    scalar.wait_ge(SM["s_hc"], DC * (w - 1))
                for j in range(TT):
                    t = w * TT + j
                    sl = t % (2 * TT)
                    usl = t % 64
                    i3 = scalar.activation(
                        out=hswin_v[:, 8 * sl:8 * sl + 8],
                        in_=pb_u.ap()[:, 8 * usl:8 * usl + 8], func=AF.Tanh)
                    i3._wait_ge(SM["sv"], t + 1)
                    i3.then_inc(SM["ss"], 1)
                    for fn in ins.get(j, ()):
                        fn()
            def gelu_tail(t):
                for hc in range(HC):
                    if hc == 0:
                        scalar.wait_ge(SM["s_f2m"], DC * t)
                    i = scalar.activation(
                        out=hst_v[:, TT * hc:TT * (hc + 1)],
                        in_=pb[2 + hc % 2].ap()[:, :], func=AF.Gelu,
                        bias=b1_sb.ap()[:, hc:hc + 1],
                        scale=al1s_sb.ap()[:, hc:hc + 1])
                    i._wait_ge(SM["s_f1m"], HC * t + hc + 1)
                    i.then_inc(SM["s_f1e"], 1)

            def silu_tail(t):
                for dout in range(DC):
                    if DC * t + dout >= 2:
                        scalar.wait_ge(SM["s_gsp"], 16 * (DC * t + dout - 1))
                    i = scalar.activation(
                        out=gst_v[dout % 2], in_=pb[4 + dout % 2].ap()[:, :],
                        func=AF.Silu, bias=b2_sb.ap()[:, dout:dout + 1],
                        scale=al2s_sb.ap()[:, dout:dout + 1])
                    i._wait_ge(SM["s_f2m"], DC * t + dout + 1)
                    i.then_inc(SM["s_f2e"], 1)
            if NT >= 2 and NT - 2 >= 1 and NT < 4:
                gelu_tail(NT - 2)
            if NT >= 2:
                silu_tail(NT - 2)
            gelu_tail(NT - 1)
            silu_tail(NT - 1)
            scalar.wait_ge(SM["ss"], S)
            scalar.wait_ge(SM["s_bxe"], DC * NT)
            scalar.wait_ge(SM["s_f1e"], HC * NT)
            scalar.wait_ge(SM["s_f2e"], DC * NT)
            scalar.wait_ge(SM["s_lns"], NL)
            scalar.wait_ge(SM["s_lnq"], NL)
            scalar.wait_ge(SM["s_sgn"], 128)
            scalar.nop().then_inc(SM["s_ph"], 1)

        @blk.tensor
        def _(tensor):
            tensor.wait_ge(SM["s_prep"], 2)
            for g in range(min(3, NG)):
                for kk in range(4):
                    tr_emit(tensor, g, kk)
            tensor.wait_ge(SM["s_gw"], 16 * DC)
            bx_mm(tensor, 0)
            for w in range(NT):
                if w == 0:
                    if 3 < NG:
                        for kk in range(4):
                            tr_emit(tensor, 3, kk)
                    if NT > 1:
                        bx_mm(tensor, 1)
                    for c in range(HC):
                        sgT_emit(tensor, c)
                    if 4 < NG:
                        for kk in range(4):
                            tr_emit(tensor, 4, kk)
                    if NT > 2:
                        bx_mm(tensor, 2)
                elif w == 1:
                    f1_mm(tensor, 0)
                    for p in range(32):
                        sgT_emit(tensor, 32 + p)
                    if 5 < NG:
                        for kk in range(4):
                            tr_emit(tensor, 5, kk)
                    if NT > 3:
                        bx_mm(tensor, 3)
                else:
                    f2_mm(tensor, w - 2, list(range(DC)))
                    if w - 1 < NT:
                        f1_mm(tensor, w - 1)
                    if w + 4 < NG:
                        for kk in range(4):
                            tr_emit(tensor, w + 4, kk)
                    if w + 2 < NT:
                        bx_mm(tensor, w + 2)
            if NT >= 2:
                f2_mm(tensor, NT - 2, list(range(DC)))
            f1_mm(tensor, NT - 1)
            f2_mm(tensor, NT - 1, list(range(DC)))
            tensor.wait_ge(SM["s_bxm"], DC * NT)
            tensor.wait_ge(SM["s_f1m"], HC * NT)
            tensor.wait_ge(SM["s_f2m"], DC * NT)
            tensor.wait_ge(SM["s_tr"], NL)
            tensor.wait_ge(SM["s_sgT"], 64)
            tensor.nop().then_inc(SM["s_ph"], 1)
            tensor.wait_ge(SM["s_ph"], N_ENG)
            tensor.wait_ge(SM["s_gw"], 16 * 2 * DC)
            for t in range(NT):
                p = t % 2
                for ec in range(DC):
                    g = DC * t + ec
                    if g >= 2:
                        tensor.wait_ge(SM["s_wce"], g - 1)
                    for dc in range(DC):
                        i = tensor.matmul(
                            out=pb[ec % 2].ap()[:, :],
                            lhsT=wct_v[:, D * dc + 128 * ec:
                                       D * dc + 128 * (ec + 1)],
                            rhs=hsb_v[p][:, TT * dc:TT * (dc + 1)],
                            start=(dc == 0), stop=(dc == DC - 1))
                        if dc == 0:
                            i._wait_ge(SM["s_hsl"], 16 * (t + 1))
                        if dc == DC - 1:
                            i.then_inc(SM["s_wcm"], 1)
                for s4 in range(4):
                    u = 4 * t + s4
                    tensor.wait_ge(SM["s_cmb"], 2 * u)
                    for dc in range(DC):
                        i = tensor.transpose(
                            out=pb[2 + dc // 4].ap()[:, 128 * (dc % 4):
                                                     128 * ((dc % 4) + 1)],
                            in_=sgst_v[p][:, TT * dc + 128 * s4:
                                          TT * dc + 128 * (s4 + 1)],
                            identity=ident_f.ap()[:, :])
                        if dc == 0:
                            i._wait_ge(SM["s_sg"], DC * (t + 1))
                        if dc == DC - 1:
                            i.then_inc(SM["s_tr2"], 1)

    return nc


def kernel(**inputs):
    S = 8192
    x = np.asarray(inputs["L0_attn_x"], np.float32)
    A = np.asarray(inputs["L0_attn_A"], np.float32)
    W_B = np.asarray(inputs["L0_attn_W_B"], np.float32)
    b_B = np.asarray(inputs["L0_attn_b_B"], np.float32)
    W_C = np.asarray(inputs["L0_attn_W_C"], np.float32)
    b_C = np.asarray(inputs["L0_attn_b_C"], np.float32)
    w1 = np.asarray(inputs["L0_ffn_w1"], np.float32)
    b1 = np.asarray(inputs["L0_ffn_b1"], np.float32)
    w2 = np.asarray(inputs["L0_ffn_w2"], np.float32)
    b2 = np.asarray(inputs["L0_ffn_b2"], np.float32)

    cm = np.ascontiguousarray
    shared = {
        "wbt": cm(W_B.T), "wct": cm(W_C.T),
        "w1n": cm(w1), "w2n": cm(w2),
        "bB": cm(b_B.reshape(DC, 128).T), "b1": cm(b1.reshape(HC, 128).T),
        "b2": cm(b2.reshape(DC, 128).T), "bC": cm(b_C.reshape(DC, 128).T),
        "afull": cm(A.reshape(DC, 128).T),
    }
    in_maps = []
    for c in range(8):
        m = dict(shared)
        m["x"] = cm(x[c // 2])
        in_maps.append(m)

    nc = build_nc(S)
    res = run_bass_kernel_spmd(nc, in_maps, core_ids=list(range(8)))
    out = np.stack([np.asarray(res.results[2 * b]["y"], np.float32)
                    for b in range(4)], axis=0)
    return out


# version 05
# speedup vs baseline: 1.0449x; 1.0449x over previous
import numpy as np
from contextlib import ExitStack

import concourse.bass as bass
import concourse.mybir as mybir
from concourse.bass_utils import run_bass_kernel_spmd

F32 = mybir.dt.float32
BF16 = mybir.dt.bfloat16
I32 = mybir.dt.int32
AF = mybir.ActivationFunctionType
ALU = mybir.AluOpType
AX = mybir.AxisListType

D = 1024
H = 4096
DC = D // 128
HC = H // 128
TT = 512

N_ENG = 5


def build_nc(S):
    NT = S // TT
    NL = S // 128
    NG = NT
    nc = bass.Bass("TRN2", target_bir_lowering=False, debug=False)

    x_in = nc.dram_tensor("x", [S, D], F32, kind="ExternalInput").ap()
    wbt = nc.dram_tensor("wbt", [D, D], F32, kind="ExternalInput").ap()
    wct = nc.dram_tensor("wct", [D, D], F32, kind="ExternalInput").ap()
    w1n = nc.dram_tensor("w1n", [H, D], F32, kind="ExternalInput").ap()
    w2n = nc.dram_tensor("w2n", [D, H], F32, kind="ExternalInput").ap()
    bB_d = nc.dram_tensor("bB", [128, DC], F32, kind="ExternalInput").ap()
    b1_d = nc.dram_tensor("b1", [128, HC], F32, kind="ExternalInput").ap()
    b2_d = nc.dram_tensor("b2", [128, DC], F32, kind="ExternalInput").ap()
    bC_d = nc.dram_tensor("bC", [128, DC], F32, kind="ExternalInput").ap()
    a_d = nc.dram_tensor("afull", [128, DC], F32, kind="ExternalInput").ap()
    y_out = nc.dram_tensor("y", [S, D], F32, kind="ExternalOutput").ap()

    xnt_d = nc.dram_tensor("xnt_d", [DC, 128, S], BF16, kind="Internal").ap()
    hs_d = nc.dram_tensor("hs_d", [NT, 128, DC * TT], BF16, kind="Internal").ap()
    g_d = nc.dram_tensor("g_d", [NT, 128, DC * TT], F32, kind="Internal").ap()
    s1_d = nc.dram_tensor("s1_d", [16, 128, 2048], BF16, kind="Internal").ap()
    s2_d = nc.dram_tensor("s2_d", [16, 128, 2048], BF16, kind="Internal").ap()

    with ExitStack() as ctx:
        E = ctx.enter_context
        ident_b = E(nc.sbuf_tensor("ident_b", [128, 128], BF16))
        ident_f = E(nc.sbuf_tensor("ident_f", [128, 128], F32))
        io_r = E(nc.sbuf_tensor("io_r", [128, 128], I32))
        io_c = E(nc.sbuf_tensor("io_c", [128, 128], I32))
        wbt_sb = E(nc.sbuf_tensor("wbt_sb", [128, DC * D], BF16))
        bB_sb = E(nc.sbuf_tensor("bB_sb", [128, DC], F32))
        b1_sb = E(nc.sbuf_tensor("b1_sb", [128, HC], F32))
        b2_sb = E(nc.sbuf_tensor("b2_sb", [128, DC], F32))
        bC_sb = E(nc.sbuf_tensor("bC_sb", [128, DC], F32))
        a_sb = E(nc.sbuf_tensor("a_sb", [128, DC], F32))
        al1_sb = E(nc.sbuf_tensor("al1_sb", [128, HC], F32))
        al1s_sb = E(nc.sbuf_tensor("al1s_sb", [128, HC], F32))
        al2_sb = E(nc.sbuf_tensor("al2_sb", [128, DC], F32))
        al2s_sb = E(nc.sbuf_tensor("al2s_sb", [128, DC], F32))
        al2p_sb = E(nc.sbuf_tensor("al2p_sb", [128, 4], F32))
        zs = E(nc.sbuf_tensor("zs", [128, 8], F32))
        m_sc = E(nc.sbuf_tensor("m_sc", [128, 16], F32))
        ln_sc = E(nc.sbuf_tensor("ln_sc", [128, 16], F32))

        AR_W = 188 * 1024
        arena = E(nc.sbuf_tensor("arena", [128, AR_W // 2], BF16))
        ar_f = arena.bitcast(F32)

        def fview(byte_off, n_el):
            assert byte_off % 4 == 0 and byte_off + 4 * n_el <= AR_W, byte_off
            return ar_f.ap()[:, byte_off // 4: byte_off // 4 + n_el]

        def bview(byte_off, n_el):
            assert byte_off % 2 == 0 and byte_off + 2 * n_el <= AR_W, byte_off
            return arena.ap()[:, byte_off // 2: byte_off // 2 + n_el]

        K = 1024
        bxw_v = [fview(0 + 16 * K * p, 8 * TT) for p in range(2)]
        xnw_v = [[bview(32 * K + 8 * K * r + 1024 * dc, TT)
                  for dc in range(DC)] for r in range(3)]
        hswin_v = fview(56 * K, 2 * 8 * TT)
        hsst_v = bview(88 * K, DC * TT)
        xt_v = [fview(96 * K + 4 * K * p, 1024) for p in range(4)]
        xnm_v = [bview(112 * K + 2 * K * p, 1024) for p in range(8)]
        xnst_v = bview(128 * K, DC * TT)
        s1p_v = [bview(136 * K + 4 * K * p, 2048) for p in range(2)]
        s2p_v = [bview(144 * K + 4 * K * p, 2048) for p in range(2)]
        gst_v = [fview(152 * K + 2 * K * p, TT) for p in range(2)]
        hst_v = bview(156 * K, HC * TT)
        wsc = [fview(156 * K + 4 * K * p, 1024) for p in range(2)]
        pstage = bview(72 * K, 2048)
        hsb_v = [bview(0 + 8 * K * p, DC * TT) for p in range(2)]
        gb_v = [fview(16 * K + 16 * K * p, DC * TT) for p in range(2)]
        sst_v = [fview(48 * K + 16 * K * p, DC * TT) for p in range(2)]
        sgst_v = [fview(80 * K + 16 * K * p, DC * TT) for p in range(2)]
        xr_v = [fview(112 * K + 4 * K * s, 1024) for s in range(4)]
        ost_v = [fview(128 * K + 4 * K * s, 1024) for s in range(4)]
        wct_v = bview(144 * K, DC * D)

        pb = [E(nc.psum_tensor(f"pb{i}", [128, 512], F32)) for i in range(6)]
        pb_tr = E(nc.psum_tensor("pb_tr", [128, 1024], BF16))
        pb_u = E(nc.psum_tensor("pb_u", [128, 512], F32))

        sem_names = [
            "s_wld", "s_xld", "s_xld2", "s_xnw", "s_gld", "s_gw",
            "s_lnc", "s_lne", "s_lns", "s_tr", "s_xsp",
            "s_prep", "s_sgn", "s_ssp", "s_wl1", "s_wl2", "s_alv",
            "s_sgT", "s_s1l", "s_s2l",
            "s_f1m", "s_f1e", "s_f2m", "s_f2e", "s_gsp",
            "s_bxm", "s_bxe",
            "ss", "sv", "vm", "s_hc", "s_hss",
            "s_hsl", "s_gl", "s_wcm", "s_wce", "s_sg", "s_tr2", "s_cmb",
            "s_xr", "s_yst", "s_ph",
        ]
        sems = {n: E(nc.semaphore(n)) for n in sem_names}
        SM = sems
        blk = E(nc.Block())

        dma_ctr = {n: 0 for n in sem_names}

        def dma(eng, sem, out, in_, waits=()):
            for s, v in waits:
                if v > 0:
                    eng.wait_ge(SM[s], v)
            pre = dma_ctr[sem]
            d = eng.dma_start(out=out, in_=in_)
            if pre > 0:
                d._wait_ge(SM[sem], pre)
            d.then_inc(SM[sem], 16)
            dma_ctr[sem] = pre + 16
            return d

        def cW(c):
            return ("s_wl1" if c % 2 == 0 else "s_wl2", 16 * (c // 2 + 1))

        def c_avr(p):
            return 32 + (p + 1) + p // 4

        def bw(eng, ph):
            eng.wait_ge(SM["s_ph"], N_ENG * ph)

        def xld_sem(k):
            if k < 20:
                return "s_xld", 16 * (k + 1)
            return "s_xld2", 16 * (k - 19)

        def x_emit(eng, g):
            for kk in range(4):
                k = 4 * g + kk
                ws = []
                if k >= 4:
                    ws = [("s_lns", 2 * (k - 4) + 1), ("s_lnc", 7 * (k - 3))]
                dma(eng, xld_sem(k)[0], xt_v[k % 4],
                    x_in[128 * k:128 * (k + 1), :], waits=ws)

        def xnw_emit(sync, g):
            for dc in range(DC):
                ws = []
                if dc == 0:
                    ws.append(("s_xsp", 16 * DC * (g + 1)))
                    if g >= 3:
                        ws.append(("s_f1m", HC * (g - 2)))
                        ws.append(("s_bxm", DC * (g - 1)))
                dma(sync, "s_xnw", xnw_v[g % 3][dc],
                    xnt_d[dc, :, TT * g:TT * (g + 1)], waits=ws)

        def s1p_emit(sync, t, q):
            P = 16 * t + q
            ws = []
            if P == 0:
                ws.append(("s_ssp", 16 * 16))
            if P >= 2:
                ws.append(("s_f1m", HC * t + 2 * q - 2))
            dma(sync, "s_s1l", s1p_v[P % 2], s1_d[q, :, :], waits=ws)

        def s2p_emit(sync, t, q):
            P = 16 * t + q
            ws = []
            if P == 0:
                ws.append(("s_ssp", 16 * 32))
            if P >= 2:
                ws.append(("s_f2m", (P - 2) // 2 + 1))
            dma(sync, "s_s2l", s2p_v[P % 2], s2_d[q, :, :], waits=ws)

        def ln_ve_parts(vector, k):
            L = ln_sc.ap()[:, 8 * (k % 2):8 * (k % 2) + 8]
            base = 7 * k

            def mk_a(k=k, L=L, base=base):
                p = k % 2
                if k >= 1:
                    vector.wait_ge(SM["s_lnc"], base - 7)
                i = vector.tensor_reduce(out=L[:, 0:1], in_=xt_v[k % 4],
                                         axis=AX.X, op=ALU.add)
                xs, xv = xld_sem(k)
                i._wait_ge(SM[xs], xv)
                i.then_inc(SM["s_lnc"], 1)
                vector.wait_ge(SM["s_lnc"], base + 1)
                i = vector.tensor_scalar_mul(out=L[:, 1:2], in0=L[:, 0:1],
                                             scalar1=1.0 / D)
                i.then_inc(SM["s_lnc"], 1)
                vector.wait_ge(SM["s_lnc"], base + 2)
                i = vector.tensor_tensor(out=L[:, 2:3], in0=L[:, 1:2],
                                         in1=L[:, 1:2], op=ALU.mult)
                i.then_inc(SM["s_lnc"], 1)

            def mk_b1(k=k, L=L, base=base):
                vector.wait_ge(SM["s_lnc"], base + 3)
                i = vector.tensor_scalar(out=L[:, 3:4], in0=L[:, 4:5],
                                         scalar1=1.0 / D, scalar2=L[:, 2:3],
                                         op0=ALU.mult, op1=ALU.subtract)
                i._wait_ge(SM["s_lns"], 2 * k + 1)
                i.then_inc(SM["s_lnc"], 1)
                vector.wait_ge(SM["s_lnc"], base + 4)
                i = vector.tensor_scalar_add(out=L[:, 5:6], in0=L[:, 3:4],
                                             scalar1=1e-5)
                i.then_inc(SM["s_lnc"], 1)
                vector.wait_ge(SM["s_lnc"], base + 5)
                i = vector.reciprocal(out=L[:, 6:7], in_=L[:, 5:6])
                i.then_inc(SM["s_lnc"], 1)

            def mk_b2(k=k, L=L):
                p = k % 2
                if k >= 8:
                    vector.wait_ge(SM["s_tr"], k - 7)
                i = vector.tensor_scalar(out=xnm_v[k % 8], in0=xt_v[k % 4],
                                         scalar1=L[:, 1:2], scalar2=L[:, 7:8],
                                         op0=ALU.subtract, op1=ALU.mult)
                i._wait_ge(SM["s_lns"], 2 * (k + 1))
                i.then_inc(SM["s_lnc"], 1)

            return mk_a, mk_b1, mk_b2

        def ln_compute_ve(vector, g):
            out = []
            for kk in range(4):
                mk_a, mk_b1, mk_b2 = ln_ve_parts(vector, 4 * g + kk)
                out.append((152 + 80 * kk, mk_a))
                out.append((184 + 80 * kk, mk_b1))
                out.append((208 + 80 * kk, mk_b2))
            return out

        def ln_compute_se(scalar, g):
            out = []
            for kk in range(4):
                k = 4 * g + kk
                L = ln_sc.ap()[:, 8 * (k % 2):8 * (k % 2) + 8]

                def mk_sq(k=k, L=L):
                    p = k % 2
                    if k >= 1:
                        scalar.wait_ge(SM["s_lns"], 2 * (k - 1))
                    if k >= 2:
                        scalar.wait_ge(SM["s_lnc"], 7 * (k - 2) + 4)
                    if k >= 8:
                        scalar.wait_ge(SM["s_tr"], k - 7)
                    i = scalar.activation(out=xnm_v[k % 8], in_=xt_v[k % 4],
                                          func=AF.Square, accum_out=L[:, 4:5])
                    xs, xv = xld_sem(k)
                    i._wait_ge(SM[xs], xv)
                    i.then_inc(SM["s_lns"], 1)

                def mk_sqrt(k=k, L=L):
                    i = scalar.activation(out=L[:, 7:8], in_=L[:, 6:7],
                                          func=AF.Sqrt)
                    i._wait_ge(SM["s_lnc"], 7 * k + 6)
                    i.then_inc(SM["s_lns"], 1)

                out.append((176 + 80 * kk, mk_sq))
                out.append((196 + 80 * kk, mk_sqrt))
            return out

        def tr_emit(tensor, g, kk):
            k = 4 * g + kk
            tensor.wait_ge(SM["s_lne"], 8 * k)
            for dcb in range(DC):
                i = tensor.transpose(
                    out=pb_tr.ap()[:, 128 * dcb:128 * (dcb + 1)],
                    in_=xnm_v[k % 8][:, 128 * dcb:128 * (dcb + 1)],
                    identity=ident_b.ap()[:, :])
                if dcb == 0:
                    i._wait_ge(SM["s_lnc"], 7 * k + 7)
                if dcb == DC - 1:
                    i.then_inc(SM["s_tr"], 1)

        def ln_evac_tile(vector, k):
            kk = k % 4
            g = k // 4
            fns = []
            for dcb in range(DC):
                def mk(k=k, kk=kk, dcb=dcb, g=g):
                    if dcb == 0 and kk == 0 and g > 0:
                        vector.wait_ge(SM["s_xsp"], 16 * DC * g)
                    i = vector.tensor_copy(
                        out=xnst_v[:, TT * dcb + 128 * kk:
                                   TT * dcb + 128 * (kk + 1)],
                        in_=pb_tr.ap()[:, 128 * dcb:128 * (dcb + 1)])
                    if dcb == 0:
                        i._wait_ge(SM["s_tr"], k + 1)
                    i.then_inc(SM["s_lne"], 1)
                fns.append(mk)
            return fns

        def ln_evac(vector, g):
            out = []
            for kk in range(4):
                for dcb, mk in enumerate(ln_evac_tile(vector, 4 * g + kk)):
                    out.append((12 + 4 * (8 * kk + dcb), mk))
            return out

        def xsp_emit(gpsimd, g):
            for dc in range(DC):
                ws = [("s_lne", 8 * (4 * g + 4))] if dc == 0 else []
                dma(gpsimd, "s_xsp", xnt_d[dc, :, TT * g:TT * (g + 1)],
                    xnst_v[:, TT * dc:TT * (dc + 1)], waits=ws)

        def bx_mm(tensor, w):
            for ec in range(DC):
                g = DC * w + ec
                if g >= 2:
                    tensor.wait_ge(SM["s_bxe"], g - 1)
                for dc in range(DC):
                    i = tensor.matmul(
                        out=pb[ec % 2].ap()[:, :],
                        lhsT=wbt_sb.ap()[:, D * dc + 128 * ec:
                                         D * dc + 128 * (ec + 1)],
                        rhs=xnw_v[w % 3][dc],
                        start=(dc == 0), stop=(dc == DC - 1))
                    if dc == 0:
                        i._wait_ge(SM["s_xnw"], 16 * DC * (w + 1))
                    if dc == DC - 1:
                        i.then_inc(SM["s_bxm"], 1)

        def f1_mm(tensor, t):
            if t == 0:
                tensor.wait_ge(SM["s_sgn"], 16)
            for q in range(16):
                tensor.wait_ge(SM["s_s1l"], 16 * (16 * t + q + 1))
                for r in range(2):
                    hc = 2 * q + r
                    g = HC * t + hc
                    if g >= 2:
                        tensor.wait_ge(SM["s_f1e"], g - 1)
                    for dc in range(DC):
                        i = tensor.matmul(
                            out=pb[2 + hc % 2].ap()[:, :],
                            lhsT=s1p_v[(16 * t + q) % 2][:, r * 1024 + dc * 128:
                                                         r * 1024 + dc * 128 + 128],
                            rhs=xnw_v[t % 3][dc],
                            start=(dc == 0), stop=(dc == DC - 1))
                        if dc == 0:
                            i._wait_ge(SM["s_xnw"], 16 * DC * (t + 1))
                        if dc == DC - 1:
                            i.then_inc(SM["s_f1m"], 1)

        def f2_mm(tensor, t, douts):
            if t == 0 and douts[0] == 0:
                tensor.wait_ge(SM["s_sgn"], 32)
            for dout in douts:
                g = DC * t + dout
                if g >= 2:
                    tensor.wait_ge(SM["s_f2e"], g - 1)
                for hc in range(HC):
                    q2 = 2 * dout + hc // 16
                    if hc % 16 == 0:
                        tensor.wait_ge(SM["s_s2l"], 16 * (16 * t + q2 + 1))
                    i = tensor.matmul(
                        out=pb[4 + dout % 2].ap()[:, :],
                        lhsT=s2p_v[(16 * t + q2) % 2][:, (hc % 16) * 128:
                                                      (hc % 16) * 128 + 128],
                        rhs=hst_v[:, TT * hc:TT * (hc + 1)],
                        start=(hc == 0), stop=(hc == HC - 1))
                    i._wait_ge(SM["s_f1e"], HC * t + hc + 1)
                    if hc == HC - 1:
                        i.then_inc(SM["s_f2m"], 1)

        @blk.sync
        def _(sync):
            for srcp, dst in ((bB_d, bB_sb), (b1_d, b1_sb), (b2_d, b2_sb),
                              (bC_d, bC_sb), (a_d, a_sb)):
                dma(sync, "s_wld", dst.ap()[:, :], srcp[:, :])
            for c in range(HC):
                ws = []
                if c >= 2:
                    ws = [("s_alv", c - 1), ("s_sgT", c - 1)]
                dma(sync, cW(c)[0], wsc[c % 2],
                    w1n[128 * c:128 * (c + 1), :], waits=ws)
            for p in range(32):
                c = 32 + p
                ws = [("s_alv", 31 + p if p < 2 else c_avr(p - 2)),
                      ("s_sgT", c - 1)]
                dc, jp = p // 4, p % 4
                dma(sync, cW(c)[0], wsc[c % 2],
                    w2n[128 * dc:128 * (dc + 1), 1024 * jp:1024 * (jp + 1)],
                    waits=ws)
            sync.wait_ge(SM["s_wld"], 16 * 5)
            sync.wait_ge(SM["s_wl1"], 16 * 32)
            sync.wait_ge(SM["s_wl2"], 16 * 32)
            sync.nop().then_inc(SM["s_ph"], 1)
            bw(sync, 1)
            for g in range(min(5, NG)):
                x_emit(sync, g)
            xnw_emit(sync, 0)
            if NT > 1:
                xnw_emit(sync, 1)
            for w in range(NT):
                if w + 2 < NT:
                    xnw_emit(sync, w + 2)
                if w >= 1:
                    for q in range(16):
                        s2p_emit(sync, w - 1, q)
                for q in range(16):
                    s1p_emit(sync, w, q)
            for q in range(16):
                s2p_emit(sync, NT - 1, q)
            sync.wait_ge(SM["s_xld"], 16 * min(NL, 20))
            sync.wait_ge(SM["s_xnw"], 16 * DC * NT)
            sync.wait_ge(SM["s_s1l"], 16 * 16 * NT)
            sync.wait_ge(SM["s_s2l"], 16 * 16 * NT)
            sync.nop().then_inc(SM["s_ph"], 1)
            bw(sync, 2)
            for t in range(NT):
                dma(sync, "s_hsl", hsb_v[t % 2], hs_d[t, :, :],
                    waits=[("s_wcm", DC * (t - 1) if t >= 2 else 0)])
                for s4 in range(4):
                    dma(sync, "s_xr", xr_v[s4],
                        x_in[TT * t + 128 * s4:TT * t + 128 * (s4 + 1), :],
                        waits=[("s_cmb", 8 * t if (t >= 1 and s4 == 0) else 0)])
            sync.wait_ge(SM["s_yst"], 16 * 4 * NT)

        @blk.gpsimd
        def _(gpsimd):
            gpsimd.iota(io_r.ap()[:, :], pattern=[[1, 128]], base=0,
                        channel_multiplier=0).then_inc(SM["s_gld"], 1)
            gpsimd.iota(io_c.ap()[:, :], pattern=[[0, 128]], base=0,
                        channel_multiplier=1).then_inc(SM["s_gld"], 1)
            for i in range(DC):
                dma(gpsimd, "s_gw", wbt_sb.ap()[:, i * D:(i + 1) * D],
                    wbt[128 * i:128 * (i + 1), :])
            for q in range(16):
                dma(gpsimd, "s_ssp", s1_d[q, :, :], pstage,
                    waits=[("s_sgn", 4 * q + 4)])
            for q2 in range(16):
                dma(gpsimd, "s_ssp", s2_d[q2, :, :], pstage,
                    waits=[("s_sgn", 64 + 4 * q2 + 4)])
            gpsimd.wait_ge(SM["s_gld"], 2)
            gpsimd.wait_ge(SM["s_gw"], 16 * DC)
            gpsimd.wait_ge(SM["s_ssp"], 16 * 32)
            gpsimd.engine_nop().then_inc(SM["s_ph"], 1)
            bw(gpsimd, 1)
            for g in range(min(3, NG)):
                xsp_emit(gpsimd, g)
            for w in range(NT):
                if w + 3 < NG:
                    xsp_emit(gpsimd, w + 3)
                if w >= 1:
                    for dout in range(DC):
                        dma(gpsimd, "s_gsp",
                            g_d[w - 1, :, TT * dout:TT * (dout + 1)],
                            gst_v[dout % 2],
                            waits=[("s_f2e", DC * (w - 1) + dout + 1)])
                    dma(gpsimd, "s_hss", hs_d[w - 1, :, :], hsst_v,
                        waits=[("s_hc", DC * w)])
                if w + 5 < NG:
                    x_emit(gpsimd, w + 5)
            dma(gpsimd, "s_hss", hs_d[NT - 1, :, :], hsst_v,
                waits=[("s_hc", DC * NT)])
            for dout in range(DC):
                dma(gpsimd, "s_gsp", g_d[NT - 1, :, TT * dout:TT * (dout + 1)],
                    gst_v[dout % 2],
                    waits=[("s_f2e", DC * (NT - 1) + dout + 1)])
            gpsimd.wait_ge(SM["s_hss"], 16 * NT)
            gpsimd.wait_ge(SM["s_gsp"], 16 * DC * NT)
            gpsimd.wait_ge(SM["s_xsp"], 16 * DC * NG)
            if NL > 20:
                gpsimd.wait_ge(SM["s_xld2"], 16 * (NL - 20))
            gpsimd.engine_nop().then_inc(SM["s_ph"], 1)
            bw(gpsimd, 2)
            for i in range(DC):
                dma(gpsimd, "s_gw", wct_v[:, i * D:(i + 1) * D],
                    wct[128 * i:128 * (i + 1), :])
            for t in range(NT):
                dma(gpsimd, "s_gl", gb_v[t % 2], g_d[t, :, :],
                    waits=[("s_sg", DC * (t - 1) if t >= 2 else 0)])
                for s4 in range(4):
                    u = 4 * t + s4
                    dma(gpsimd, "s_yst",
                        y_out[TT * t + 128 * s4:TT * t + 128 * (s4 + 1), :],
                        ost_v[s4], waits=[("s_cmb", 2 * (u + 1))])

        @blk.vector
        def _(vector):
            i = vector.tensor_tensor(out=ident_b.ap()[:, :], in0=io_r.ap()[:, :],
                                     in1=io_c.ap()[:, :], op=ALU.is_equal)
            i._wait_ge(SM["s_gld"], 2)
            i.then_inc(SM["s_prep"], 1)
            i = vector.tensor_copy(out=ident_f.ap()[:, :], in_=ident_b.ap()[:, :])
            i._wait_ge(SM["s_prep"], 1)
            i.then_inc(SM["s_prep"], 1)
            vector.memset(zs.ap()[:, :], 0.0).then_inc(SM["s_prep"], 1)
            for hc in range(HC):
                i = vector.tensor_reduce(out=al1_sb.ap()[:, hc:hc + 1],
                                         in_=wsc[hc % 2], axis=AX.X,
                                         op=ALU.add, apply_absolute_value=True)
                i._wait_ge(SM[cW(hc)[0]], cW(hc)[1])
                i.then_inc(SM["s_alv"], 1)
            for p in range(32):
                if p % 4 == 0 and p >= 4:
                    vector.wait_ge(SM["s_alv"], c_avr(p - 1) + 1)
                i = vector.tensor_reduce(out=al2p_sb.ap()[:, p % 4:p % 4 + 1],
                                         in_=wsc[(32 + p) % 2], axis=AX.X,
                                         op=ALU.add, apply_absolute_value=True)
                i._wait_ge(SM[cW(32 + p)[0]], cW(32 + p)[1])
                i.then_inc(SM["s_alv"], 1)
                if p % 4 == 3:
                    i = vector.tensor_reduce(out=al2_sb.ap()[:, p // 4:p // 4 + 1],
                                             in_=al2p_sb.ap()[:, :], axis=AX.X,
                                             op=ALU.add)
                    i._wait_ge(SM["s_alv"], c_avr(p))
                    i.then_inc(SM["s_alv"], 1)
            vector.wait_ge(SM["s_alv"], 72)
            vector.tensor_scalar_mul(out=al1s_sb.ap()[:, :], in0=al1_sb.ap()[:, :],
                                     scalar1=1.0 / D).then_inc(SM["s_prep"], 1)
            vector.tensor_scalar_mul(out=al2s_sb.ap()[:, :], in0=al2_sb.ap()[:, :],
                                     scalar1=1.0 / H).then_inc(SM["s_prep"], 1)
            vector.wait_ge(SM["s_prep"], 5)
            vector.engine_nop().then_inc(SM["s_ph"], 1)
            bw(vector, 1)
            pre_tiles = 4 * min(4, NG)
            pre_ev = 4 * min(3, NG)
            for k in range(pre_tiles):
                mk_a, mk_b1, mk_b2 = ln_ve_parts(vector, k)
                mk_a()
                mk_b1()
                mk_b2()
                if k >= 1 and (k - 1) < pre_ev:
                    for fn in ln_evac_tile(vector, k - 1):
                        fn()
            if pre_ev >= pre_tiles:
                for fn in ln_evac_tile(vector, pre_tiles - 1):
                    fn()
            hswr = hswin_v.rearrange("p (s c) -> p c s", c=8)
            for w in range(NT):
                ins = {}

                def put(j, fn):
                    ins.setdefault(j, []).append(fn)
                if w >= 1:
                    for dc in range(DC):
                        def mkc(dc=dc, w=w):
                            pw = (w - 1) % 2
                            if dc == 0:
                                vector.wait_ge(SM["s_hss"], 16 * (w - 1))
                            i = vector.tensor_copy(
                                out=hsst_v[:, TT * dc:TT * (dc + 1)],
                                in_=hswr[:, dc, pw * TT:pw * TT + TT])
                            i._wait_ge(SM["ss"], w * TT)
                            i.then_inc(SM["s_hc"], 1)
                        put(64 + 56 * dc, mkc)
                if w + 3 < NG:
                    for j, fn in ln_evac(vector, w + 3):
                        put(j, fn)
                if w + 4 < NG:
                    for j, fn in ln_compute_ve(vector, w + 4):
                        put(j, fn)
                vector.wait_ge(SM["s_bxe"], DC * (w + 1))
                for j in range(TT):
                    t = w * TT + j
                    prev = (t - 1) % (2 * TT)
                    hp = zs.ap()[:, 0:8] if t == 0 else \
                        hswin_v[:, 8 * prev:8 * prev + 8]
                    i1 = vector.tensor_tensor(
                        out=m_sc.ap()[:, 8 * (t % 2):8 * (t % 2) + 8],
                        in0=hp, in1=a_sb.ap()[:, 0:8], op=ALU.mult)
                    if t > 0:
                        i1._wait_ge(SM["ss"], t)
                    i1.then_inc(SM["vm"], 1)
                    usl = t % 64
                    i2 = vector.tensor_tensor(
                        out=pb_u.ap()[:, 8 * usl:8 * usl + 8],
                        in0=m_sc.ap()[:, 8 * (t % 2):8 * (t % 2) + 8],
                        in1=bxw_v[w % 2][:, 8 * j:8 * j + 8], op=ALU.add)
                    i2._wait_ge(SM["vm"], t + 1)
                    i2.then_inc(SM["sv"], 1)
                    for fn in ins.get(j, ()):
                        fn()
            vector.wait_ge(SM["ss"], NT * TT)
            vector.wait_ge(SM["s_hss"], 16 * (NT - 1))
            pw = (NT - 1) % 2
            for dc in range(DC):
                i = vector.tensor_copy(
                    out=hsst_v[:, TT * dc:TT * (dc + 1)],
                    in_=hswr[:, dc, pw * TT:pw * TT + TT])
                i.then_inc(SM["s_hc"], 1)
            vector.wait_ge(SM["s_hc"], DC * NT)
            vector.wait_ge(SM["sv"], S)
            vector.wait_ge(SM["vm"], S)
            vector.wait_ge(SM["s_lne"], 8 * NL)
            vector.wait_ge(SM["s_lnc"], 7 * NL)
            vector.engine_nop().then_inc(SM["s_ph"], 1)
            bw(vector, 2)
            for t in range(NT):
                p = t % 2
                if t >= 1:
                    vector.wait_ge(SM["s_wce"], DC * (t - 1))
                    vector.wait_ge(SM["s_yst"], 16 * 4 * t)
                for ec in range(DC):
                    i = vector.tensor_scalar_add(
                        out=sst_v[p][:, TT * ec:TT * (ec + 1)],
                        in0=pb[ec % 2].ap()[:, :],
                        scalar1=bC_sb.ap()[:, ec:ec + 1])
                    i._wait_ge(SM["s_wcm"], DC * t + ec + 1)
                    i.then_inc(SM["s_wce"], 1)
                vector.wait_ge(SM["s_wce"], DC * (t + 1))
                for ec in range(DC):
                    i = vector.tensor_tensor(
                        out=sgst_v[p][:, TT * ec:TT * (ec + 1)],
                        in0=sst_v[p][:, TT * ec:TT * (ec + 1)],
                        in1=gb_v[p][:, TT * ec:TT * (ec + 1)], op=ALU.mult)
                    if ec == 0:
                        i._wait_ge(SM["s_gl"], 16 * (t + 1))
                    i.then_inc(SM["s_sg"], 1)
                for s4 in range(4):
                    u = 4 * t + s4
                    vector.wait_ge(SM["s_xr"], 16 * (u + 1))
                    i = vector.tensor_tensor(
                        out=ost_v[s4][:, 0:512], in0=pb[2].ap()[:, :],
                        in1=xr_v[s4][:, 0:512], op=ALU.add)
                    i._wait_ge(SM["s_tr2"], u + 1)
                    i.then_inc(SM["s_cmb"], 1)
                    i = vector.tensor_tensor(
                        out=ost_v[s4][:, 512:1024], in0=pb[3].ap()[:, :],
                        in1=xr_v[s4][:, 512:1024], op=ALU.add)
                    i._wait_ge(SM["s_xr"], 16 * (u + 1))
                    i.then_inc(SM["s_cmb"], 1)

        @blk.scalar
        def _(scalar):
            for c in range(64):
                for half in range(2):
                    if half == 0 and c >= 2:
                        scalar.wait_ge(SM["s_ssp"], 16 * (c // 2))
                    i = scalar.activation(
                        out=pstage[:, (c % 2) * 1024 + 512 * half:
                                   (c % 2) * 1024 + 512 * (half + 1)],
                        in_=pb[4 + half].ap()[:, :], func=AF.Sign)
                    i._wait_ge(SM["s_sgT"], c + 1)
                    i.then_inc(SM["s_sgn"], 1)
            scalar.wait_ge(SM["s_sgn"], 128)
            scalar.nop().then_inc(SM["s_ph"], 1)
            bw(scalar, 1)
            for g in range(min(4, NG)):
                for _, fn in sorted(ln_compute_se(scalar, g), key=lambda x: x[0]):
                    fn()
            bxwr = [bxw_v[p].rearrange("p (t c) -> p c t", c=8) for p in range(2)]
            for ec in range(DC):
                i = scalar.add(out=bxwr[0][:, ec, :], in_=pb[ec % 2].ap()[:, :],
                               add=bB_sb.ap()[:, ec:ec + 1])
                i._wait_ge(SM["s_bxm"], ec + 1)
                i.then_inc(SM["s_bxe"], 1)
            for w in range(NT):
                ins = {}

                def put(j, fn):
                    ins.setdefault(j, []).append(fn)
                if w + 1 < NT:
                    for ec in range(DC):
                        def mke(ec=ec, w=w):
                            if ec == 0:
                                scalar.wait_ge(SM["sv"], w * TT)
                            i = scalar.add(out=bxwr[(w + 1) % 2][:, ec, :],
                                           in_=pb[ec % 2].ap()[:, :],
                                           add=bB_sb.ap()[:, ec:ec + 1])
                            i._wait_ge(SM["s_bxm"], DC * (w + 1) + ec + 1)
                            i.then_inc(SM["s_bxe"], 1)
                        put(150 + 8 * ec, mke)
                if w >= 1:
                    for dout in range(DC):
                        def mks(dout=dout, w=w):
                            t = w - 1
                            if DC * t + dout >= 2:
                                scalar.wait_ge(SM["s_gsp"],
                                               16 * (DC * t + dout - 1))
                            i = scalar.activation(
                                out=gst_v[dout % 2],
                                in_=pb[4 + dout % 2].ap()[:, :], func=AF.Silu,
                                bias=b2_sb.ap()[:, dout:dout + 1],
                                scale=al2s_sb.ap()[:, dout:dout + 1])
                            i._wait_ge(SM["s_f2m"], DC * t + dout + 1)
                            i.then_inc(SM["s_f2e"], 1)
                        put(230 + 16 * dout, mks)
                for hc in range(HC):
                    def mkg(hc=hc, w=w):
                        if hc == 0:
                            scalar.wait_ge(SM["s_f2m"], DC * w)
                            if w == 0:
                                scalar.wait_ge(SM["s_prep"], 5)
                        i = scalar.activation(
                            out=hst_v[:, TT * hc:TT * (hc + 1)],
                            in_=pb[2 + hc % 2].ap()[:, :], func=AF.Gelu,
                            bias=b1_sb.ap()[:, hc:hc + 1],
                            scale=al1s_sb.ap()[:, hc:hc + 1])
                        i._wait_ge(SM["s_f1m"], HC * w + hc + 1)
                        i.then_inc(SM["s_f1e"], 1)
                    put(384 + 4 * hc, mkg)
                if w + 4 < NG:
                    for j, fn in ln_compute_se(scalar, w + 4):
                        put(j, fn)
                if w >= 2:
                    scalar.wait_ge(SM["s_hc"], DC * (w - 1))
                for j in range(TT):
                    t = w * TT + j
                    sl = t % (2 * TT)
                    usl = t % 64
                    i3 = scalar.activation(
                        out=hswin_v[:, 8 * sl:8 * sl + 8],
                        in_=pb_u.ap()[:, 8 * usl:8 * usl + 8], func=AF.Tanh)
                    i3._wait_ge(SM["sv"], t + 1)
                    i3.then_inc(SM["ss"], 1)
                    for fn in ins.get(j, ()):
                        fn()
            for dout in range(DC):
                t = NT - 1
                if DC * t + dout >= 2:
                    scalar.wait_ge(SM["s_gsp"], 16 * (DC * t + dout - 1))
                i = scalar.activation(
                    out=gst_v[dout % 2], in_=pb[4 + dout % 2].ap()[:, :],
                    func=AF.Silu, bias=b2_sb.ap()[:, dout:dout + 1],
                    scale=al2s_sb.ap()[:, dout:dout + 1])
                i._wait_ge(SM["s_f2m"], DC * t + dout + 1)
                i.then_inc(SM["s_f2e"], 1)
            scalar.wait_ge(SM["ss"], S)
            scalar.wait_ge(SM["s_bxe"], DC * NT)
            scalar.wait_ge(SM["s_f1e"], HC * NT)
            scalar.wait_ge(SM["s_f2e"], DC * NT)
            scalar.wait_ge(SM["s_lns"], 2 * NL)
            scalar.nop().then_inc(SM["s_ph"], 1)
            bw(scalar, 2)

        @blk.tensor
        def _(tensor):
            tensor.wait_ge(SM["s_prep"], 2)
            for c in range(64):
                if c >= 1:
                    tensor.wait_ge(SM["s_sgn"], 2 * c)
                for b in range(8):
                    bank = pb[4 + b // 4]
                    i = tensor.transpose(
                        out=bank.ap()[:, 128 * (b % 4):128 * (b % 4 + 1)],
                        in_=wsc[c % 2][:, 128 * b:128 * (b + 1)],
                        identity=ident_f.ap()[:, :])
                    if b == 0:
                        i._wait_ge(SM[cW(c)[0]], cW(c)[1])
                    if b == 7:
                        i.then_inc(SM["s_sgT"], 1)
            tensor.wait_ge(SM["s_sgT"], 64)
            tensor.nop().then_inc(SM["s_ph"], 1)
            bw(tensor, 1)
            for g in range(min(3, NG)):
                for kk in range(4):
                    tr_emit(tensor, g, kk)
            bx_mm(tensor, 0)
            for w in range(NT):
                if w + 3 < NG:
                    for kk in range(4):
                        tr_emit(tensor, w + 3, kk)
                if w + 1 < NT:
                    bx_mm(tensor, w + 1)
                if w >= 1:
                    f2_mm(tensor, w - 1, [0, 1, 2, 3])
                    f2_mm(tensor, w - 1, [4, 5, 6, 7])
                f1_mm(tensor, w)
            f2_mm(tensor, NT - 1, [0, 1, 2, 3])
            f2_mm(tensor, NT - 1, [4, 5, 6, 7])
            tensor.wait_ge(SM["s_bxm"], DC * NT)
            tensor.wait_ge(SM["s_f1m"], HC * NT)
            tensor.wait_ge(SM["s_f2m"], DC * NT)
            tensor.wait_ge(SM["s_tr"], NL)
            tensor.nop().then_inc(SM["s_ph"], 1)
            bw(tensor, 2)
            tensor.wait_ge(SM["s_gw"], 16 * 2 * DC)
            for t in range(NT):
                p = t % 2
                for ec in range(DC):
                    g = DC * t + ec
                    if g >= 2:
                        tensor.wait_ge(SM["s_wce"], g - 1)
                    for dc in range(DC):
                        i = tensor.matmul(
                            out=pb[ec % 2].ap()[:, :],
                            lhsT=wct_v[:, D * dc + 128 * ec:
                                       D * dc + 128 * (ec + 1)],
                            rhs=hsb_v[p][:, TT * dc:TT * (dc + 1)],
                            start=(dc == 0), stop=(dc == DC - 1))
                        if dc == 0:
                            i._wait_ge(SM["s_hsl"], 16 * (t + 1))
                        if dc == DC - 1:
                            i.then_inc(SM["s_wcm"], 1)
                for s4 in range(4):
                    u = 4 * t + s4
                    tensor.wait_ge(SM["s_cmb"], 2 * u)
                    for dc in range(DC):
                        i = tensor.transpose(
                            out=pb[2 + dc // 4].ap()[:, 128 * (dc % 4):
                                                     128 * ((dc % 4) + 1)],
                            in_=sgst_v[p][:, TT * dc + 128 * s4:
                                          TT * dc + 128 * (s4 + 1)],
                            identity=ident_f.ap()[:, :])
                        if dc == 0:
                            i._wait_ge(SM["s_sg"], DC * (t + 1))
                        if dc == DC - 1:
                            i.then_inc(SM["s_tr2"], 1)

    return nc


def kernel(**inputs):
    S = 8192
    x = np.asarray(inputs["L0_attn_x"], np.float32)
    A = np.asarray(inputs["L0_attn_A"], np.float32)
    W_B = np.asarray(inputs["L0_attn_W_B"], np.float32)
    b_B = np.asarray(inputs["L0_attn_b_B"], np.float32)
    W_C = np.asarray(inputs["L0_attn_W_C"], np.float32)
    b_C = np.asarray(inputs["L0_attn_b_C"], np.float32)
    w1 = np.asarray(inputs["L0_ffn_w1"], np.float32)
    b1 = np.asarray(inputs["L0_ffn_b1"], np.float32)
    w2 = np.asarray(inputs["L0_ffn_w2"], np.float32)
    b2 = np.asarray(inputs["L0_ffn_b2"], np.float32)

    cm = np.ascontiguousarray
    shared = {
        "wbt": cm(W_B.T), "wct": cm(W_C.T),
        "w1n": cm(w1), "w2n": cm(w2),
        "bB": cm(b_B.reshape(DC, 128).T), "b1": cm(b1.reshape(HC, 128).T),
        "b2": cm(b2.reshape(DC, 128).T), "bC": cm(b_C.reshape(DC, 128).T),
        "afull": cm(A.reshape(DC, 128).T),
    }
    in_maps = []
    for c in range(8):
        m = dict(shared)
        m["x"] = cm(x[c // 2])
        in_maps.append(m)

    nc = build_nc(S)
    res = run_bass_kernel_spmd(nc, in_maps, core_ids=list(range(8)))
    out = np.stack([np.asarray(res.results[2 * b]["y"], np.float32)
                    for b in range(4)], axis=0)
    return out


# version 06
# speedup vs baseline: 1.0587x; 1.0132x over previous
import numpy as np
from contextlib import ExitStack

import concourse.bass as bass
import concourse.mybir as mybir
from concourse.bass_utils import run_bass_kernel_spmd

F32 = mybir.dt.float32
BF16 = mybir.dt.bfloat16
I32 = mybir.dt.int32
AF = mybir.ActivationFunctionType
ALU = mybir.AluOpType
AX = mybir.AxisListType

D = 1024
H = 4096
DC = D // 128
HC = H // 128
TT = 512

N_ENG = 5


def build_nc(S):
    NT = S // TT
    NL = S // 128
    NG = NT
    nc = bass.Bass("TRN2", target_bir_lowering=False, debug=False)

    x_in = nc.dram_tensor("x", [S, D], F32, kind="ExternalInput").ap()
    wbt = nc.dram_tensor("wbt", [D, D], F32, kind="ExternalInput").ap()
    wct = nc.dram_tensor("wct", [D, D], F32, kind="ExternalInput").ap()
    w1n = nc.dram_tensor("w1n", [H, D], F32, kind="ExternalInput").ap()
    w2n = nc.dram_tensor("w2n", [D, H], F32, kind="ExternalInput").ap()
    bB_d = nc.dram_tensor("bB", [128, DC], F32, kind="ExternalInput").ap()
    b1_d = nc.dram_tensor("b1", [128, HC], F32, kind="ExternalInput").ap()
    b2_d = nc.dram_tensor("b2", [128, DC], F32, kind="ExternalInput").ap()
    bC_d = nc.dram_tensor("bC", [128, DC], F32, kind="ExternalInput").ap()
    a_d = nc.dram_tensor("afull", [128, DC], F32, kind="ExternalInput").ap()
    y_out = nc.dram_tensor("y", [S, D], F32, kind="ExternalOutput").ap()

    xnt_d = nc.dram_tensor("xnt_d", [DC, 128, S], BF16, kind="Internal").ap()
    hs_d = nc.dram_tensor("hs_d", [NT, 128, DC * TT], BF16, kind="Internal").ap()
    g_d = nc.dram_tensor("g_d", [NT, 128, DC * TT], F32, kind="Internal").ap()
    s1_d = nc.dram_tensor("s1_d", [16, 128, 2048], BF16, kind="Internal").ap()
    s2_d = nc.dram_tensor("s2_d", [16, 128, 2048], BF16, kind="Internal").ap()

    with ExitStack() as ctx:
        E = ctx.enter_context
        ident_b = E(nc.sbuf_tensor("ident_b", [128, 128], BF16))
        ident_f = E(nc.sbuf_tensor("ident_f", [128, 128], F32))
        io_r = E(nc.sbuf_tensor("io_r", [128, 128], I32))
        io_c = E(nc.sbuf_tensor("io_c", [128, 128], I32))
        wbt_sb = E(nc.sbuf_tensor("wbt_sb", [128, DC * D], BF16))
        bB_sb = E(nc.sbuf_tensor("bB_sb", [128, DC], F32))
        b1_sb = E(nc.sbuf_tensor("b1_sb", [128, HC], F32))
        b2_sb = E(nc.sbuf_tensor("b2_sb", [128, DC], F32))
        bC_sb = E(nc.sbuf_tensor("bC_sb", [128, DC], F32))
        a_sb = E(nc.sbuf_tensor("a_sb", [128, DC], F32))
        al1_sb = E(nc.sbuf_tensor("al1_sb", [128, HC], F32))
        al1s_sb = E(nc.sbuf_tensor("al1s_sb", [128, HC], F32))
        al2_sb = E(nc.sbuf_tensor("al2_sb", [128, DC], F32))
        al2s_sb = E(nc.sbuf_tensor("al2s_sb", [128, DC], F32))
        al2p_sb = E(nc.sbuf_tensor("al2p_sb", [128, 4], F32))
        zs = E(nc.sbuf_tensor("zs", [128, 8], F32))
        m_sc = E(nc.sbuf_tensor("m_sc", [128, 16], F32))
        ln_sc = E(nc.sbuf_tensor("ln_sc", [128, 16], F32))

        AR_W = 188 * 1024
        arena = E(nc.sbuf_tensor("arena", [128, AR_W // 2], BF16))
        ar_f = arena.bitcast(F32)

        def fview(byte_off, n_el):
            assert byte_off % 4 == 0 and byte_off + 4 * n_el <= AR_W, byte_off
            return ar_f.ap()[:, byte_off // 4: byte_off // 4 + n_el]

        def bview(byte_off, n_el):
            assert byte_off % 2 == 0 and byte_off + 2 * n_el <= AR_W, byte_off
            return arena.ap()[:, byte_off // 2: byte_off // 2 + n_el]

        K = 1024
        bxw_v = [fview(0 + 16 * K * p, 8 * TT) for p in range(2)]
        xnw_v = [[bview(32 * K + 8 * K * r + 1024 * dc, TT)
                  for dc in range(DC)] for r in range(3)]
        hswin_v = fview(56 * K, 2 * 8 * TT)
        hsst_v = bview(88 * K, DC * TT)
        xt_v = [fview(96 * K + 4 * K * p, 1024) for p in range(4)]
        xnm_v = [bview(112 * K + 2 * K * p, 1024) for p in range(8)]
        xnst_v = bview(128 * K, DC * TT)
        s1p_v = [bview(136 * K + 4 * K * p, 2048) for p in range(2)]
        s2p_v = [bview(144 * K + 4 * K * p, 2048) for p in range(2)]
        gst_v = [fview(152 * K + 2 * K * p, TT) for p in range(2)]
        hst_v = bview(156 * K, HC * TT)
        wsc = [fview(156 * K + 4 * K * p, 1024) for p in range(4)]
        pstage = [bview(72 * K + 4 * K * p, 2048) for p in range(2)]
        hsb_v = [bview(0 + 8 * K * p, DC * TT) for p in range(2)]
        gb_v = [fview(16 * K + 16 * K * p, DC * TT) for p in range(2)]
        sst_v = [fview(48 * K + 16 * K * p, DC * TT) for p in range(2)]
        sgst_v = [fview(80 * K + 16 * K * p, DC * TT) for p in range(2)]
        xr_v = [fview(112 * K + 4 * K * s, 1024) for s in range(4)]
        ost_v = [fview(128 * K + 4 * K * s, 1024) for s in range(4)]
        wct_v = bview(144 * K, DC * D)

        pb = [E(nc.psum_tensor(f"pb{i}", [128, 512], F32)) for i in range(6)]
        pb_tr = E(nc.psum_tensor("pb_tr", [128, 1024], BF16))
        pb_u = E(nc.psum_tensor("pb_u", [128, 512], F32))

        sem_names = [
            "s_wld", "s_xld", "s_xld2", "s_xnw", "s_gld", "s_gw",
            "s_lnc", "s_lne", "s_lns", "s_tr", "s_xsp",
            "s_prep", "s_sgn", "s_ssp", "s_wl1", "s_wl2", "s_alv",
            "s_sgT", "s_s1l", "s_s2l",
            "s_f1m", "s_f1e", "s_f2m", "s_f2e", "s_gsp",
            "s_bxm", "s_bxe",
            "ss", "sv", "vm", "s_hc", "s_hss",
            "s_hsl", "s_gl", "s_wcm", "s_wce", "s_sg", "s_tr2", "s_cmb",
            "s_xr", "s_yst", "s_ph",
        ]
        sems = {n: E(nc.semaphore(n)) for n in sem_names}
        SM = sems
        blk = E(nc.Block())

        dma_ctr = {n: 0 for n in sem_names}

        def dma(eng, sem, out, in_, waits=()):
            for s, v in waits:
                if v > 0:
                    eng.wait_ge(SM[s], v)
            pre = dma_ctr[sem]
            d = eng.dma_start(out=out, in_=in_)
            if pre > 0:
                d._wait_ge(SM[sem], pre)
            d.then_inc(SM[sem], 16)
            dma_ctr[sem] = pre + 16
            return d

        def cW(c):
            return ("s_wl1" if c % 2 == 0 else "s_wl2", 16 * (c // 2 + 1))

        def c_avr(p):
            return 32 + (p + 1) + p // 4

        def bw(eng, ph):
            eng.wait_ge(SM["s_ph"], N_ENG * ph)

        def xld_sem(k):
            if k < 20:
                return "s_xld", 16 * (k + 1)
            return "s_xld2", 16 * (k - 19)

        def x_emit(eng, g):
            for kk in range(4):
                k = 4 * g + kk
                ws = []
                if k >= 4:
                    ws = [("s_lns", 2 * (k - 4) + 1), ("s_lnc", 7 * (k - 3))]
                dma(eng, xld_sem(k)[0], xt_v[k % 4],
                    x_in[128 * k:128 * (k + 1), :], waits=ws)

        def xnw_emit(sync, g):
            for dc in range(DC):
                ws = []
                if dc == 0:
                    ws.append(("s_xsp", 16 * DC * (g + 1)))
                    if g >= 3:
                        ws.append(("s_f1m", HC * (g - 2)))
                        ws.append(("s_bxm", DC * (g - 1)))
                dma(sync, "s_xnw", xnw_v[g % 3][dc],
                    xnt_d[dc, :, TT * g:TT * (g + 1)], waits=ws)

        def s1p_emit(sync, t, q):
            P = 16 * t + q
            ws = []
            if P == 0:
                ws.append(("s_ssp", 16 * 16))
            if P >= 2:
                ws.append(("s_f1m", HC * t + 2 * q - 2))
            dma(sync, "s_s1l", s1p_v[P % 2], s1_d[q, :, :], waits=ws)

        def s2p_emit(sync, t, q):
            P = 16 * t + q
            ws = []
            if P == 0:
                ws.append(("s_ssp", 16 * 32))
            if P >= 2:
                ws.append(("s_f2m", (P - 2) // 2 + 1))
            dma(sync, "s_s2l", s2p_v[P % 2], s2_d[q, :, :], waits=ws)

        def ln_ve_parts(vector, k):
            L = ln_sc.ap()[:, 8 * (k % 2):8 * (k % 2) + 8]
            base = 7 * k

            def mk_a(k=k, L=L, base=base):
                p = k % 2
                if k >= 1:
                    vector.wait_ge(SM["s_lnc"], base - 7)
                i = vector.tensor_reduce(out=L[:, 0:1], in_=xt_v[k % 4],
                                         axis=AX.X, op=ALU.add)
                xs, xv = xld_sem(k)
                i._wait_ge(SM[xs], xv)
                i.then_inc(SM["s_lnc"], 1)
                vector.wait_ge(SM["s_lnc"], base + 1)
                i = vector.tensor_scalar_mul(out=L[:, 1:2], in0=L[:, 0:1],
                                             scalar1=1.0 / D)
                i.then_inc(SM["s_lnc"], 1)
                vector.wait_ge(SM["s_lnc"], base + 2)
                i = vector.tensor_tensor(out=L[:, 2:3], in0=L[:, 1:2],
                                         in1=L[:, 1:2], op=ALU.mult)
                i.then_inc(SM["s_lnc"], 1)

            def mk_b1(k=k, L=L, base=base):
                vector.wait_ge(SM["s_lnc"], base + 3)
                i = vector.tensor_scalar(out=L[:, 3:4], in0=L[:, 4:5],
                                         scalar1=1.0 / D, scalar2=L[:, 2:3],
                                         op0=ALU.mult, op1=ALU.subtract)
                i._wait_ge(SM["s_lns"], 2 * k + 1)
                i.then_inc(SM["s_lnc"], 1)
                vector.wait_ge(SM["s_lnc"], base + 4)
                i = vector.tensor_scalar_add(out=L[:, 5:6], in0=L[:, 3:4],
                                             scalar1=1e-5)
                i.then_inc(SM["s_lnc"], 1)
                vector.wait_ge(SM["s_lnc"], base + 5)
                i = vector.reciprocal(out=L[:, 6:7], in_=L[:, 5:6])
                i.then_inc(SM["s_lnc"], 1)

            def mk_b2(k=k, L=L):
                p = k % 2
                if k >= 8:
                    vector.wait_ge(SM["s_tr"], k - 7)
                i = vector.tensor_scalar(out=xnm_v[k % 8], in0=xt_v[k % 4],
                                         scalar1=L[:, 1:2], scalar2=L[:, 7:8],
                                         op0=ALU.subtract, op1=ALU.mult)
                i._wait_ge(SM["s_lns"], 2 * (k + 1))
                i.then_inc(SM["s_lnc"], 1)

            return mk_a, mk_b1, mk_b2

        def ln_compute_ve(vector, g):
            out = []
            for kk in range(4):
                mk_a, mk_b1, mk_b2 = ln_ve_parts(vector, 4 * g + kk)
                out.append((152 + 80 * kk, mk_a))
                out.append((184 + 80 * kk, mk_b1))
                out.append((208 + 80 * kk, mk_b2))
            return out

        def ln_compute_se(scalar, g):
            out = []
            for kk in range(4):
                k = 4 * g + kk
                L = ln_sc.ap()[:, 8 * (k % 2):8 * (k % 2) + 8]

                def mk_sq(k=k, L=L):
                    p = k % 2
                    if k >= 1:
                        scalar.wait_ge(SM["s_lns"], 2 * (k - 1))
                    if k >= 2:
                        scalar.wait_ge(SM["s_lnc"], 7 * (k - 2) + 4)
                    if k >= 8:
                        scalar.wait_ge(SM["s_tr"], k - 7)
                    i = scalar.activation(out=xnm_v[k % 8], in_=xt_v[k % 4],
                                          func=AF.Square, accum_out=L[:, 4:5])
                    xs, xv = xld_sem(k)
                    i._wait_ge(SM[xs], xv)
                    i.then_inc(SM["s_lns"], 1)

                def mk_sqrt(k=k, L=L):
                    i = scalar.activation(out=L[:, 7:8], in_=L[:, 6:7],
                                          func=AF.Sqrt)
                    i._wait_ge(SM["s_lnc"], 7 * k + 6)
                    i.then_inc(SM["s_lns"], 1)

                out.append((176 + 80 * kk, mk_sq))
                out.append((196 + 80 * kk, mk_sqrt))
            return out

        def tr_emit(tensor, g, kk):
            k = 4 * g + kk
            tensor.wait_ge(SM["s_lne"], 8 * k)
            for dcb in range(DC):
                i = tensor.transpose(
                    out=pb_tr.ap()[:, 128 * dcb:128 * (dcb + 1)],
                    in_=xnm_v[k % 8][:, 128 * dcb:128 * (dcb + 1)],
                    identity=ident_b.ap()[:, :])
                if dcb == 0:
                    i._wait_ge(SM["s_lnc"], 7 * k + 7)
                if dcb == DC - 1:
                    i.then_inc(SM["s_tr"], 1)

        def ln_evac_tile(vector, k):
            kk = k % 4
            g = k // 4
            fns = []
            for dcb in range(DC):
                def mk(k=k, kk=kk, dcb=dcb, g=g):
                    if dcb == 0 and kk == 0 and g > 0:
                        vector.wait_ge(SM["s_xsp"], 16 * DC * g)
                    i = vector.tensor_copy(
                        out=xnst_v[:, TT * dcb + 128 * kk:
                                   TT * dcb + 128 * (kk + 1)],
                        in_=pb_tr.ap()[:, 128 * dcb:128 * (dcb + 1)])
                    if dcb == 0:
                        i._wait_ge(SM["s_tr"], k + 1)
                    i.then_inc(SM["s_lne"], 1)
                fns.append(mk)
            return fns

        def ln_evac(vector, g):
            out = []
            for kk in range(4):
                for dcb, mk in enumerate(ln_evac_tile(vector, 4 * g + kk)):
                    out.append((12 + 4 * (8 * kk + dcb), mk))
            return out

        def xsp_emit(gpsimd, g):
            for dc in range(DC):
                ws = [("s_lne", 8 * (4 * g + 4))] if dc == 0 else []
                dma(gpsimd, "s_xsp", xnt_d[dc, :, TT * g:TT * (g + 1)],
                    xnst_v[:, TT * dc:TT * (dc + 1)], waits=ws)

        def bx_mm(tensor, w):
            for ec in range(DC):
                g = DC * w + ec
                if g >= 2:
                    tensor.wait_ge(SM["s_bxe"], g - 1)
                for dc in range(DC):
                    i = tensor.matmul(
                        out=pb[ec % 2].ap()[:, :],
                        lhsT=wbt_sb.ap()[:, D * dc + 128 * ec:
                                         D * dc + 128 * (ec + 1)],
                        rhs=xnw_v[w % 3][dc],
                        start=(dc == 0), stop=(dc == DC - 1))
                    if dc == 0:
                        i._wait_ge(SM["s_xnw"], 16 * DC * (w + 1))
                    if dc == DC - 1:
                        i.then_inc(SM["s_bxm"], 1)

        def f1_mm(tensor, t):
            if t == 0:
                tensor.wait_ge(SM["s_sgn"], 16)
            for q in range(16):
                tensor.wait_ge(SM["s_s1l"], 16 * (16 * t + q + 1))
                for r in range(2):
                    hc = 2 * q + r
                    g = HC * t + hc
                    if g >= 2:
                        tensor.wait_ge(SM["s_f1e"], g - 1)
                    for dc in range(DC):
                        i = tensor.matmul(
                            out=pb[2 + hc % 2].ap()[:, :],
                            lhsT=s1p_v[(16 * t + q) % 2][:, r * 1024 + dc * 128:
                                                         r * 1024 + dc * 128 + 128],
                            rhs=xnw_v[t % 3][dc],
                            start=(dc == 0), stop=(dc == DC - 1))
                        if dc == 0:
                            i._wait_ge(SM["s_xnw"], 16 * DC * (t + 1))
                        if dc == DC - 1:
                            i.then_inc(SM["s_f1m"], 1)

        def f2_mm(tensor, t, douts):
            if t == 0 and douts[0] == 0:
                tensor.wait_ge(SM["s_sgn"], 32)
            for dout in douts:
                g = DC * t + dout
                if g >= 2:
                    tensor.wait_ge(SM["s_f2e"], g - 1)
                for hc in range(HC):
                    q2 = 2 * dout + hc // 16
                    if hc % 16 == 0:
                        tensor.wait_ge(SM["s_s2l"], 16 * (16 * t + q2 + 1))
                    i = tensor.matmul(
                        out=pb[4 + dout % 2].ap()[:, :],
                        lhsT=s2p_v[(16 * t + q2) % 2][:, (hc % 16) * 128:
                                                      (hc % 16) * 128 + 128],
                        rhs=hst_v[:, TT * hc:TT * (hc + 1)],
                        start=(hc == 0), stop=(hc == HC - 1))
                    i._wait_ge(SM["s_f1e"], HC * t + hc + 1)
                    if hc == HC - 1:
                        i.then_inc(SM["s_f2m"], 1)

        @blk.sync
        def _(sync):
            for srcp, dst in ((bB_d, bB_sb), (b1_d, b1_sb), (b2_d, b2_sb),
                              (bC_d, bC_sb), (a_d, a_sb)):
                dma(sync, "s_wld", dst.ap()[:, :], srcp[:, :])
            for c in range(HC):
                ws = []
                if c >= 4:
                    ws = [("s_alv", c - 3), ("s_sgT", c - 3)]
                dma(sync, cW(c)[0], wsc[c % 4],
                    w1n[128 * c:128 * (c + 1), :], waits=ws)
            for p in range(32):
                c = 32 + p
                ws = [("s_alv", 29 + p if p < 4 else c_avr(p - 4)),
                      ("s_sgT", c - 3)]
                dc, jp = p // 4, p % 4
                dma(sync, cW(c)[0], wsc[c % 4],
                    w2n[128 * dc:128 * (dc + 1), 1024 * jp:1024 * (jp + 1)],
                    waits=ws)
            sync.wait_ge(SM["s_wld"], 16 * 5)
            sync.wait_ge(SM["s_wl1"], 16 * 32)
            sync.wait_ge(SM["s_wl2"], 16 * 32)
            sync.nop().then_inc(SM["s_ph"], 1)
            bw(sync, 1)
            for g in range(min(5, NG)):
                x_emit(sync, g)
            xnw_emit(sync, 0)
            if NT > 1:
                xnw_emit(sync, 1)
            for w in range(NT):
                if w + 2 < NT:
                    xnw_emit(sync, w + 2)
                if w >= 1:
                    for q in range(16):
                        s2p_emit(sync, w - 1, q)
                for q in range(16):
                    s1p_emit(sync, w, q)
            for q in range(16):
                s2p_emit(sync, NT - 1, q)
            sync.wait_ge(SM["s_xld"], 16 * min(NL, 20))
            sync.wait_ge(SM["s_xnw"], 16 * DC * NT)
            sync.wait_ge(SM["s_s1l"], 16 * 16 * NT)
            sync.wait_ge(SM["s_s2l"], 16 * 16 * NT)
            sync.nop().then_inc(SM["s_ph"], 1)
            bw(sync, 2)
            for t in range(NT):
                dma(sync, "s_hsl", hsb_v[t % 2], hs_d[t, :, :],
                    waits=[("s_wcm", DC * (t - 1) if t >= 2 else 0)])
                for s4 in range(4):
                    dma(sync, "s_xr", xr_v[s4],
                        x_in[TT * t + 128 * s4:TT * t + 128 * (s4 + 1), :],
                        waits=[("s_cmb", 8 * t if (t >= 1 and s4 == 0) else 0)])
            sync.wait_ge(SM["s_yst"], 16 * 4 * NT)

        @blk.gpsimd
        def _(gpsimd):
            gpsimd.iota(io_r.ap()[:, :], pattern=[[1, 128]], base=0,
                        channel_multiplier=0).then_inc(SM["s_gld"], 1)
            gpsimd.iota(io_c.ap()[:, :], pattern=[[0, 128]], base=0,
                        channel_multiplier=1).then_inc(SM["s_gld"], 1)
            for i in range(DC):
                dma(gpsimd, "s_gw", wbt_sb.ap()[:, i * D:(i + 1) * D],
                    wbt[128 * i:128 * (i + 1), :])
            for q in range(16):
                dma(gpsimd, "s_ssp", s1_d[q, :, :], pstage[q % 2],
                    waits=[("s_sgn", 4 * q + 4)])
            for q2 in range(16):
                dma(gpsimd, "s_ssp", s2_d[q2, :, :], pstage[q2 % 2],
                    waits=[("s_sgn", 64 + 4 * q2 + 4)])
            gpsimd.wait_ge(SM["s_gld"], 2)
            gpsimd.wait_ge(SM["s_gw"], 16 * DC)
            gpsimd.wait_ge(SM["s_ssp"], 16 * 32)
            gpsimd.engine_nop().then_inc(SM["s_ph"], 1)
            bw(gpsimd, 1)
            for g in range(min(3, NG)):
                xsp_emit(gpsimd, g)
            for w in range(NT):
                if w + 3 < NG:
                    xsp_emit(gpsimd, w + 3)
                if w >= 1:
                    for dout in range(DC):
                        dma(gpsimd, "s_gsp",
                            g_d[w - 1, :, TT * dout:TT * (dout + 1)],
                            gst_v[dout % 2],
                            waits=[("s_f2e", DC * (w - 1) + dout + 1)])
                    dma(gpsimd, "s_hss", hs_d[w - 1, :, :], hsst_v,
                        waits=[("s_hc", DC * w)])
                if w + 5 < NG:
                    x_emit(gpsimd, w + 5)
            dma(gpsimd, "s_hss", hs_d[NT - 1, :, :], hsst_v,
                waits=[("s_hc", DC * NT)])
            for dout in range(DC):
                dma(gpsimd, "s_gsp", g_d[NT - 1, :, TT * dout:TT * (dout + 1)],
                    gst_v[dout % 2],
                    waits=[("s_f2e", DC * (NT - 1) + dout + 1)])
            gpsimd.wait_ge(SM["s_hss"], 16 * NT)
            gpsimd.wait_ge(SM["s_gsp"], 16 * DC * NT)
            gpsimd.wait_ge(SM["s_xsp"], 16 * DC * NG)
            if NL > 20:
                gpsimd.wait_ge(SM["s_xld2"], 16 * (NL - 20))
            gpsimd.engine_nop().then_inc(SM["s_ph"], 1)
            bw(gpsimd, 2)
            for i in range(DC):
                dma(gpsimd, "s_gw", wct_v[:, i * D:(i + 1) * D],
                    wct[128 * i:128 * (i + 1), :])
            for t in range(NT):
                dma(gpsimd, "s_gl", gb_v[t % 2], g_d[t, :, :],
                    waits=[("s_sg", DC * (t - 1) if t >= 2 else 0)])
                for s4 in range(4):
                    u = 4 * t + s4
                    dma(gpsimd, "s_yst",
                        y_out[TT * t + 128 * s4:TT * t + 128 * (s4 + 1), :],
                        ost_v[s4], waits=[("s_cmb", 2 * (u + 1))])

        @blk.vector
        def _(vector):
            i = vector.tensor_tensor(out=ident_b.ap()[:, :], in0=io_r.ap()[:, :],
                                     in1=io_c.ap()[:, :], op=ALU.is_equal)
            i._wait_ge(SM["s_gld"], 2)
            i.then_inc(SM["s_prep"], 1)
            i = vector.tensor_copy(out=ident_f.ap()[:, :], in_=ident_b.ap()[:, :])
            i._wait_ge(SM["s_prep"], 1)
            i.then_inc(SM["s_prep"], 1)
            vector.memset(zs.ap()[:, :], 0.0).then_inc(SM["s_prep"], 1)
            for hc in range(HC):
                i = vector.tensor_reduce(out=al1_sb.ap()[:, hc:hc + 1],
                                         in_=wsc[hc % 4], axis=AX.X,
                                         op=ALU.add, apply_absolute_value=True)
                i._wait_ge(SM[cW(hc)[0]], cW(hc)[1])
                i.then_inc(SM["s_alv"], 1)
            for p in range(32):
                if p % 4 == 0 and p >= 4:
                    vector.wait_ge(SM["s_alv"], c_avr(p - 1) + 1)
                i = vector.tensor_reduce(out=al2p_sb.ap()[:, p % 4:p % 4 + 1],
                                         in_=wsc[(32 + p) % 4], axis=AX.X,
                                         op=ALU.add, apply_absolute_value=True)
                i._wait_ge(SM[cW(32 + p)[0]], cW(32 + p)[1])
                i.then_inc(SM["s_alv"], 1)
                if p % 4 == 3:
                    i = vector.tensor_reduce(out=al2_sb.ap()[:, p // 4:p // 4 + 1],
                                             in_=al2p_sb.ap()[:, :], axis=AX.X,
                                             op=ALU.add)
                    i._wait_ge(SM["s_alv"], c_avr(p))
                    i.then_inc(SM["s_alv"], 1)
            vector.wait_ge(SM["s_alv"], 72)
            vector.tensor_scalar_mul(out=al1s_sb.ap()[:, :], in0=al1_sb.ap()[:, :],
                                     scalar1=1.0 / D).then_inc(SM["s_prep"], 1)
            vector.tensor_scalar_mul(out=al2s_sb.ap()[:, :], in0=al2_sb.ap()[:, :],
                                     scalar1=1.0 / H).then_inc(SM["s_prep"], 1)
            vector.wait_ge(SM["s_prep"], 5)
            vector.engine_nop().then_inc(SM["s_ph"], 1)
            bw(vector, 1)
            pre_tiles = 4 * min(4, NG)
            pre_ev = 4 * min(3, NG)
            for k in range(pre_tiles):
                mk_a, mk_b1, mk_b2 = ln_ve_parts(vector, k)
                mk_a()
                mk_b1()
                mk_b2()
                if k >= 1 and (k - 1) < pre_ev:
                    for fn in ln_evac_tile(vector, k - 1):
                        fn()
            if pre_ev >= pre_tiles:
                for fn in ln_evac_tile(vector, pre_tiles - 1):
                    fn()
            hswr = hswin_v.rearrange("p (s c) -> p c s", c=8)
            for w in range(NT):
                ins = {}

                def put(j, fn):
                    ins.setdefault(j, []).append(fn)
                if w >= 1:
                    for dc in range(DC):
                        def mkc(dc=dc, w=w):
                            pw = (w - 1) % 2
                            if dc == 0:
                                vector.wait_ge(SM["s_hss"], 16 * (w - 1))
                            i = vector.tensor_copy(
                                out=hsst_v[:, TT * dc:TT * (dc + 1)],
                                in_=hswr[:, dc, pw * TT:pw * TT + TT])
                            i._wait_ge(SM["ss"], w * TT)
                            i.then_inc(SM["s_hc"], 1)
                        put(64 + 56 * dc, mkc)
                if w + 3 < NG:
                    for j, fn in ln_evac(vector, w + 3):
                        put(j, fn)
                if w + 4 < NG:
                    for j, fn in ln_compute_ve(vector, w + 4):
                        put(j, fn)
                vector.wait_ge(SM["s_bxe"], DC * (w + 1))
                for j in range(TT):
                    t = w * TT + j
                    prev = (t - 1) % (2 * TT)
                    hp = zs.ap()[:, 0:8] if t == 0 else \
                        hswin_v[:, 8 * prev:8 * prev + 8]
                    i1 = vector.tensor_tensor(
                        out=m_sc.ap()[:, 8 * (t % 2):8 * (t % 2) + 8],
                        in0=hp, in1=a_sb.ap()[:, 0:8], op=ALU.mult)
                    if t > 0:
                        i1._wait_ge(SM["ss"], t)
                    i1.then_inc(SM["vm"], 1)
                    usl = t % 64
                    i2 = vector.tensor_tensor(
                        out=pb_u.ap()[:, 8 * usl:8 * usl + 8],
                        in0=m_sc.ap()[:, 8 * (t % 2):8 * (t % 2) + 8],
                        in1=bxw_v[w % 2][:, 8 * j:8 * j + 8], op=ALU.add)
                    i2._wait_ge(SM["vm"], t + 1)
                    i2.then_inc(SM["sv"], 1)
                    for fn in ins.get(j, ()):
                        fn()
            vector.wait_ge(SM["ss"], NT * TT)
            vector.wait_ge(SM["s_hss"], 16 * (NT - 1))
            pw = (NT - 1) % 2
            for dc in range(DC):
                i = vector.tensor_copy(
                    out=hsst_v[:, TT * dc:TT * (dc + 1)],
                    in_=hswr[:, dc, pw * TT:pw * TT + TT])
                i.then_inc(SM["s_hc"], 1)
            vector.wait_ge(SM["s_hc"], DC * NT)
            vector.wait_ge(SM["sv"], S)
            vector.wait_ge(SM["vm"], S)
            vector.wait_ge(SM["s_lne"], 8 * NL)
            vector.wait_ge(SM["s_lnc"], 7 * NL)
            vector.engine_nop().then_inc(SM["s_ph"], 1)
            bw(vector, 2)
            for t in range(NT):
                p = t % 2
                if t >= 1:
                    vector.wait_ge(SM["s_wce"], DC * (t - 1))
                    vector.wait_ge(SM["s_yst"], 16 * 4 * t)
                for ec in range(DC):
                    i = vector.tensor_scalar_add(
                        out=sst_v[p][:, TT * ec:TT * (ec + 1)],
                        in0=pb[ec % 2].ap()[:, :],
                        scalar1=bC_sb.ap()[:, ec:ec + 1])
                    i._wait_ge(SM["s_wcm"], DC * t + ec + 1)
                    i.then_inc(SM["s_wce"], 1)
                vector.wait_ge(SM["s_wce"], DC * (t + 1))
                for ec in range(DC):
                    i = vector.tensor_tensor(
                        out=sgst_v[p][:, TT * ec:TT * (ec + 1)],
                        in0=sst_v[p][:, TT * ec:TT * (ec + 1)],
                        in1=gb_v[p][:, TT * ec:TT * (ec + 1)], op=ALU.mult)
                    if ec == 0:
                        i._wait_ge(SM["s_gl"], 16 * (t + 1))
                    i.then_inc(SM["s_sg"], 1)
                for s4 in range(4):
                    u = 4 * t + s4
                    vector.wait_ge(SM["s_xr"], 16 * (u + 1))
                    i = vector.tensor_tensor(
                        out=ost_v[s4][:, 0:512], in0=pb[2].ap()[:, :],
                        in1=xr_v[s4][:, 0:512], op=ALU.add)
                    i._wait_ge(SM["s_tr2"], u + 1)
                    i.then_inc(SM["s_cmb"], 1)
                    i = vector.tensor_tensor(
                        out=ost_v[s4][:, 512:1024], in0=pb[3].ap()[:, :],
                        in1=xr_v[s4][:, 512:1024], op=ALU.add)
                    i._wait_ge(SM["s_xr"], 16 * (u + 1))
                    i.then_inc(SM["s_cmb"], 1)

        @blk.scalar
        def _(scalar):
            for c in range(64):
                q = c // 2
                for half in range(2):
                    if half == 0 and c >= 4:
                        scalar.wait_ge(SM["s_ssp"], 16 * (q - 1))
                    i = scalar.activation(
                        out=pstage[q % 2][:, (c % 2) * 1024 + 512 * half:
                                          (c % 2) * 1024 + 512 * (half + 1)],
                        in_=pb[2 + 2 * (c % 2) + half].ap()[:, :], func=AF.Sign)
                    i._wait_ge(SM["s_sgT"], c + 1)
                    i.then_inc(SM["s_sgn"], 1)
            scalar.wait_ge(SM["s_sgn"], 128)
            scalar.nop().then_inc(SM["s_ph"], 1)
            bw(scalar, 1)
            for g in range(min(4, NG)):
                for _, fn in sorted(ln_compute_se(scalar, g), key=lambda x: x[0]):
                    fn()
            bxwr = [bxw_v[p].rearrange("p (t c) -> p c t", c=8) for p in range(2)]
            for ec in range(DC):
                i = scalar.add(out=bxwr[0][:, ec, :], in_=pb[ec % 2].ap()[:, :],
                               add=bB_sb.ap()[:, ec:ec + 1])
                i._wait_ge(SM["s_bxm"], ec + 1)
                i.then_inc(SM["s_bxe"], 1)
            for w in range(NT):
                ins = {}

                def put(j, fn):
                    ins.setdefault(j, []).append(fn)
                if w + 1 < NT:
                    for ec in range(DC):
                        def mke(ec=ec, w=w):
                            if ec == 0:
                                scalar.wait_ge(SM["sv"], w * TT)
                            i = scalar.add(out=bxwr[(w + 1) % 2][:, ec, :],
                                           in_=pb[ec % 2].ap()[:, :],
                                           add=bB_sb.ap()[:, ec:ec + 1])
                            i._wait_ge(SM["s_bxm"], DC * (w + 1) + ec + 1)
                            i.then_inc(SM["s_bxe"], 1)
                        put(150 + 8 * ec, mke)
                if w >= 1:
                    for dout in range(DC):
                        def mks(dout=dout, w=w):
                            t = w - 1
                            if DC * t + dout >= 2:
                                scalar.wait_ge(SM["s_gsp"],
                                               16 * (DC * t + dout - 1))
                            i = scalar.activation(
                                out=gst_v[dout % 2],
                                in_=pb[4 + dout % 2].ap()[:, :], func=AF.Silu,
                                bias=b2_sb.ap()[:, dout:dout + 1],
                                scale=al2s_sb.ap()[:, dout:dout + 1])
                            i._wait_ge(SM["s_f2m"], DC * t + dout + 1)
                            i.then_inc(SM["s_f2e"], 1)
                        put(230 + 16 * dout, mks)
                for hc in range(HC):
                    def mkg(hc=hc, w=w):
                        if hc == 0:
                            scalar.wait_ge(SM["s_f2m"], DC * w)
                            if w == 0:
                                scalar.wait_ge(SM["s_prep"], 5)
                        i = scalar.activation(
                            out=hst_v[:, TT * hc:TT * (hc + 1)],
                            in_=pb[2 + hc % 2].ap()[:, :], func=AF.Gelu,
                            bias=b1_sb.ap()[:, hc:hc + 1],
                            scale=al1s_sb.ap()[:, hc:hc + 1])
                        i._wait_ge(SM["s_f1m"], HC * w + hc + 1)
                        i.then_inc(SM["s_f1e"], 1)
                    put(384 + 4 * hc, mkg)
                if w + 4 < NG:
                    for j, fn in ln_compute_se(scalar, w + 4):
                        put(j, fn)
                if w >= 2:
                    scalar.wait_ge(SM["s_hc"], DC * (w - 1))
                for j in range(TT):
                    t = w * TT + j
                    sl = t % (2 * TT)
                    usl = t % 64
                    i3 = scalar.activation(
                        out=hswin_v[:, 8 * sl:8 * sl + 8],
                        in_=pb_u.ap()[:, 8 * usl:8 * usl + 8], func=AF.Tanh)
                    i3._wait_ge(SM["sv"], t + 1)
                    i3.then_inc(SM["ss"], 1)
                    for fn in ins.get(j, ()):
                        fn()
            for dout in range(DC):
                t = NT - 1
                if DC * t + dout >= 2:
                    scalar.wait_ge(SM["s_gsp"], 16 * (DC * t + dout - 1))
                i = scalar.activation(
                    out=gst_v[dout % 2], in_=pb[4 + dout % 2].ap()[:, :],
                    func=AF.Silu, bias=b2_sb.ap()[:, dout:dout + 1],
                    scale=al2s_sb.ap()[:, dout:dout + 1])
                i._wait_ge(SM["s_f2m"], DC * t + dout + 1)
                i.then_inc(SM["s_f2e"], 1)
            scalar.wait_ge(SM["ss"], S)
            scalar.wait_ge(SM["s_bxe"], DC * NT)
            scalar.wait_ge(SM["s_f1e"], HC * NT)
            scalar.wait_ge(SM["s_f2e"], DC * NT)
            scalar.wait_ge(SM["s_lns"], 2 * NL)
            scalar.nop().then_inc(SM["s_ph"], 1)
            bw(scalar, 2)

        @blk.tensor
        def _(tensor):
            tensor.wait_ge(SM["s_prep"], 2)
            for c in range(64):
                if c >= 2:
                    tensor.wait_ge(SM["s_sgn"], 2 * c - 2)
                for b in range(8):
                    bank = pb[2 + 2 * (c % 2) + b // 4]
                    i = tensor.transpose(
                        out=bank.ap()[:, 128 * (b % 4):128 * (b % 4 + 1)],
                        in_=wsc[c % 4][:, 128 * b:128 * (b + 1)],
                        identity=ident_f.ap()[:, :])
                    if b == 0:
                        i._wait_ge(SM[cW(c)[0]], cW(c)[1])
                    if b == 7:
                        i.then_inc(SM["s_sgT"], 1)
            tensor.wait_ge(SM["s_sgT"], 64)
            tensor.nop().then_inc(SM["s_ph"], 1)
            bw(tensor, 1)
            for g in range(min(3, NG)):
                for kk in range(4):
                    tr_emit(tensor, g, kk)
            bx_mm(tensor, 0)
            for w in range(NT):
                if w + 3 < NG:
                    for kk in range(4):
                        tr_emit(tensor, w + 3, kk)
                if w + 1 < NT:
                    bx_mm(tensor, w + 1)
                if w >= 1:
                    f2_mm(tensor, w - 1, [0, 1, 2, 3])
                    f2_mm(tensor, w - 1, [4, 5, 6, 7])
                f1_mm(tensor, w)
            f2_mm(tensor, NT - 1, [0, 1, 2, 3])
            f2_mm(tensor, NT - 1, [4, 5, 6, 7])
            tensor.wait_ge(SM["s_bxm"], DC * NT)
            tensor.wait_ge(SM["s_f1m"], HC * NT)
            tensor.wait_ge(SM["s_f2m"], DC * NT)
            tensor.wait_ge(SM["s_tr"], NL)
            tensor.nop().then_inc(SM["s_ph"], 1)
            bw(tensor, 2)
            tensor.wait_ge(SM["s_gw"], 16 * 2 * DC)
            for t in range(NT):
                p = t % 2
                for ec in range(DC):
                    g = DC * t + ec
                    if g >= 2:
                        tensor.wait_ge(SM["s_wce"], g - 1)
                    for dc in range(DC):
                        i = tensor.matmul(
                            out=pb[ec % 2].ap()[:, :],
                            lhsT=wct_v[:, D * dc + 128 * ec:
                                       D * dc + 128 * (ec + 1)],
                            rhs=hsb_v[p][:, TT * dc:TT * (dc + 1)],
                            start=(dc == 0), stop=(dc == DC - 1))
                        if dc == 0:
                            i._wait_ge(SM["s_hsl"], 16 * (t + 1))
                        if dc == DC - 1:
                            i.then_inc(SM["s_wcm"], 1)
                for s4 in range(4):
                    u = 4 * t + s4
                    tensor.wait_ge(SM["s_cmb"], 2 * u)
                    for dc in range(DC):
                        i = tensor.transpose(
                            out=pb[2 + dc // 4].ap()[:, 128 * (dc % 4):
                                                     128 * ((dc % 4) + 1)],
                            in_=sgst_v[p][:, TT * dc + 128 * s4:
                                          TT * dc + 128 * (s4 + 1)],
                            identity=ident_f.ap()[:, :])
                        if dc == 0:
                            i._wait_ge(SM["s_sg"], DC * (t + 1))
                        if dc == DC - 1:
                            i.then_inc(SM["s_tr2"], 1)

    return nc


def kernel(**inputs):
    S = 8192
    x = np.asarray(inputs["L0_attn_x"], np.float32)
    A = np.asarray(inputs["L0_attn_A"], np.float32)
    W_B = np.asarray(inputs["L0_attn_W_B"], np.float32)
    b_B = np.asarray(inputs["L0_attn_b_B"], np.float32)
    W_C = np.asarray(inputs["L0_attn_W_C"], np.float32)
    b_C = np.asarray(inputs["L0_attn_b_C"], np.float32)
    w1 = np.asarray(inputs["L0_ffn_w1"], np.float32)
    b1 = np.asarray(inputs["L0_ffn_b1"], np.float32)
    w2 = np.asarray(inputs["L0_ffn_w2"], np.float32)
    b2 = np.asarray(inputs["L0_ffn_b2"], np.float32)

    cm = np.ascontiguousarray
    shared = {
        "wbt": cm(W_B.T), "wct": cm(W_C.T),
        "w1n": cm(w1), "w2n": cm(w2),
        "bB": cm(b_B.reshape(DC, 128).T), "b1": cm(b1.reshape(HC, 128).T),
        "b2": cm(b2.reshape(DC, 128).T), "bC": cm(b_C.reshape(DC, 128).T),
        "afull": cm(A.reshape(DC, 128).T),
    }
    in_maps = []
    for c in range(8):
        m = dict(shared)
        m["x"] = cm(x[c // 2])
        in_maps.append(m)

    nc = build_nc(S)
    res = run_bass_kernel_spmd(nc, in_maps, core_ids=list(range(8)))
    out = np.stack([np.asarray(res.results[2 * b]["y"], np.float32)
                    for b in range(4)], axis=0)
    return out


# version 07
# speedup vs baseline: 1.0634x; 1.0045x over previous
import numpy as np
from contextlib import ExitStack

import concourse.bass as bass
import concourse.mybir as mybir
from concourse.bass_utils import run_bass_kernel_spmd

F32 = mybir.dt.float32
BF16 = mybir.dt.bfloat16
I32 = mybir.dt.int32
AF = mybir.ActivationFunctionType
ALU = mybir.AluOpType
AX = mybir.AxisListType

D = 1024
H = 4096
DC = D // 128
HC = H // 128
TT = 512

N_ENG = 5


def build_nc(S):
    NT = S // TT
    NL = S // 128
    NG = NT
    nc = bass.Bass("TRN2", target_bir_lowering=False, debug=False)

    x_in = nc.dram_tensor("x", [S, D], F32, kind="ExternalInput").ap()
    wbt = nc.dram_tensor("wbt", [D, D], F32, kind="ExternalInput").ap()
    wct = nc.dram_tensor("wct", [D, D], F32, kind="ExternalInput").ap()
    w1n = nc.dram_tensor("w1n", [H, D], F32, kind="ExternalInput").ap()
    w2n = nc.dram_tensor("w2n", [D, H], F32, kind="ExternalInput").ap()
    bB_d = nc.dram_tensor("bB", [128, DC], F32, kind="ExternalInput").ap()
    b1_d = nc.dram_tensor("b1", [128, HC], F32, kind="ExternalInput").ap()
    b2_d = nc.dram_tensor("b2", [128, DC], F32, kind="ExternalInput").ap()
    bC_d = nc.dram_tensor("bC", [128, DC], F32, kind="ExternalInput").ap()
    a_d = nc.dram_tensor("afull", [128, DC], F32, kind="ExternalInput").ap()
    y_out = nc.dram_tensor("y", [S, D], F32, kind="ExternalOutput").ap()

    xnt_d = nc.dram_tensor("xnt_d", [DC, 128, S], BF16, kind="Internal").ap()
    hs_d = nc.dram_tensor("hs_d", [NT, 128, DC * TT], BF16, kind="Internal").ap()
    g_d = nc.dram_tensor("g_d", [NT, 128, DC * TT], F32, kind="Internal").ap()
    s1_d = nc.dram_tensor("s1_d", [16, 128, 2048], BF16, kind="Internal").ap()
    s2_d = nc.dram_tensor("s2_d", [16, 128, 2048], BF16, kind="Internal").ap()

    with ExitStack() as ctx:
        E = ctx.enter_context
        ident_b = E(nc.sbuf_tensor("ident_b", [128, 128], BF16))
        ident_f = E(nc.sbuf_tensor("ident_f", [128, 128], F32))
        io_r = E(nc.sbuf_tensor("io_r", [128, 128], I32))
        io_c = E(nc.sbuf_tensor("io_c", [128, 128], I32))
        wbt_sb = E(nc.sbuf_tensor("wbt_sb", [128, DC * D], BF16))
        bB_sb = E(nc.sbuf_tensor("bB_sb", [128, DC], F32))
        b1_sb = E(nc.sbuf_tensor("b1_sb", [128, HC], F32))
        b2_sb = E(nc.sbuf_tensor("b2_sb", [128, DC], F32))
        bC_sb = E(nc.sbuf_tensor("bC_sb", [128, DC], F32))
        a_sb = E(nc.sbuf_tensor("a_sb", [128, DC], F32))
        al1_sb = E(nc.sbuf_tensor("al1_sb", [128, HC], F32))
        al1s_sb = E(nc.sbuf_tensor("al1s_sb", [128, HC], F32))
        al2_sb = E(nc.sbuf_tensor("al2_sb", [128, DC], F32))
        al2s_sb = E(nc.sbuf_tensor("al2s_sb", [128, DC], F32))
        al2p_sb = E(nc.sbuf_tensor("al2p_sb", [128, 4], F32))
        zs = E(nc.sbuf_tensor("zs", [128, 8], F32))
        m_sc = E(nc.sbuf_tensor("m_sc", [128, 16], F32))
        ln_sc = E(nc.sbuf_tensor("ln_sc", [128, 16], F32))

        AR_W = 188 * 1024
        arena = E(nc.sbuf_tensor("arena", [128, AR_W // 2], BF16))
        ar_f = arena.bitcast(F32)

        def fview(byte_off, n_el):
            assert byte_off % 4 == 0 and byte_off + 4 * n_el <= AR_W, byte_off
            return ar_f.ap()[:, byte_off // 4: byte_off // 4 + n_el]

        def bview(byte_off, n_el):
            assert byte_off % 2 == 0 and byte_off + 2 * n_el <= AR_W, byte_off
            return arena.ap()[:, byte_off // 2: byte_off // 2 + n_el]

        K = 1024
        bxw_v = [fview(0 + 16 * K * p, 8 * TT) for p in range(2)]
        xnw_v = [[bview(32 * K + 8 * K * r + 1024 * dc, TT)
                  for dc in range(DC)] for r in range(3)]
        hswin_v = fview(56 * K, 2 * 8 * TT)
        hsst_v = bview(88 * K, DC * TT)
        xt_v = [fview(96 * K + 4 * K * p, 1024) for p in range(4)]
        xnm_v = [bview(112 * K + 2 * K * p, 1024) for p in range(8)]
        xnst_v = bview(128 * K, DC * TT)
        s1p_v = [bview(136 * K + 4 * K * p, 2048) for p in range(2)]
        s2p_v = [bview(144 * K + 4 * K * p, 2048) for p in range(2)]
        gst_v = [fview(152 * K + 2 * K * p, TT) for p in range(2)]
        hst_v = bview(156 * K, HC * TT)
        wsc = [fview(156 * K + 4 * K * p, 1024) for p in range(4)]
        pstage = [bview(72 * K + 4 * K * p, 2048) for p in range(2)]
        hsb_v = [bview(0 + 8 * K * p, DC * TT) for p in range(2)]
        gb_v = [fview(16 * K + 16 * K * p, DC * TT) for p in range(2)]
        sst_v = [fview(48 * K + 16 * K * p, DC * TT) for p in range(2)]
        sgst_v = [fview(80 * K + 16 * K * p, DC * TT) for p in range(2)]
        xr_v = [fview(112 * K + 4 * K * s, 1024) for s in range(4)]
        ost_v = [fview(128 * K + 4 * K * s, 1024) for s in range(4)]
        wct_v = bview(144 * K, DC * D)

        pb = [E(nc.psum_tensor(f"pb{i}", [128, 512], F32)) for i in range(6)]
        pb_tr = E(nc.psum_tensor("pb_tr", [128, 1024], BF16))
        pb_u = E(nc.psum_tensor("pb_u", [128, 512], F32))

        sem_names = [
            "s_wld", "s_xld", "s_xld2", "s_xnw", "s_gld", "s_gw",
            "s_lnc", "s_lne", "s_lns", "s_tr", "s_xsp",
            "s_prep", "s_sgn", "s_ssp", "s_wl1", "s_wl2", "s_alv",
            "s_sgT", "s_s1l", "s_s2l",
            "s_f1m", "s_f1e", "s_f2m", "s_f2e", "s_gsp",
            "s_bxm", "s_bxe",
            "ss", "sv", "vm", "s_hc", "s_hss",
            "s_hsl", "s_gl", "s_wcm", "s_wce", "s_sg", "s_tr2", "s_cmb",
            "s_xr", "s_yst", "s_ph",
        ]
        sems = {n: E(nc.semaphore(n)) for n in sem_names}
        SM = sems
        blk = E(nc.Block())

        dma_ctr = {n: 0 for n in sem_names}

        def dma(eng, sem, out, in_, waits=()):
            for s, v in waits:
                if v > 0:
                    eng.wait_ge(SM[s], v)
            pre = dma_ctr[sem]
            d = eng.dma_start(out=out, in_=in_)
            if pre > 0:
                d._wait_ge(SM[sem], pre)
            d.then_inc(SM[sem], 16)
            dma_ctr[sem] = pre + 16
            return d

        def cW(c):
            return ("s_wl1" if c % 2 == 0 else "s_wl2", 16 * (c // 2 + 1))

        def c_avr(p):
            return 32 + (p + 1) + p // 4

        def bw(eng, ph):
            eng.wait_ge(SM["s_ph"], N_ENG * ph)

        def xld_sem(k):
            if k < 20:
                return "s_xld", 16 * (k + 1)
            return "s_xld2", 16 * (k - 19)

        def x_emit(eng, g):
            for kk in range(4):
                k = 4 * g + kk
                ws = []
                if k >= 4:
                    ws = [("s_lns", 2 * (k - 4) + 1), ("s_lnc", 7 * (k - 3))]
                dma(eng, xld_sem(k)[0], xt_v[k % 4],
                    x_in[128 * k:128 * (k + 1), :], waits=ws)

        def xnw_emit(sync, g):
            for dc in range(DC):
                ws = []
                if dc == 0:
                    ws.append(("s_xsp", 16 * DC * (g + 1)))
                    if g >= 3:
                        ws.append(("s_f1m", HC * (g - 2)))
                        ws.append(("s_bxm", DC * (g - 1)))
                dma(sync, "s_xnw", xnw_v[g % 3][dc],
                    xnt_d[dc, :, TT * g:TT * (g + 1)], waits=ws)

        def s1p_emit(sync, t, q):
            P = 16 * t + q
            ws = []
            if P == 0:
                ws.append(("s_ssp", 16 * 16))
            if P >= 2:
                ws.append(("s_f1m", HC * t + 2 * q - 2))
            dma(sync, "s_s1l", s1p_v[P % 2], s1_d[q, :, :], waits=ws)

        def s2p_emit(sync, t, q):
            P = 16 * t + q
            ws = []
            if P == 0:
                ws.append(("s_ssp", 16 * 32))
            if P >= 2:
                ws.append(("s_f2m", (P - 2) // 2 + 1))
            dma(sync, "s_s2l", s2p_v[P % 2], s2_d[q, :, :], waits=ws)

        def ln_ve_parts(vector, k):
            L = ln_sc.ap()[:, 8 * (k % 2):8 * (k % 2) + 8]
            base = 7 * k

            def mk_a(k=k, L=L, base=base):
                p = k % 2
                if k >= 1:
                    vector.wait_ge(SM["s_lnc"], base - 7)
                i = vector.tensor_reduce(out=L[:, 0:1], in_=xt_v[k % 4],
                                         axis=AX.X, op=ALU.add)
                xs, xv = xld_sem(k)
                i._wait_ge(SM[xs], xv)
                i.then_inc(SM["s_lnc"], 1)
                vector.wait_ge(SM["s_lnc"], base + 1)
                i = vector.tensor_scalar_mul(out=L[:, 1:2], in0=L[:, 0:1],
                                             scalar1=1.0 / D)
                i.then_inc(SM["s_lnc"], 1)
                vector.wait_ge(SM["s_lnc"], base + 2)
                i = vector.tensor_tensor(out=L[:, 2:3], in0=L[:, 1:2],
                                         in1=L[:, 1:2], op=ALU.mult)
                i.then_inc(SM["s_lnc"], 1)

            def mk_b1(k=k, L=L, base=base):
                vector.wait_ge(SM["s_lnc"], base + 3)
                i = vector.tensor_scalar(out=L[:, 3:4], in0=L[:, 4:5],
                                         scalar1=1.0 / D, scalar2=L[:, 2:3],
                                         op0=ALU.mult, op1=ALU.subtract)
                i._wait_ge(SM["s_lns"], 2 * k + 1)
                i.then_inc(SM["s_lnc"], 1)
                vector.wait_ge(SM["s_lnc"], base + 4)
                i = vector.tensor_scalar_add(out=L[:, 5:6], in0=L[:, 3:4],
                                             scalar1=1e-5)
                i.then_inc(SM["s_lnc"], 1)
                vector.wait_ge(SM["s_lnc"], base + 5)
                i = vector.reciprocal(out=L[:, 6:7], in_=L[:, 5:6])
                i.then_inc(SM["s_lnc"], 1)

            def mk_b2(k=k, L=L):
                p = k % 2
                if k >= 8:
                    vector.wait_ge(SM["s_tr"], k - 7)
                i = vector.tensor_scalar(out=xnm_v[k % 8], in0=xt_v[k % 4],
                                         scalar1=L[:, 1:2], scalar2=L[:, 7:8],
                                         op0=ALU.subtract, op1=ALU.mult)
                i._wait_ge(SM["s_lns"], 2 * (k + 1))
                i.then_inc(SM["s_lnc"], 1)

            return mk_a, mk_b1, mk_b2

        def ln_compute_ve(vector, g):
            out = []
            for kk in range(4):
                mk_a, mk_b1, mk_b2 = ln_ve_parts(vector, 4 * g + kk)
                out.append((152 + 80 * kk, mk_a))
                out.append((184 + 80 * kk, mk_b1))
                out.append((208 + 80 * kk, mk_b2))
            return out

        def ln_compute_se(scalar, g):
            out = []
            for kk in range(4):
                k = 4 * g + kk
                L = ln_sc.ap()[:, 8 * (k % 2):8 * (k % 2) + 8]

                def mk_sq(k=k, L=L):
                    p = k % 2
                    if k >= 1:
                        scalar.wait_ge(SM["s_lns"], 2 * (k - 1))
                    if k >= 2:
                        scalar.wait_ge(SM["s_lnc"], 7 * (k - 2) + 4)
                    if k >= 8:
                        scalar.wait_ge(SM["s_tr"], k - 7)
                    i = scalar.activation(out=xnm_v[k % 8], in_=xt_v[k % 4],
                                          func=AF.Square, accum_out=L[:, 4:5])
                    xs, xv = xld_sem(k)
                    i._wait_ge(SM[xs], xv)
                    i.then_inc(SM["s_lns"], 1)

                def mk_sqrt(k=k, L=L):
                    i = scalar.activation(out=L[:, 7:8], in_=L[:, 6:7],
                                          func=AF.Sqrt)
                    i._wait_ge(SM["s_lnc"], 7 * k + 6)
                    i.then_inc(SM["s_lns"], 1)

                out.append((176 + 80 * kk, mk_sq))
                out.append((196 + 80 * kk, mk_sqrt))
            return out

        def tr_emit(tensor, g, kk):
            k = 4 * g + kk
            tensor.wait_ge(SM["s_lne"], 8 * k)
            for dcb in range(DC):
                i = tensor.transpose(
                    out=pb_tr.ap()[:, 128 * dcb:128 * (dcb + 1)],
                    in_=xnm_v[k % 8][:, 128 * dcb:128 * (dcb + 1)],
                    identity=ident_b.ap()[:, :])
                if dcb == 0:
                    i._wait_ge(SM["s_lnc"], 7 * k + 7)
                if dcb == DC - 1:
                    i.then_inc(SM["s_tr"], 1)

        def ln_evac_tile(vector, k):
            kk = k % 4
            g = k // 4
            fns = []
            for dcb in range(DC):
                def mk(k=k, kk=kk, dcb=dcb, g=g):
                    if dcb == 0 and kk == 0 and g > 0:
                        vector.wait_ge(SM["s_xsp"], 16 * DC * g)
                    i = vector.tensor_copy(
                        out=xnst_v[:, TT * dcb + 128 * kk:
                                   TT * dcb + 128 * (kk + 1)],
                        in_=pb_tr.ap()[:, 128 * dcb:128 * (dcb + 1)])
                    if dcb == 0:
                        i._wait_ge(SM["s_tr"], k + 1)
                    i.then_inc(SM["s_lne"], 1)
                fns.append(mk)
            return fns

        def ln_evac(vector, g):
            out = []
            for kk in range(4):
                for dcb, mk in enumerate(ln_evac_tile(vector, 4 * g + kk)):
                    out.append((12 + 4 * (8 * kk + dcb), mk))
            return out

        def xsp_emit(gpsimd, g):
            for dc in range(DC):
                ws = [("s_lne", 8 * (4 * g + 4))] if dc == 0 else []
                dma(gpsimd, "s_xsp", xnt_d[dc, :, TT * g:TT * (g + 1)],
                    xnst_v[:, TT * dc:TT * (dc + 1)], waits=ws)

        def bx_mm(tensor, w):
            for ec in range(DC):
                g = DC * w + ec
                if g >= 2:
                    tensor.wait_ge(SM["s_bxe"], g - 1)
                for dc in range(DC):
                    i = tensor.matmul(
                        out=pb[ec % 2].ap()[:, :],
                        lhsT=wbt_sb.ap()[:, D * dc + 128 * ec:
                                         D * dc + 128 * (ec + 1)],
                        rhs=xnw_v[w % 3][dc],
                        start=(dc == 0), stop=(dc == DC - 1))
                    if dc == 0:
                        i._wait_ge(SM["s_xnw"], 16 * DC * (w + 1))
                    if dc == DC - 1:
                        i.then_inc(SM["s_bxm"], 1)

        def f1_mm(tensor, t):
            if t == 0:
                tensor.wait_ge(SM["s_sgn"], 16)
            for q in range(16):
                tensor.wait_ge(SM["s_s1l"], 16 * (16 * t + q + 1))
                for r in range(2):
                    hc = 2 * q + r
                    g = HC * t + hc
                    if g >= 2:
                        tensor.wait_ge(SM["s_f1e"], g - 1)
                    for dc in range(DC):
                        i = tensor.matmul(
                            out=pb[2 + hc % 2].ap()[:, :],
                            lhsT=s1p_v[(16 * t + q) % 2][:, r * 1024 + dc * 128:
                                                         r * 1024 + dc * 128 + 128],
                            rhs=xnw_v[t % 3][dc],
                            start=(dc == 0), stop=(dc == DC - 1))
                        if dc == 0:
                            i._wait_ge(SM["s_xnw"], 16 * DC * (t + 1))
                        if dc == DC - 1:
                            i.then_inc(SM["s_f1m"], 1)

        def f2_mm(tensor, t, douts):
            if t == 0 and douts[0] == 0:
                tensor.wait_ge(SM["s_sgn"], 32)
            for dout in douts:
                g = DC * t + dout
                if g >= 2:
                    tensor.wait_ge(SM["s_f2e"], g - 1)
                for hc in range(HC):
                    q2 = 2 * dout + hc // 16
                    if hc % 16 == 0:
                        tensor.wait_ge(SM["s_s2l"], 16 * (16 * t + q2 + 1))
                    i = tensor.matmul(
                        out=pb[4 + dout % 2].ap()[:, :],
                        lhsT=s2p_v[(16 * t + q2) % 2][:, (hc % 16) * 128:
                                                      (hc % 16) * 128 + 128],
                        rhs=hst_v[:, TT * hc:TT * (hc + 1)],
                        start=(hc == 0), stop=(hc == HC - 1))
                    i._wait_ge(SM["s_f1e"], HC * t + hc + 1)
                    if hc == HC - 1:
                        i.then_inc(SM["s_f2m"], 1)

        @blk.sync
        def _(sync):
            for srcp, dst in ((bB_d, bB_sb), (b1_d, b1_sb), (b2_d, b2_sb),
                              (bC_d, bC_sb), (a_d, a_sb)):
                dma(sync, "s_wld", dst.ap()[:, :], srcp[:, :])
            for c in range(HC):
                ws = []
                if c >= 4:
                    ws = [("s_alv", c - 3), ("s_sgT", c - 3)]
                dma(sync, cW(c)[0], wsc[c % 4],
                    w1n[128 * c:128 * (c + 1), :], waits=ws)
            for p in range(32):
                c = 32 + p
                ws = [("s_alv", 29 + p if p < 4 else c_avr(p - 4)),
                      ("s_sgT", c - 3)]
                dc, jp = p // 4, p % 4
                dma(sync, cW(c)[0], wsc[c % 4],
                    w2n[128 * dc:128 * (dc + 1), 1024 * jp:1024 * (jp + 1)],
                    waits=ws)
            sync.wait_ge(SM["s_wld"], 16 * 5)
            sync.wait_ge(SM["s_wl1"], 16 * 32)
            sync.wait_ge(SM["s_wl2"], 16 * 32)
            sync.nop().then_inc(SM["s_ph"], 1)
            bw(sync, 1)
            for g in range(min(5, NG)):
                x_emit(sync, g)
            xnw_emit(sync, 0)
            if NT > 1:
                xnw_emit(sync, 1)
            for w in range(NT):
                if w + 2 < NT:
                    xnw_emit(sync, w + 2)
                if w >= 1:
                    for q in range(16):
                        s2p_emit(sync, w - 1, q)
                for q in range(16):
                    s1p_emit(sync, w, q)
            for q in range(16):
                s2p_emit(sync, NT - 1, q)
            sync.wait_ge(SM["s_xld"], 16 * min(NL, 20))
            sync.wait_ge(SM["s_xnw"], 16 * DC * NT)
            sync.wait_ge(SM["s_s1l"], 16 * 16 * NT)
            sync.wait_ge(SM["s_s2l"], 16 * 16 * NT)
            sync.nop().then_inc(SM["s_ph"], 1)
            bw(sync, 2)
            for t in range(NT):
                dma(sync, "s_hsl", hsb_v[t % 2], hs_d[t, :, :],
                    waits=[("s_wcm", DC * (t - 1) if t >= 2 else 0)])
                for s4 in range(4):
                    dma(sync, "s_xr", xr_v[s4],
                        x_in[TT * t + 128 * s4:TT * t + 128 * (s4 + 1), :],
                        waits=[("s_cmb", 8 * t if (t >= 1 and s4 == 0) else 0)])
            sync.wait_ge(SM["s_yst"], 16 * 4 * NT)

        @blk.gpsimd
        def _(gpsimd):
            gpsimd.iota(io_r.ap()[:, :], pattern=[[1, 128]], base=0,
                        channel_multiplier=0).then_inc(SM["s_gld"], 1)
            gpsimd.iota(io_c.ap()[:, :], pattern=[[0, 128]], base=0,
                        channel_multiplier=1).then_inc(SM["s_gld"], 1)
            for i in range(DC):
                dma(gpsimd, "s_gw", wbt_sb.ap()[:, i * D:(i + 1) * D],
                    wbt[128 * i:128 * (i + 1), :])
            for q in range(16):
                dma(gpsimd, "s_ssp", s1_d[q, :, :], pstage[q % 2],
                    waits=[("s_sgn", 4 * q + 4)])
            for q2 in range(16):
                dma(gpsimd, "s_ssp", s2_d[q2, :, :], pstage[q2 % 2],
                    waits=[("s_sgn", 64 + 4 * q2 + 4)])
            gpsimd.wait_ge(SM["s_gld"], 2)
            gpsimd.wait_ge(SM["s_gw"], 16 * DC)
            gpsimd.wait_ge(SM["s_ssp"], 16 * 32)
            gpsimd.engine_nop().then_inc(SM["s_ph"], 1)
            bw(gpsimd, 1)
            for g in range(min(3, NG)):
                xsp_emit(gpsimd, g)
            for w in range(NT):
                if w + 3 < NG:
                    xsp_emit(gpsimd, w + 3)
                if w >= 1:
                    for dout in range(DC):
                        dma(gpsimd, "s_gsp",
                            g_d[w - 1, :, TT * dout:TT * (dout + 1)],
                            gst_v[dout % 2],
                            waits=[("s_f2e", DC * (w - 1) + dout + 1)])
                    dma(gpsimd, "s_hss", hs_d[w - 1, :, :], hsst_v,
                        waits=[("s_hc", DC * w)])
                if w + 5 < NG:
                    x_emit(gpsimd, w + 5)
            dma(gpsimd, "s_hss", hs_d[NT - 1, :, :], hsst_v,
                waits=[("s_hc", DC * NT)])
            for dout in range(DC):
                dma(gpsimd, "s_gsp", g_d[NT - 1, :, TT * dout:TT * (dout + 1)],
                    gst_v[dout % 2],
                    waits=[("s_f2e", DC * (NT - 1) + dout + 1)])
            gpsimd.wait_ge(SM["s_hss"], 16 * NT)
            gpsimd.wait_ge(SM["s_gsp"], 16 * DC * NT)
            gpsimd.wait_ge(SM["s_xsp"], 16 * DC * NG)
            if NL > 20:
                gpsimd.wait_ge(SM["s_xld2"], 16 * (NL - 20))
            gpsimd.engine_nop().then_inc(SM["s_ph"], 1)
            bw(gpsimd, 2)
            for i in range(DC):
                dma(gpsimd, "s_gw", wct_v[:, i * D:(i + 1) * D],
                    wct[128 * i:128 * (i + 1), :])
            for t in range(NT):
                dma(gpsimd, "s_gl", gb_v[t % 2], g_d[t, :, :],
                    waits=[("s_sg", DC * (t - 1) if t >= 2 else 0)])
                for s4 in range(4):
                    u = 4 * t + s4
                    dma(gpsimd, "s_yst",
                        y_out[TT * t + 128 * s4:TT * t + 128 * (s4 + 1), :],
                        ost_v[s4], waits=[("s_cmb", 2 * (u + 1))])

        @blk.vector
        def _(vector):
            i = vector.tensor_tensor(out=ident_b.ap()[:, :], in0=io_r.ap()[:, :],
                                     in1=io_c.ap()[:, :], op=ALU.is_equal)
            i._wait_ge(SM["s_gld"], 2)
            i.then_inc(SM["s_prep"], 1)
            i = vector.tensor_copy(out=ident_f.ap()[:, :], in_=ident_b.ap()[:, :])
            i._wait_ge(SM["s_prep"], 1)
            i.then_inc(SM["s_prep"], 1)
            vector.memset(zs.ap()[:, :], 0.0).then_inc(SM["s_prep"], 1)
            for hc in range(HC):
                i = vector.tensor_reduce(out=al1_sb.ap()[:, hc:hc + 1],
                                         in_=wsc[hc % 4], axis=AX.X,
                                         op=ALU.add, apply_absolute_value=True)
                i._wait_ge(SM[cW(hc)[0]], cW(hc)[1])
                i.then_inc(SM["s_alv"], 1)
            for p in range(32):
                if p % 4 == 0 and p >= 4:
                    vector.wait_ge(SM["s_alv"], c_avr(p - 1) + 1)
                i = vector.tensor_reduce(out=al2p_sb.ap()[:, p % 4:p % 4 + 1],
                                         in_=wsc[(32 + p) % 4], axis=AX.X,
                                         op=ALU.add, apply_absolute_value=True)
                i._wait_ge(SM[cW(32 + p)[0]], cW(32 + p)[1])
                i.then_inc(SM["s_alv"], 1)
                if p % 4 == 3:
                    i = vector.tensor_reduce(out=al2_sb.ap()[:, p // 4:p // 4 + 1],
                                             in_=al2p_sb.ap()[:, :], axis=AX.X,
                                             op=ALU.add)
                    i._wait_ge(SM["s_alv"], c_avr(p))
                    i.then_inc(SM["s_alv"], 1)
            vector.wait_ge(SM["s_alv"], 72)
            vector.tensor_scalar_mul(out=al1s_sb.ap()[:, :], in0=al1_sb.ap()[:, :],
                                     scalar1=1.0 / D).then_inc(SM["s_prep"], 1)
            vector.tensor_scalar_mul(out=al2s_sb.ap()[:, :], in0=al2_sb.ap()[:, :],
                                     scalar1=1.0 / H).then_inc(SM["s_prep"], 1)
            vector.wait_ge(SM["s_prep"], 5)
            vector.engine_nop().then_inc(SM["s_ph"], 1)
            bw(vector, 1)
            pre_tiles = 4 * min(4, NG)
            pre_ev = 4 * min(3, NG)
            for k in range(pre_tiles):
                mk_a, mk_b1, mk_b2 = ln_ve_parts(vector, k)
                mk_a()
                mk_b1()
                mk_b2()
                if k >= 1 and (k - 1) < pre_ev:
                    for fn in ln_evac_tile(vector, k - 1):
                        fn()
            if pre_ev >= pre_tiles:
                for fn in ln_evac_tile(vector, pre_tiles - 1):
                    fn()
            hswr = hswin_v.rearrange("p (s c) -> p c s", c=8)
            for w in range(NT):
                ins = {}

                def put(j, fn):
                    ins.setdefault(j, []).append(fn)
                if w >= 1:
                    for dc in range(DC):
                        def mkc(dc=dc, w=w):
                            pw = (w - 1) % 2
                            if dc == 0:
                                vector.wait_ge(SM["s_hss"], 16 * (w - 1))
                            i = vector.tensor_copy(
                                out=hsst_v[:, TT * dc:TT * (dc + 1)],
                                in_=hswr[:, dc, pw * TT:pw * TT + TT])
                            i._wait_ge(SM["ss"], w * TT)
                            i.then_inc(SM["s_hc"], 1)
                        put(64 + 56 * dc, mkc)
                if w + 3 < NG:
                    for j, fn in ln_evac(vector, w + 3):
                        put(j, fn)
                if w + 4 < NG:
                    for j, fn in ln_compute_ve(vector, w + 4):
                        put(j, fn)
                vector.wait_ge(SM["s_bxe"], DC * (w + 1))
                for j in range(TT):
                    t = w * TT + j
                    prev = (t - 1) % (2 * TT)
                    hp = zs.ap()[:, 0:8] if t == 0 else \
                        hswin_v[:, 8 * prev:8 * prev + 8]
                    i1 = vector.tensor_tensor(
                        out=m_sc.ap()[:, 8 * (t % 2):8 * (t % 2) + 8],
                        in0=hp, in1=a_sb.ap()[:, 0:8], op=ALU.mult)
                    if t > 0:
                        i1._wait_ge(SM["ss"], t)
                    i1.then_inc(SM["vm"], 1)
                    usl = t % 64
                    i2 = vector.tensor_tensor(
                        out=pb_u.ap()[:, 8 * usl:8 * usl + 8],
                        in0=m_sc.ap()[:, 8 * (t % 2):8 * (t % 2) + 8],
                        in1=bxw_v[w % 2][:, 8 * j:8 * j + 8], op=ALU.add)
                    i2._wait_ge(SM["vm"], t + 1)
                    i2.then_inc(SM["sv"], 1)
                    for fn in ins.get(j, ()):
                        fn()
            vector.wait_ge(SM["ss"], NT * TT)
            vector.wait_ge(SM["s_hss"], 16 * (NT - 1))
            pw = (NT - 1) % 2
            for dc in range(DC):
                i = vector.tensor_copy(
                    out=hsst_v[:, TT * dc:TT * (dc + 1)],
                    in_=hswr[:, dc, pw * TT:pw * TT + TT])
                i.then_inc(SM["s_hc"], 1)
            vector.wait_ge(SM["s_hc"], DC * NT)
            vector.wait_ge(SM["sv"], S)
            vector.wait_ge(SM["vm"], S)
            vector.wait_ge(SM["s_lne"], 8 * NL)
            vector.wait_ge(SM["s_lnc"], 7 * NL)
            vector.engine_nop().then_inc(SM["s_ph"], 1)
            bw(vector, 2)
            for t in range(NT):
                p = t % 2
                if t >= 1:
                    vector.wait_ge(SM["s_wce"], DC * (t - 1))
                    vector.wait_ge(SM["s_yst"], 16 * 4 * t)
                for ec in range(DC):
                    i = vector.tensor_scalar_add(
                        out=sst_v[p][:, TT * ec:TT * (ec + 1)],
                        in0=pb[(0, 1, 4, 5)[ec % 4]].ap()[:, :],
                        scalar1=bC_sb.ap()[:, ec:ec + 1])
                    i._wait_ge(SM["s_wcm"], DC * t + ec + 1)
                    i.then_inc(SM["s_wce"], 1)
                vector.wait_ge(SM["s_wce"], DC * (t + 1))
                for ec in range(DC):
                    i = vector.tensor_tensor(
                        out=sgst_v[p][:, TT * ec:TT * (ec + 1)],
                        in0=sst_v[p][:, TT * ec:TT * (ec + 1)],
                        in1=gb_v[p][:, TT * ec:TT * (ec + 1)], op=ALU.mult)
                    if ec == 0:
                        i._wait_ge(SM["s_gl"], 16 * (t + 1))
                    i.then_inc(SM["s_sg"], 1)
                for s4 in range(4):
                    u = 4 * t + s4
                    vector.wait_ge(SM["s_xr"], 16 * (u + 1))
                    i = vector.tensor_tensor(
                        out=ost_v[s4][:, 0:512], in0=pb[2].ap()[:, :],
                        in1=xr_v[s4][:, 0:512], op=ALU.add)
                    i._wait_ge(SM["s_tr2"], u + 1)
                    i.then_inc(SM["s_cmb"], 1)
                    i = vector.tensor_tensor(
                        out=ost_v[s4][:, 512:1024], in0=pb[3].ap()[:, :],
                        in1=xr_v[s4][:, 512:1024], op=ALU.add)
                    i._wait_ge(SM["s_xr"], 16 * (u + 1))
                    i.then_inc(SM["s_cmb"], 1)

        @blk.scalar
        def _(scalar):
            for c in range(64):
                q = c // 2
                for half in range(2):
                    if half == 0 and c >= 4:
                        scalar.wait_ge(SM["s_ssp"], 16 * (q - 1))
                    i = scalar.activation(
                        out=pstage[q % 2][:, (c % 2) * 1024 + 512 * half:
                                          (c % 2) * 1024 + 512 * (half + 1)],
                        in_=pb[2 + 2 * (c % 2) + half].ap()[:, :], func=AF.Sign)
                    i._wait_ge(SM["s_sgT"], c + 1)
                    i.then_inc(SM["s_sgn"], 1)
            scalar.wait_ge(SM["s_sgn"], 128)
            scalar.nop().then_inc(SM["s_ph"], 1)
            bw(scalar, 1)
            for g in range(min(4, NG)):
                for _, fn in sorted(ln_compute_se(scalar, g), key=lambda x: x[0]):
                    fn()
            bxwr = [bxw_v[p].rearrange("p (t c) -> p c t", c=8) for p in range(2)]
            for ec in range(DC):
                i = scalar.add(out=bxwr[0][:, ec, :], in_=pb[ec % 2].ap()[:, :],
                               add=bB_sb.ap()[:, ec:ec + 1])
                i._wait_ge(SM["s_bxm"], ec + 1)
                i.then_inc(SM["s_bxe"], 1)
            for w in range(NT):
                ins = {}

                def put(j, fn):
                    ins.setdefault(j, []).append(fn)
                if w + 1 < NT:
                    for ec in range(DC):
                        def mke(ec=ec, w=w):
                            if ec == 0:
                                scalar.wait_ge(SM["sv"], w * TT)
                            i = scalar.add(out=bxwr[(w + 1) % 2][:, ec, :],
                                           in_=pb[ec % 2].ap()[:, :],
                                           add=bB_sb.ap()[:, ec:ec + 1])
                            i._wait_ge(SM["s_bxm"], DC * (w + 1) + ec + 1)
                            i.then_inc(SM["s_bxe"], 1)
                        put(150 + 8 * ec, mke)
                if w >= 1:
                    for dout in range(DC):
                        def mks(dout=dout, w=w):
                            t = w - 1
                            if DC * t + dout >= 2:
                                scalar.wait_ge(SM["s_gsp"],
                                               16 * (DC * t + dout - 1))
                            i = scalar.activation(
                                out=gst_v[dout % 2],
                                in_=pb[4 + dout % 2].ap()[:, :], func=AF.Silu,
                                bias=b2_sb.ap()[:, dout:dout + 1],
                                scale=al2s_sb.ap()[:, dout:dout + 1])
                            i._wait_ge(SM["s_f2m"], DC * t + dout + 1)
                            i.then_inc(SM["s_f2e"], 1)
                        put(230 + 16 * dout, mks)
                for hc in range(HC):
                    def mkg(hc=hc, w=w):
                        if hc == 0:
                            scalar.wait_ge(SM["s_f2m"], DC * w)
                            if w == 0:
                                scalar.wait_ge(SM["s_prep"], 5)
                        i = scalar.activation(
                            out=hst_v[:, TT * hc:TT * (hc + 1)],
                            in_=pb[2 + hc % 2].ap()[:, :], func=AF.Gelu,
                            bias=b1_sb.ap()[:, hc:hc + 1],
                            scale=al1s_sb.ap()[:, hc:hc + 1])
                        i._wait_ge(SM["s_f1m"], HC * w + hc + 1)
                        i.then_inc(SM["s_f1e"], 1)
                    put(384 + 4 * hc, mkg)
                if w + 4 < NG:
                    for j, fn in ln_compute_se(scalar, w + 4):
                        put(j, fn)
                if w >= 2:
                    scalar.wait_ge(SM["s_hc"], DC * (w - 1))
                for j in range(TT):
                    t = w * TT + j
                    sl = t % (2 * TT)
                    usl = t % 64
                    i3 = scalar.activation(
                        out=hswin_v[:, 8 * sl:8 * sl + 8],
                        in_=pb_u.ap()[:, 8 * usl:8 * usl + 8], func=AF.Tanh)
                    i3._wait_ge(SM["sv"], t + 1)
                    i3.then_inc(SM["ss"], 1)
                    for fn in ins.get(j, ()):
                        fn()
            for dout in range(DC):
                t = NT - 1
                if DC * t + dout >= 2:
                    scalar.wait_ge(SM["s_gsp"], 16 * (DC * t + dout - 1))
                i = scalar.activation(
                    out=gst_v[dout % 2], in_=pb[4 + dout % 2].ap()[:, :],
                    func=AF.Silu, bias=b2_sb.ap()[:, dout:dout + 1],
                    scale=al2s_sb.ap()[:, dout:dout + 1])
                i._wait_ge(SM["s_f2m"], DC * t + dout + 1)
                i.then_inc(SM["s_f2e"], 1)
            scalar.wait_ge(SM["ss"], S)
            scalar.wait_ge(SM["s_bxe"], DC * NT)
            scalar.wait_ge(SM["s_f1e"], HC * NT)
            scalar.wait_ge(SM["s_f2e"], DC * NT)
            scalar.wait_ge(SM["s_lns"], 2 * NL)
            scalar.nop().then_inc(SM["s_ph"], 1)
            bw(scalar, 2)

        @blk.tensor
        def _(tensor):
            tensor.wait_ge(SM["s_prep"], 2)
            for c in range(64):
                if c >= 2:
                    tensor.wait_ge(SM["s_sgn"], 2 * c - 2)
                for b in range(8):
                    bank = pb[2 + 2 * (c % 2) + b // 4]
                    i = tensor.transpose(
                        out=bank.ap()[:, 128 * (b % 4):128 * (b % 4 + 1)],
                        in_=wsc[c % 4][:, 128 * b:128 * (b + 1)],
                        identity=ident_f.ap()[:, :])
                    if b == 0:
                        i._wait_ge(SM[cW(c)[0]], cW(c)[1])
                    if b == 7:
                        i.then_inc(SM["s_sgT"], 1)
            tensor.wait_ge(SM["s_sgT"], 64)
            tensor.nop().then_inc(SM["s_ph"], 1)
            bw(tensor, 1)
            for g in range(min(3, NG)):
                for kk in range(4):
                    tr_emit(tensor, g, kk)
            bx_mm(tensor, 0)
            for w in range(NT):
                if w + 3 < NG:
                    for kk in range(4):
                        tr_emit(tensor, w + 3, kk)
                if w + 1 < NT:
                    bx_mm(tensor, w + 1)
                if w >= 1:
                    f2_mm(tensor, w - 1, [0, 1, 2, 3])
                    f2_mm(tensor, w - 1, [4, 5, 6, 7])
                f1_mm(tensor, w)
            f2_mm(tensor, NT - 1, [0, 1, 2, 3])
            f2_mm(tensor, NT - 1, [4, 5, 6, 7])
            tensor.wait_ge(SM["s_bxm"], DC * NT)
            tensor.wait_ge(SM["s_f1m"], HC * NT)
            tensor.wait_ge(SM["s_f2m"], DC * NT)
            tensor.wait_ge(SM["s_tr"], NL)
            tensor.nop().then_inc(SM["s_ph"], 1)
            bw(tensor, 2)
            tensor.wait_ge(SM["s_gw"], 16 * 2 * DC)
            WB = (0, 1, 4, 5)
            for t in range(NT):
                p = t % 2
                for ec in range(DC):
                    g = DC * t + ec
                    if g >= 4:
                        tensor.wait_ge(SM["s_wce"], g - 3)
                    for dc in range(DC):
                        i = tensor.matmul(
                            out=pb[WB[ec % 4]].ap()[:, :],
                            lhsT=wct_v[:, D * dc + 128 * ec:
                                       D * dc + 128 * (ec + 1)],
                            rhs=hsb_v[p][:, TT * dc:TT * (dc + 1)],
                            start=(dc == 0), stop=(dc == DC - 1))
                        if dc == 0:
                            i._wait_ge(SM["s_hsl"], 16 * (t + 1))
                        if dc == DC - 1:
                            i.then_inc(SM["s_wcm"], 1)
                for s4 in range(4):
                    u = 4 * t + s4
                    tensor.wait_ge(SM["s_cmb"], 2 * u)
                    for dc in range(DC):
                        i = tensor.transpose(
                            out=pb[2 + dc // 4].ap()[:, 128 * (dc % 4):
                                                     128 * ((dc % 4) + 1)],
                            in_=sgst_v[p][:, TT * dc + 128 * s4:
                                          TT * dc + 128 * (s4 + 1)],
                            identity=ident_f.ap()[:, :])
                        if dc == 0:
                            i._wait_ge(SM["s_sg"], DC * (t + 1))
                        if dc == DC - 1:
                            i.then_inc(SM["s_tr2"], 1)

    return nc


def kernel(**inputs):
    S = 8192
    x = np.asarray(inputs["L0_attn_x"], np.float32)
    A = np.asarray(inputs["L0_attn_A"], np.float32)
    W_B = np.asarray(inputs["L0_attn_W_B"], np.float32)
    b_B = np.asarray(inputs["L0_attn_b_B"], np.float32)
    W_C = np.asarray(inputs["L0_attn_W_C"], np.float32)
    b_C = np.asarray(inputs["L0_attn_b_C"], np.float32)
    w1 = np.asarray(inputs["L0_ffn_w1"], np.float32)
    b1 = np.asarray(inputs["L0_ffn_b1"], np.float32)
    w2 = np.asarray(inputs["L0_ffn_w2"], np.float32)
    b2 = np.asarray(inputs["L0_ffn_b2"], np.float32)

    cm = np.ascontiguousarray
    shared = {
        "wbt": cm(W_B.T), "wct": cm(W_C.T),
        "w1n": cm(w1), "w2n": cm(w2),
        "bB": cm(b_B.reshape(DC, 128).T), "b1": cm(b1.reshape(HC, 128).T),
        "b2": cm(b2.reshape(DC, 128).T), "bC": cm(b_C.reshape(DC, 128).T),
        "afull": cm(A.reshape(DC, 128).T),
    }
    in_maps = []
    for c in range(8):
        m = dict(shared)
        m["x"] = cm(x[c // 2])
        in_maps.append(m)

    nc = build_nc(S)
    res = run_bass_kernel_spmd(nc, in_maps, core_ids=list(range(8)))
    out = np.stack([np.asarray(res.results[2 * b]["y"], np.float32)
                    for b in range(4)], axis=0)
    return out


# version 08 (final)
# speedup vs baseline: 1.0638x; 1.0004x over previous
import numpy as np
from contextlib import ExitStack

import concourse.bass as bass
import concourse.mybir as mybir
from concourse.bass_utils import run_bass_kernel_spmd

F32 = mybir.dt.float32
BF16 = mybir.dt.bfloat16
I32 = mybir.dt.int32
AF = mybir.ActivationFunctionType
ALU = mybir.AluOpType
AX = mybir.AxisListType

D = 1024
H = 4096
DC = D // 128
HC = H // 128
TT = 512

N_ENG = 5


def build_nc(S):
    NT = S // TT
    NL = S // 128
    NG = NT
    nc = bass.Bass("TRN2", target_bir_lowering=False, debug=False)

    x_in = nc.dram_tensor("x", [S, D], F32, kind="ExternalInput").ap()
    wbt = nc.dram_tensor("wbt", [D, D], F32, kind="ExternalInput").ap()
    wct = nc.dram_tensor("wct", [D, D], F32, kind="ExternalInput").ap()
    w1n = nc.dram_tensor("w1n", [H, D], F32, kind="ExternalInput").ap()
    w2n = nc.dram_tensor("w2n", [D, H], F32, kind="ExternalInput").ap()
    bB_d = nc.dram_tensor("bB", [128, DC], F32, kind="ExternalInput").ap()
    b1_d = nc.dram_tensor("b1", [128, HC], F32, kind="ExternalInput").ap()
    b2_d = nc.dram_tensor("b2", [128, DC], F32, kind="ExternalInput").ap()
    bC_d = nc.dram_tensor("bC", [128, DC], F32, kind="ExternalInput").ap()
    a_d = nc.dram_tensor("afull", [128, DC], F32, kind="ExternalInput").ap()
    y_out = nc.dram_tensor("y", [S, D], F32, kind="ExternalOutput").ap()

    xnt_d = nc.dram_tensor("xnt_d", [DC, 128, S], BF16, kind="Internal").ap()
    hs_d = nc.dram_tensor("hs_d", [NT, 128, DC * TT], BF16, kind="Internal").ap()
    g_d = nc.dram_tensor("g_d", [NT, 128, DC * TT], F32, kind="Internal").ap()
    s1_d = nc.dram_tensor("s1_d", [16, 128, 2048], BF16, kind="Internal").ap()
    s2_d = nc.dram_tensor("s2_d", [16, 128, 2048], BF16, kind="Internal").ap()

    with ExitStack() as ctx:
        E = ctx.enter_context
        ident_b = E(nc.sbuf_tensor("ident_b", [128, 128], BF16))
        ident_f = E(nc.sbuf_tensor("ident_f", [128, 128], F32))
        io_r = E(nc.sbuf_tensor("io_r", [128, 128], I32))
        io_c = E(nc.sbuf_tensor("io_c", [128, 128], I32))
        wbt_sb = E(nc.sbuf_tensor("wbt_sb", [128, DC * D], BF16))
        bB_sb = E(nc.sbuf_tensor("bB_sb", [128, DC], F32))
        b1_sb = E(nc.sbuf_tensor("b1_sb", [128, HC], F32))
        b2_sb = E(nc.sbuf_tensor("b2_sb", [128, DC], F32))
        bC_sb = E(nc.sbuf_tensor("bC_sb", [128, DC], F32))
        a_sb = E(nc.sbuf_tensor("a_sb", [128, DC], F32))
        al1_sb = E(nc.sbuf_tensor("al1_sb", [128, HC], F32))
        al1s_sb = E(nc.sbuf_tensor("al1s_sb", [128, HC], F32))
        al2_sb = E(nc.sbuf_tensor("al2_sb", [128, DC], F32))
        al2s_sb = E(nc.sbuf_tensor("al2s_sb", [128, DC], F32))
        al2p_sb = E(nc.sbuf_tensor("al2p_sb", [128, 4], F32))
        zs = E(nc.sbuf_tensor("zs", [128, 8], F32))
        m_sc = E(nc.sbuf_tensor("m_sc", [128, 16], F32))
        ln_sc = E(nc.sbuf_tensor("ln_sc", [128, 16], F32))

        AR_W = 188 * 1024
        arena = E(nc.sbuf_tensor("arena", [128, AR_W // 2], BF16))
        ar_f = arena.bitcast(F32)

        def fview(byte_off, n_el):
            assert byte_off % 4 == 0 and byte_off + 4 * n_el <= AR_W, byte_off
            return ar_f.ap()[:, byte_off // 4: byte_off // 4 + n_el]

        def bview(byte_off, n_el):
            assert byte_off % 2 == 0 and byte_off + 2 * n_el <= AR_W, byte_off
            return arena.ap()[:, byte_off // 2: byte_off // 2 + n_el]

        K = 1024
        bxw_v = [fview(0 + 16 * K * p, 8 * TT) for p in range(2)]
        xnw_v = [[bview(32 * K + 8 * K * r + 1024 * dc, TT)
                  for dc in range(DC)] for r in range(3)]
        hswin_v = fview(56 * K, 2 * 8 * TT)
        hsst_v = bview(88 * K, DC * TT)
        xt_v = [fview(96 * K + 4 * K * p, 1024) for p in range(4)]
        xnm_v = [bview(112 * K + 2 * K * p, 1024) for p in range(8)]
        xnst_v = bview(128 * K, DC * TT)
        s1p_v = [bview(136 * K + 4 * K * p, 2048) for p in range(2)]
        s2p_v = [bview(144 * K + 4 * K * p, 2048) for p in range(2)]
        gst_v = [fview(152 * K + 2 * K * p, TT) for p in range(2)]
        hst_v = bview(156 * K, HC * TT)
        wsc = [fview(156 * K + 4 * K * p, 1024) for p in range(4)]
        pstage = [bview(72 * K + 4 * K * p, 2048) for p in range(2)]
        hsb_v = [bview(0 + 8 * K * p, DC * TT) for p in range(2)]
        gb_v = [fview(16 * K + 16 * K * p, DC * TT) for p in range(2)]
        sst_v = [fview(48 * K + 16 * K * p, DC * TT) for p in range(2)]
        sgst_v = [fview(80 * K + 16 * K * p, DC * TT) for p in range(2)]
        xr_v = [fview(112 * K + 4 * K * s, 1024) for s in range(4)]
        ost_v = [fview(128 * K + 4 * K * s, 1024) for s in range(4)]
        wct_v = bview(144 * K, DC * D)

        pb = [E(nc.psum_tensor(f"pb{i}", [128, 512], F32)) for i in range(6)]
        pb_tr = E(nc.psum_tensor("pb_tr", [128, 1024], BF16))
        pb_u = E(nc.psum_tensor("pb_u", [128, 512], F32))

        sem_names = [
            "s_wld", "s_xld", "s_xld2", "s_xnw", "s_gld", "s_gw",
            "s_lnc", "s_lne", "s_lns", "s_tr", "s_xsp",
            "s_prep", "s_sgn", "s_ssp", "s_wl1", "s_wl2", "s_alv",
            "s_sgT", "s_s1l", "s_s2l",
            "s_f1m", "s_f1e", "s_f2m", "s_f2e", "s_gsp",
            "s_bxm", "s_bxe",
            "ss", "sv", "vm", "s_hc", "s_hss",
            "s_hsl", "s_gl", "s_wcm", "s_wce", "s_sg", "s_tr2", "s_cmb",
            "s_xr", "s_yst", "s_ph",
        ]
        sems = {n: E(nc.semaphore(n)) for n in sem_names}
        SM = sems
        blk = E(nc.Block())

        dma_ctr = {n: 0 for n in sem_names}

        def dma(eng, sem, out, in_, waits=()):
            for s, v in waits:
                if v > 0:
                    eng.wait_ge(SM[s], v)
            pre = dma_ctr[sem]
            d = eng.dma_start(out=out, in_=in_)
            if pre > 0:
                d._wait_ge(SM[sem], pre)
            d.then_inc(SM[sem], 16)
            dma_ctr[sem] = pre + 16
            return d

        def cW(c):
            return ("s_wl1" if c % 2 == 0 else "s_wl2", 16 * (c // 2 + 1))

        def c_avr(p):
            return 32 + (p + 1) + p // 4

        def bw(eng, ph):
            eng.wait_ge(SM["s_ph"], N_ENG * ph)

        def xld_sem(k):
            if k < 20:
                return "s_xld", 16 * (k + 1)
            return "s_xld2", 16 * (k - 19)

        def x_emit(eng, g):
            for kk in range(4):
                k = 4 * g + kk
                ws = []
                if k >= 4:
                    ws = [("s_lns", 2 * (k - 4) + 1), ("s_lnc", 7 * (k - 3))]
                dma(eng, xld_sem(k)[0], xt_v[k % 4],
                    x_in[128 * k:128 * (k + 1), :], waits=ws)

        def xnw_emit(sync, g):
            for dc in range(DC):
                ws = []
                if dc == 0:
                    ws.append(("s_xsp", 16 * DC * (g + 1)))
                    if g >= 3:
                        ws.append(("s_f1m", HC * (g - 2)))
                        ws.append(("s_bxm", DC * (g - 1)))
                dma(sync, "s_xnw", xnw_v[g % 3][dc],
                    xnt_d[dc, :, TT * g:TT * (g + 1)], waits=ws)

        def s1p_emit(sync, t, q):
            P = 16 * t + q
            ws = []
            if P == 0:
                ws.append(("s_ssp", 16 * 16))
            if P >= 2:
                ws.append(("s_f1m", HC * t + 2 * q - 2))
            dma(sync, "s_s1l", s1p_v[P % 2], s1_d[q, :, :], waits=ws)

        def s2p_emit(sync, t, q):
            P = 16 * t + q
            ws = []
            if P == 0:
                ws.append(("s_ssp", 16 * 32))
            if P >= 2:
                ws.append(("s_f2m", (P - 2) // 2 + 1))
            dma(sync, "s_s2l", s2p_v[P % 2], s2_d[q, :, :], waits=ws)

        def ln_ve_parts(vector, k):
            L = ln_sc.ap()[:, 8 * (k % 2):8 * (k % 2) + 8]
            base = 7 * k

            def mk_a(k=k, L=L, base=base):
                p = k % 2
                if k >= 1:
                    vector.wait_ge(SM["s_lnc"], base - 7)
                i = vector.tensor_reduce(out=L[:, 0:1], in_=xt_v[k % 4],
                                         axis=AX.X, op=ALU.add)
                xs, xv = xld_sem(k)
                i._wait_ge(SM[xs], xv)
                i.then_inc(SM["s_lnc"], 1)
                vector.wait_ge(SM["s_lnc"], base + 1)
                i = vector.tensor_scalar_mul(out=L[:, 1:2], in0=L[:, 0:1],
                                             scalar1=1.0 / D)
                i.then_inc(SM["s_lnc"], 1)
                vector.wait_ge(SM["s_lnc"], base + 2)
                i = vector.tensor_tensor(out=L[:, 2:3], in0=L[:, 1:2],
                                         in1=L[:, 1:2], op=ALU.mult)
                i.then_inc(SM["s_lnc"], 1)

            def mk_b1(k=k, L=L, base=base):
                vector.wait_ge(SM["s_lnc"], base + 3)
                i = vector.tensor_scalar(out=L[:, 3:4], in0=L[:, 4:5],
                                         scalar1=1.0 / D, scalar2=L[:, 2:3],
                                         op0=ALU.mult, op1=ALU.subtract)
                i._wait_ge(SM["s_lns"], 2 * k + 1)
                i.then_inc(SM["s_lnc"], 1)
                vector.wait_ge(SM["s_lnc"], base + 4)
                i = vector.tensor_scalar_add(out=L[:, 5:6], in0=L[:, 3:4],
                                             scalar1=1e-5)
                i.then_inc(SM["s_lnc"], 1)
                vector.wait_ge(SM["s_lnc"], base + 5)
                i = vector.reciprocal(out=L[:, 6:7], in_=L[:, 5:6])
                i.then_inc(SM["s_lnc"], 1)

            def mk_b2(k=k, L=L):
                p = k % 2
                if k >= 8:
                    vector.wait_ge(SM["s_tr"], k - 7)
                i = vector.tensor_scalar(out=xnm_v[k % 8], in0=xt_v[k % 4],
                                         scalar1=L[:, 1:2], scalar2=L[:, 7:8],
                                         op0=ALU.subtract, op1=ALU.mult)
                i._wait_ge(SM["s_lns"], 2 * (k + 1))
                i.then_inc(SM["s_lnc"], 1)

            return mk_a, mk_b1, mk_b2

        def ln_compute_ve(vector, g):
            out = []
            for kk in range(4):
                mk_a, mk_b1, mk_b2 = ln_ve_parts(vector, 4 * g + kk)
                out.append((152 + 80 * kk, mk_a))
                out.append((184 + 80 * kk, mk_b1))
                out.append((208 + 80 * kk, mk_b2))
            return out

        def ln_compute_se(scalar, g):
            out = []
            for kk in range(4):
                k = 4 * g + kk
                L = ln_sc.ap()[:, 8 * (k % 2):8 * (k % 2) + 8]

                def mk_sq(k=k, L=L):
                    p = k % 2
                    if k >= 1:
                        scalar.wait_ge(SM["s_lns"], 2 * (k - 1))
                    if k >= 2:
                        scalar.wait_ge(SM["s_lnc"], 7 * (k - 2) + 4)
                    if k >= 8:
                        scalar.wait_ge(SM["s_tr"], k - 7)
                    i = scalar.activation(out=xnm_v[k % 8], in_=xt_v[k % 4],
                                          func=AF.Square, accum_out=L[:, 4:5])
                    xs, xv = xld_sem(k)
                    i._wait_ge(SM[xs], xv)
                    i.then_inc(SM["s_lns"], 1)

                def mk_sqrt(k=k, L=L):
                    i = scalar.activation(out=L[:, 7:8], in_=L[:, 6:7],
                                          func=AF.Sqrt)
                    i._wait_ge(SM["s_lnc"], 7 * k + 6)
                    i.then_inc(SM["s_lns"], 1)

                out.append((176 + 80 * kk, mk_sq))
                out.append((196 + 80 * kk, mk_sqrt))
            return out

        def tr_emit(tensor, g, kk):
            k = 4 * g + kk
            tensor.wait_ge(SM["s_lne"], 8 * k)
            for dcb in range(DC):
                i = tensor.transpose(
                    out=pb_tr.ap()[:, 128 * dcb:128 * (dcb + 1)],
                    in_=xnm_v[k % 8][:, 128 * dcb:128 * (dcb + 1)],
                    identity=ident_b.ap()[:, :])
                if dcb == 0:
                    i._wait_ge(SM["s_lnc"], 7 * k + 7)
                if dcb == DC - 1:
                    i.then_inc(SM["s_tr"], 1)

        def ln_evac_tile(vector, k):
            kk = k % 4
            g = k // 4
            fns = []
            for dcb in range(DC):
                def mk(k=k, kk=kk, dcb=dcb, g=g):
                    if dcb == 0 and kk == 0 and g > 0:
                        vector.wait_ge(SM["s_xsp"], 16 * DC * g)
                    i = vector.tensor_copy(
                        out=xnst_v[:, TT * dcb + 128 * kk:
                                   TT * dcb + 128 * (kk + 1)],
                        in_=pb_tr.ap()[:, 128 * dcb:128 * (dcb + 1)])
                    if dcb == 0:
                        i._wait_ge(SM["s_tr"], k + 1)
                    i.then_inc(SM["s_lne"], 1)
                fns.append(mk)
            return fns

        def ln_evac(vector, g):
            out = []
            for kk in range(4):
                for dcb, mk in enumerate(ln_evac_tile(vector, 4 * g + kk)):
                    out.append((12 + 4 * (8 * kk + dcb), mk))
            return out

        def xsp_emit(gpsimd, g):
            for dc in range(DC):
                ws = [("s_lne", 8 * (4 * g + 4))] if dc == 0 else []
                dma(gpsimd, "s_xsp", xnt_d[dc, :, TT * g:TT * (g + 1)],
                    xnst_v[:, TT * dc:TT * (dc + 1)], waits=ws)

        def bx_mm(tensor, w):
            for ec in range(DC):
                g = DC * w + ec
                if g >= 2:
                    tensor.wait_ge(SM["s_bxe"], g - 1)
                for dc in range(DC):
                    i = tensor.matmul(
                        out=pb[ec % 2].ap()[:, :],
                        lhsT=wbt_sb.ap()[:, D * dc + 128 * ec:
                                         D * dc + 128 * (ec + 1)],
                        rhs=xnw_v[w % 3][dc],
                        start=(dc == 0), stop=(dc == DC - 1))
                    if dc == 0:
                        i._wait_ge(SM["s_xnw"], 16 * DC * (w + 1))
                    if dc == DC - 1:
                        i.then_inc(SM["s_bxm"], 1)

        def f1_mm(tensor, t):
            if t == 0:
                tensor.wait_ge(SM["s_sgn"], 16)
            for q in range(16):
                tensor.wait_ge(SM["s_s1l"], 16 * (16 * t + q + 1))
                for r in range(2):
                    hc = 2 * q + r
                    g = HC * t + hc
                    if g >= 2:
                        tensor.wait_ge(SM["s_f1e"], g - 1)
                    for dc in range(DC):
                        i = tensor.matmul(
                            out=pb[2 + hc % 2].ap()[:, :],
                            lhsT=s1p_v[(16 * t + q) % 2][:, r * 1024 + dc * 128:
                                                         r * 1024 + dc * 128 + 128],
                            rhs=xnw_v[t % 3][dc],
                            start=(dc == 0), stop=(dc == DC - 1))
                        if dc == 0:
                            i._wait_ge(SM["s_xnw"], 16 * DC * (t + 1))
                        if dc == DC - 1:
                            i.then_inc(SM["s_f1m"], 1)

        def f2_mm(tensor, t, douts):
            if t == 0 and douts[0] == 0:
                tensor.wait_ge(SM["s_sgn"], 32)
            for dout in douts:
                g = DC * t + dout
                if g >= 2:
                    tensor.wait_ge(SM["s_f2e"], g - 1)
                for hc in range(HC):
                    q2 = 2 * dout + hc // 16
                    if hc % 16 == 0:
                        tensor.wait_ge(SM["s_s2l"], 16 * (16 * t + q2 + 1))
                    i = tensor.matmul(
                        out=pb[4 + dout % 2].ap()[:, :],
                        lhsT=s2p_v[(16 * t + q2) % 2][:, (hc % 16) * 128:
                                                      (hc % 16) * 128 + 128],
                        rhs=hst_v[:, TT * hc:TT * (hc + 1)],
                        start=(hc == 0), stop=(hc == HC - 1))
                    i._wait_ge(SM["s_f1e"], HC * t + hc + 1)
                    if hc == HC - 1:
                        i.then_inc(SM["s_f2m"], 1)

        @blk.sync
        def _(sync):
            for srcp, dst in ((bB_d, bB_sb), (b1_d, b1_sb), (b2_d, b2_sb),
                              (bC_d, bC_sb), (a_d, a_sb)):
                dma(sync, "s_wld", dst.ap()[:, :], srcp[:, :])
            for c in range(HC):
                ws = []
                if c >= 4:
                    ws = [("s_alv", c - 3), ("s_sgT", c - 3)]
                dma(sync, cW(c)[0], wsc[c % 4],
                    w1n[128 * c:128 * (c + 1), :], waits=ws)
            for p in range(32):
                c = 32 + p
                ws = [("s_alv", 29 + p if p < 4 else c_avr(p - 4)),
                      ("s_sgT", c - 3)]
                dc, jp = p // 4, p % 4
                dma(sync, cW(c)[0], wsc[c % 4],
                    w2n[128 * dc:128 * (dc + 1), 1024 * jp:1024 * (jp + 1)],
                    waits=ws)
            sync.wait_ge(SM["s_wld"], 16 * 5)
            sync.wait_ge(SM["s_wl1"], 16 * 32)
            sync.wait_ge(SM["s_wl2"], 16 * 32)
            sync.nop().then_inc(SM["s_ph"], 1)
            bw(sync, 1)
            for g in range(min(5, NG)):
                x_emit(sync, g)
            xnw_emit(sync, 0)
            if NT > 1:
                xnw_emit(sync, 1)
            for w in range(NT):
                if w + 2 < NT:
                    xnw_emit(sync, w + 2)
                if w >= 1:
                    for q in range(16):
                        s2p_emit(sync, w - 1, q)
                for q in range(16):
                    s1p_emit(sync, w, q)
            for q in range(16):
                s2p_emit(sync, NT - 1, q)
            sync.wait_ge(SM["s_xld"], 16 * min(NL, 20))
            sync.wait_ge(SM["s_xnw"], 16 * DC * NT)
            sync.wait_ge(SM["s_s1l"], 16 * 16 * NT)
            sync.wait_ge(SM["s_s2l"], 16 * 16 * NT)
            sync.nop().then_inc(SM["s_ph"], 1)
            bw(sync, 2)
            for t in range(NT):
                dma(sync, "s_hsl", hsb_v[t % 2], hs_d[t, :, :],
                    waits=[("s_wcm", DC * (t - 1) if t >= 2 else 0)])
                for s4 in range(4):
                    dma(sync, "s_xr", xr_v[s4],
                        x_in[TT * t + 128 * s4:TT * t + 128 * (s4 + 1), :],
                        waits=[("s_cmb", 8 * t if (t >= 1 and s4 == 0) else 0)])
            sync.wait_ge(SM["s_yst"], 16 * 4 * NT)

        @blk.gpsimd
        def _(gpsimd):
            gpsimd.iota(io_r.ap()[:, :], pattern=[[1, 128]], base=0,
                        channel_multiplier=0).then_inc(SM["s_gld"], 1)
            gpsimd.iota(io_c.ap()[:, :], pattern=[[0, 128]], base=0,
                        channel_multiplier=1).then_inc(SM["s_gld"], 1)
            for i in range(DC):
                dma(gpsimd, "s_gw", wbt_sb.ap()[:, i * D:(i + 1) * D],
                    wbt[128 * i:128 * (i + 1), :])
            for q in range(16):
                dma(gpsimd, "s_ssp", s1_d[q, :, :], pstage[q % 2],
                    waits=[("s_sgn", 4 * q + 4)])
            for q2 in range(16):
                dma(gpsimd, "s_ssp", s2_d[q2, :, :], pstage[q2 % 2],
                    waits=[("s_sgn", 64 + 4 * q2 + 4)])
            gpsimd.wait_ge(SM["s_gld"], 2)
            gpsimd.wait_ge(SM["s_gw"], 16 * DC)
            gpsimd.wait_ge(SM["s_ssp"], 16 * 32)
            gpsimd.engine_nop().then_inc(SM["s_ph"], 1)
            bw(gpsimd, 1)
            for g in range(min(3, NG)):
                xsp_emit(gpsimd, g)
            for w in range(NT):
                if w + 3 < NG:
                    xsp_emit(gpsimd, w + 3)
                if w >= 1:
                    for dout in range(DC):
                        dma(gpsimd, "s_gsp",
                            g_d[w - 1, :, TT * dout:TT * (dout + 1)],
                            gst_v[dout % 2],
                            waits=[("s_f2e", DC * (w - 1) + dout + 1)])
                    dma(gpsimd, "s_hss", hs_d[w - 1, :, :], hsst_v,
                        waits=[("s_hc", DC * w)])
                if w + 5 < NG:
                    x_emit(gpsimd, w + 5)
            dma(gpsimd, "s_hss", hs_d[NT - 1, :, :], hsst_v,
                waits=[("s_hc", DC * NT)])
            for dout in range(DC):
                dma(gpsimd, "s_gsp", g_d[NT - 1, :, TT * dout:TT * (dout + 1)],
                    gst_v[dout % 2],
                    waits=[("s_f2e", DC * (NT - 1) + dout + 1)])
            gpsimd.wait_ge(SM["s_hss"], 16 * NT)
            gpsimd.wait_ge(SM["s_gsp"], 16 * DC * NT)
            gpsimd.wait_ge(SM["s_xsp"], 16 * DC * NG)
            if NL > 20:
                gpsimd.wait_ge(SM["s_xld2"], 16 * (NL - 20))
            gpsimd.engine_nop().then_inc(SM["s_ph"], 1)
            bw(gpsimd, 2)
            for i in range(DC):
                dma(gpsimd, "s_gw", wct_v[:, i * D:(i + 1) * D],
                    wct[128 * i:128 * (i + 1), :])
            for t in range(NT):
                dma(gpsimd, "s_gl", gb_v[t % 2], g_d[t, :, :],
                    waits=[("s_sg", DC * (t - 1) if t >= 2 else 0)])
                for s4 in range(4):
                    u = 4 * t + s4
                    dma(gpsimd, "s_yst",
                        y_out[TT * t + 128 * s4:TT * t + 128 * (s4 + 1), :],
                        ost_v[s4], waits=[("s_cmb", 2 * (u + 1))])

        @blk.vector
        def _(vector):
            i = vector.tensor_tensor(out=ident_b.ap()[:, :], in0=io_r.ap()[:, :],
                                     in1=io_c.ap()[:, :], op=ALU.is_equal)
            i._wait_ge(SM["s_gld"], 2)
            i.then_inc(SM["s_prep"], 1)
            i = vector.tensor_copy(out=ident_f.ap()[:, :], in_=ident_b.ap()[:, :])
            i._wait_ge(SM["s_prep"], 1)
            i.then_inc(SM["s_prep"], 1)
            vector.memset(zs.ap()[:, :], 0.0).then_inc(SM["s_prep"], 1)
            for hc in range(HC):
                i = vector.tensor_reduce(out=al1_sb.ap()[:, hc:hc + 1],
                                         in_=wsc[hc % 4], axis=AX.X,
                                         op=ALU.add, apply_absolute_value=True)
                i._wait_ge(SM[cW(hc)[0]], cW(hc)[1])
                i.then_inc(SM["s_alv"], 1)
            for p in range(32):
                if p % 4 == 0 and p >= 4:
                    vector.wait_ge(SM["s_alv"], c_avr(p - 1) + 1)
                i = vector.tensor_reduce(out=al2p_sb.ap()[:, p % 4:p % 4 + 1],
                                         in_=wsc[(32 + p) % 4], axis=AX.X,
                                         op=ALU.add, apply_absolute_value=True)
                i._wait_ge(SM[cW(32 + p)[0]], cW(32 + p)[1])
                i.then_inc(SM["s_alv"], 1)
                if p % 4 == 3:
                    i = vector.tensor_reduce(out=al2_sb.ap()[:, p // 4:p // 4 + 1],
                                             in_=al2p_sb.ap()[:, :], axis=AX.X,
                                             op=ALU.add)
                    i._wait_ge(SM["s_alv"], c_avr(p))
                    i.then_inc(SM["s_alv"], 1)
            vector.wait_ge(SM["s_alv"], 72)
            vector.tensor_scalar_mul(out=al1s_sb.ap()[:, :], in0=al1_sb.ap()[:, :],
                                     scalar1=1.0 / D).then_inc(SM["s_prep"], 1)
            vector.tensor_scalar_mul(out=al2s_sb.ap()[:, :], in0=al2_sb.ap()[:, :],
                                     scalar1=1.0 / H).then_inc(SM["s_prep"], 1)
            vector.wait_ge(SM["s_prep"], 5)
            vector.engine_nop().then_inc(SM["s_ph"], 1)
            bw(vector, 1)
            pre_tiles = 4 * min(4, NG)
            pre_ev = 4 * min(3, NG)
            for k in range(pre_tiles):
                mk_a, mk_b1, mk_b2 = ln_ve_parts(vector, k)
                mk_a()
                mk_b1()
                mk_b2()
                if k >= 1 and (k - 1) < pre_ev:
                    for fn in ln_evac_tile(vector, k - 1):
                        fn()
            if pre_ev >= pre_tiles:
                for fn in ln_evac_tile(vector, pre_tiles - 1):
                    fn()
            hswr = hswin_v.rearrange("p (s c) -> p c s", c=8)
            for w in range(NT):
                ins = {}

                def put(j, fn):
                    ins.setdefault(j, []).append(fn)
                if w >= 1:
                    for dc in range(DC):
                        def mkc(dc=dc, w=w):
                            pw = (w - 1) % 2
                            if dc == 0:
                                vector.wait_ge(SM["s_hss"], 16 * (w - 1))
                            i = vector.tensor_copy(
                                out=hsst_v[:, TT * dc:TT * (dc + 1)],
                                in_=hswr[:, dc, pw * TT:pw * TT + TT])
                            i._wait_ge(SM["ss"], w * TT)
                            i.then_inc(SM["s_hc"], 1)
                        put(64 + 56 * dc, mkc)
                if w + 3 < NG:
                    for j, fn in ln_evac(vector, w + 3):
                        put(j, fn)
                if w + 4 < NG:
                    for j, fn in ln_compute_ve(vector, w + 4):
                        put(j, fn)
                vector.wait_ge(SM["s_bxe"], DC * (w + 1))
                for j in range(TT):
                    t = w * TT + j
                    prev = (t - 1) % (2 * TT)
                    hp = zs.ap()[:, 0:8] if t == 0 else \
                        hswin_v[:, 8 * prev:8 * prev + 8]
                    i1 = vector.tensor_tensor(
                        out=m_sc.ap()[:, 8 * (t % 2):8 * (t % 2) + 8],
                        in0=hp, in1=a_sb.ap()[:, 0:8], op=ALU.mult)
                    if t > 0:
                        i1._wait_ge(SM["ss"], t)
                    i1.then_inc(SM["vm"], 1)
                    usl = t % 64
                    i2 = vector.tensor_tensor(
                        out=pb_u.ap()[:, 8 * usl:8 * usl + 8],
                        in0=m_sc.ap()[:, 8 * (t % 2):8 * (t % 2) + 8],
                        in1=bxw_v[w % 2][:, 8 * j:8 * j + 8], op=ALU.add)
                    i2._wait_ge(SM["vm"], t + 1)
                    i2.then_inc(SM["sv"], 1)
                    for fn in ins.get(j, ()):
                        fn()
            vector.wait_ge(SM["ss"], NT * TT)
            vector.wait_ge(SM["s_hss"], 16 * (NT - 1))
            pw = (NT - 1) % 2
            for dc in range(DC):
                i = vector.tensor_copy(
                    out=hsst_v[:, TT * dc:TT * (dc + 1)],
                    in_=hswr[:, dc, pw * TT:pw * TT + TT])
                i.then_inc(SM["s_hc"], 1)
            vector.wait_ge(SM["s_hc"], DC * NT)
            vector.wait_ge(SM["sv"], S)
            vector.wait_ge(SM["vm"], S)
            vector.wait_ge(SM["s_lne"], 8 * NL)
            vector.wait_ge(SM["s_lnc"], 7 * NL)
            vector.engine_nop().then_inc(SM["s_ph"], 1)
            bw(vector, 2)
            for t in range(NT):
                p = t % 2
                if t >= 1:
                    vector.wait_ge(SM["s_wce"], DC * (t - 1))
                for ec in range(DC):
                    i = vector.tensor_scalar_add(
                        out=sst_v[p][:, TT * ec:TT * (ec + 1)],
                        in0=pb[(0, 1, 4, 5)[ec % 4]].ap()[:, :],
                        scalar1=bC_sb.ap()[:, ec:ec + 1])
                    i._wait_ge(SM["s_wcm"], DC * t + ec + 1)
                    i.then_inc(SM["s_wce"], 1)
                vector.wait_ge(SM["s_wce"], DC * (t + 1))
                for ec in range(DC):
                    i = vector.tensor_tensor(
                        out=sgst_v[p][:, TT * ec:TT * (ec + 1)],
                        in0=sst_v[p][:, TT * ec:TT * (ec + 1)],
                        in1=gb_v[p][:, TT * ec:TT * (ec + 1)], op=ALU.mult)
                    if ec == 0:
                        i._wait_ge(SM["s_gl"], 16 * (t + 1))
                    i.then_inc(SM["s_sg"], 1)
                for s4 in range(4):
                    u = 4 * t + s4
                    vector.wait_ge(SM["s_xr"], 16 * (u + 1))
                    if t >= 1:
                        vector.wait_ge(SM["s_yst"], 16 * (4 * (t - 1) + s4 + 1))
                    i = vector.tensor_tensor(
                        out=ost_v[s4][:, 0:512], in0=pb[2].ap()[:, :],
                        in1=xr_v[s4][:, 0:512], op=ALU.add)
                    i._wait_ge(SM["s_tr2"], u + 1)
                    i.then_inc(SM["s_cmb"], 1)
                    i = vector.tensor_tensor(
                        out=ost_v[s4][:, 512:1024], in0=pb[3].ap()[:, :],
                        in1=xr_v[s4][:, 512:1024], op=ALU.add)
                    i._wait_ge(SM["s_xr"], 16 * (u + 1))
                    i.then_inc(SM["s_cmb"], 1)

        @blk.scalar
        def _(scalar):
            for c in range(64):
                q = c // 2
                for half in range(2):
                    if half == 0 and c >= 4:
                        scalar.wait_ge(SM["s_ssp"], 16 * (q - 1))
                    i = scalar.activation(
                        out=pstage[q % 2][:, (c % 2) * 1024 + 512 * half:
                                          (c % 2) * 1024 + 512 * (half + 1)],
                        in_=pb[2 + 2 * (c % 2) + half].ap()[:, :], func=AF.Sign)
                    i._wait_ge(SM["s_sgT"], c + 1)
                    i.then_inc(SM["s_sgn"], 1)
            scalar.wait_ge(SM["s_sgn"], 128)
            scalar.nop().then_inc(SM["s_ph"], 1)
            bw(scalar, 1)
            for g in range(min(4, NG)):
                for _, fn in sorted(ln_compute_se(scalar, g), key=lambda x: x[0]):
                    fn()
            bxwr = [bxw_v[p].rearrange("p (t c) -> p c t", c=8) for p in range(2)]
            for ec in range(DC):
                i = scalar.add(out=bxwr[0][:, ec, :], in_=pb[ec % 2].ap()[:, :],
                               add=bB_sb.ap()[:, ec:ec + 1])
                i._wait_ge(SM["s_bxm"], ec + 1)
                i.then_inc(SM["s_bxe"], 1)
            for w in range(NT):
                ins = {}

                def put(j, fn):
                    ins.setdefault(j, []).append(fn)
                if w + 1 < NT:
                    for ec in range(DC):
                        def mke(ec=ec, w=w):
                            if ec == 0:
                                scalar.wait_ge(SM["sv"], w * TT)
                            i = scalar.add(out=bxwr[(w + 1) % 2][:, ec, :],
                                           in_=pb[ec % 2].ap()[:, :],
                                           add=bB_sb.ap()[:, ec:ec + 1])
                            i._wait_ge(SM["s_bxm"], DC * (w + 1) + ec + 1)
                            i.then_inc(SM["s_bxe"], 1)
                        put(150 + 8 * ec, mke)
                if w >= 1:
                    for dout in range(DC):
                        def mks(dout=dout, w=w):
                            t = w - 1
                            if DC * t + dout >= 2:
                                scalar.wait_ge(SM["s_gsp"],
                                               16 * (DC * t + dout - 1))
                            i = scalar.activation(
                                out=gst_v[dout % 2],
                                in_=pb[4 + dout % 2].ap()[:, :], func=AF.Silu,
                                bias=b2_sb.ap()[:, dout:dout + 1],
                                scale=al2s_sb.ap()[:, dout:dout + 1])
                            i._wait_ge(SM["s_f2m"], DC * t + dout + 1)
                            i.then_inc(SM["s_f2e"], 1)
                        put(230 + 16 * dout, mks)
                for hc in range(HC):
                    def mkg(hc=hc, w=w):
                        if hc == 0:
                            scalar.wait_ge(SM["s_f2m"], DC * w)
                            if w == 0:
                                scalar.wait_ge(SM["s_prep"], 5)
                        i = scalar.activation(
                            out=hst_v[:, TT * hc:TT * (hc + 1)],
                            in_=pb[2 + hc % 2].ap()[:, :], func=AF.Gelu,
                            bias=b1_sb.ap()[:, hc:hc + 1],
                            scale=al1s_sb.ap()[:, hc:hc + 1])
                        i._wait_ge(SM["s_f1m"], HC * w + hc + 1)
                        i.then_inc(SM["s_f1e"], 1)
                    put(384 + 4 * hc, mkg)
                if w + 4 < NG:
                    for j, fn in ln_compute_se(scalar, w + 4):
                        put(j, fn)
                if w >= 2:
                    scalar.wait_ge(SM["s_hc"], DC * (w - 1))
                for j in range(TT):
                    t = w * TT + j
                    sl = t % (2 * TT)
                    usl = t % 64
                    i3 = scalar.activation(
                        out=hswin_v[:, 8 * sl:8 * sl + 8],
                        in_=pb_u.ap()[:, 8 * usl:8 * usl + 8], func=AF.Tanh)
                    i3._wait_ge(SM["sv"], t + 1)
                    i3.then_inc(SM["ss"], 1)
                    for fn in ins.get(j, ()):
                        fn()
            for dout in range(DC):
                t = NT - 1
                if DC * t + dout >= 2:
                    scalar.wait_ge(SM["s_gsp"], 16 * (DC * t + dout - 1))
                i = scalar.activation(
                    out=gst_v[dout % 2], in_=pb[4 + dout % 2].ap()[:, :],
                    func=AF.Silu, bias=b2_sb.ap()[:, dout:dout + 1],
                    scale=al2s_sb.ap()[:, dout:dout + 1])
                i._wait_ge(SM["s_f2m"], DC * t + dout + 1)
                i.then_inc(SM["s_f2e"], 1)
            scalar.wait_ge(SM["ss"], S)
            scalar.wait_ge(SM["s_bxe"], DC * NT)
            scalar.wait_ge(SM["s_f1e"], HC * NT)
            scalar.wait_ge(SM["s_f2e"], DC * NT)
            scalar.wait_ge(SM["s_lns"], 2 * NL)
            scalar.nop().then_inc(SM["s_ph"], 1)
            bw(scalar, 2)

        @blk.tensor
        def _(tensor):
            tensor.wait_ge(SM["s_prep"], 2)
            for c in range(64):
                if c >= 2:
                    tensor.wait_ge(SM["s_sgn"], 2 * c - 2)
                for b in range(8):
                    bank = pb[2 + 2 * (c % 2) + b // 4]
                    i = tensor.transpose(
                        out=bank.ap()[:, 128 * (b % 4):128 * (b % 4 + 1)],
                        in_=wsc[c % 4][:, 128 * b:128 * (b + 1)],
                        identity=ident_f.ap()[:, :])
                    if b == 0:
                        i._wait_ge(SM[cW(c)[0]], cW(c)[1])
                    if b == 7:
                        i.then_inc(SM["s_sgT"], 1)
            tensor.wait_ge(SM["s_sgT"], 64)
            tensor.nop().then_inc(SM["s_ph"], 1)
            bw(tensor, 1)
            for g in range(min(3, NG)):
                for kk in range(4):
                    tr_emit(tensor, g, kk)
            bx_mm(tensor, 0)
            for w in range(NT):
                if w + 3 < NG:
                    for kk in range(4):
                        tr_emit(tensor, w + 3, kk)
                if w + 1 < NT:
                    bx_mm(tensor, w + 1)
                if w >= 1:
                    f2_mm(tensor, w - 1, [0, 1, 2, 3])
                    f2_mm(tensor, w - 1, [4, 5, 6, 7])
                f1_mm(tensor, w)
            f2_mm(tensor, NT - 1, [0, 1, 2, 3])
            f2_mm(tensor, NT - 1, [4, 5, 6, 7])
            tensor.wait_ge(SM["s_bxm"], DC * NT)
            tensor.wait_ge(SM["s_f1m"], HC * NT)
            tensor.wait_ge(SM["s_f2m"], DC * NT)
            tensor.wait_ge(SM["s_tr"], NL)
            tensor.nop().then_inc(SM["s_ph"], 1)
            bw(tensor, 2)
            tensor.wait_ge(SM["s_gw"], 16 * 2 * DC)
            WB = (0, 1, 4, 5)
            for t in range(NT):
                p = t % 2
                for ec in range(DC):
                    g = DC * t + ec
                    if g >= 4:
                        tensor.wait_ge(SM["s_wce"], g - 3)
                    for dc in range(DC):
                        i = tensor.matmul(
                            out=pb[WB[ec % 4]].ap()[:, :],
                            lhsT=wct_v[:, D * dc + 128 * ec:
                                       D * dc + 128 * (ec + 1)],
                            rhs=hsb_v[p][:, TT * dc:TT * (dc + 1)],
                            start=(dc == 0), stop=(dc == DC - 1))
                        if dc == 0:
                            i._wait_ge(SM["s_hsl"], 16 * (t + 1))
                        if dc == DC - 1:
                            i.then_inc(SM["s_wcm"], 1)
                for s4 in range(4):
                    u = 4 * t + s4
                    tensor.wait_ge(SM["s_cmb"], 2 * u)
                    for dc in range(DC):
                        i = tensor.transpose(
                            out=pb[2 + dc // 4].ap()[:, 128 * (dc % 4):
                                                     128 * ((dc % 4) + 1)],
                            in_=sgst_v[p][:, TT * dc + 128 * s4:
                                          TT * dc + 128 * (s4 + 1)],
                            identity=ident_f.ap()[:, :])
                        if dc == 0:
                            i._wait_ge(SM["s_sg"], DC * (t + 1))
                        if dc == DC - 1:
                            i.then_inc(SM["s_tr2"], 1)

    return nc


def kernel(**inputs):
    S = 8192
    x = np.asarray(inputs["L0_attn_x"], np.float32)
    A = np.asarray(inputs["L0_attn_A"], np.float32)
    W_B = np.asarray(inputs["L0_attn_W_B"], np.float32)
    b_B = np.asarray(inputs["L0_attn_b_B"], np.float32)
    W_C = np.asarray(inputs["L0_attn_W_C"], np.float32)
    b_C = np.asarray(inputs["L0_attn_b_C"], np.float32)
    w1 = np.asarray(inputs["L0_ffn_w1"], np.float32)
    b1 = np.asarray(inputs["L0_ffn_b1"], np.float32)
    w2 = np.asarray(inputs["L0_ffn_w2"], np.float32)
    b2 = np.asarray(inputs["L0_ffn_b2"], np.float32)

    cm = np.ascontiguousarray
    shared = {
        "wbt": cm(W_B.T), "wct": cm(W_C.T),
        "w1n": cm(w1), "w2n": cm(w2),
        "bB": cm(b_B.reshape(DC, 128).T), "b1": cm(b1.reshape(HC, 128).T),
        "b2": cm(b2.reshape(DC, 128).T), "bC": cm(b_C.reshape(DC, 128).T),
        "afull": cm(A.reshape(DC, 128).T),
    }
    in_maps = []
    for c in range(8):
        m = dict(shared)
        m["x"] = cm(x[c // 2])
        in_maps.append(m)

    nc = build_nc(S)
    res = run_bass_kernel_spmd(nc, in_maps, core_ids=list(range(8)))
    out = np.stack([np.asarray(res.results[2 * b]["y"], np.float32)
                    for b in range(4)], axis=0)
    return out
